# version 17
# baseline (speedup 1.0000x reference)
import numpy as np
import ml_dtypes
from contextlib import ExitStack

import concourse.bass as bass
import concourse.mybir as mybir
from concourse.bass_utils import run_bass_kernel_spmd

F32 = mybir.dt.float32
BF16 = mybir.dt.bfloat16
I32 = mybir.dt.int32
U8 = mybir.dt.uint8
F16 = mybir.dt.float16
AF = mybir.ActivationFunctionType
ALU = mybir.AluOpType
BF = ml_dtypes.bfloat16

NC = 8
T = 8192
D = 2048
TC = 1024
NCH = 16
KILL = -30000.0
SHIFT = -12.0
NV_L0 = (28, 60)
DEVMAP = [None]
SIM_CORE_OVERRIDE = [None]


def zz_chunks(c):
    return (c, 15 - c)


class Prog:
    ENGS = ("tensor", "vector", "scalar", "gpsimd", "sync")

    def __init__(self, nc, stack):
        self.nc = nc
        self.stack = stack
        self.q = {e: [] for e in self.ENGS}
        self.sems = {}
        self.cnt = {}

    def _sem(self, name):
        if name not in self.sems:
            self.sems[name] = self.stack.enter_context(self.nc.semaphore(name))
        if name not in self.cnt:
            self.cnt[name] = 0
        return self.sems[name]

    def now(self, name):
        self._sem(name)
        return (name, self.cnt[name])

    def do(self, eng, meth, *args, wait=(), inc=None, inck=1, **kw):
        tok = None
        incs = ()
        if inc is not None:
            self._sem(inc)
            self.cnt[inc] += inck
            tok = (inc, self.cnt[inc])
            incs = ((inc, inck),)
        wl = tuple((w[0], w[1]) for w in wait if w is not None)
        self.q[eng].append((meth, args, kw, wl, incs))
        return tok

    def load(self, dst, src, eng="gpsimd", sem=None, wait=()):
        sem = sem or ("LD" if eng == "gpsimd" else "SD")
        return self.do(eng, "dma_start", wait=wait, inc=sem, inck=16,
                       out=dst, in_=src)

    def percore(self, bk, build_fn, boxes=()):
        cnt0 = dict(self.cnt)
        g0 = list(bk.guard)
        b0 = [list(b_) for b_ in boxes]
        marks = {e: len(self.q[e]) for e in self.ENGS}
        streams = []
        fin_cnt = None
        fin_g = None
        fin_b = None
        for core in range(NC):
            self.cnt = dict(cnt0)
            bk.guard = list(g0)
            for b_, init in zip(boxes, b0):
                b_[:] = list(init)
            build_fn(core)
            streams.append({e: self.q[e][marks[e]:] for e in self.ENGS})
            for e in self.ENGS:
                del self.q[e][marks[e]:]
            if fin_cnt is None:
                fin_cnt = dict(self.cnt)
                fin_g = list(bk.guard)
                fin_b = [list(b_) for b_ in boxes]
            else:
                assert fin_cnt == self.cnt, "per-core sem count mismatch"
                assert fin_g == bk.guard, "per-core bank guard mismatch"
                assert fin_b == [list(b_) for b_ in boxes], "box mismatch"
        self.cnt = fin_cnt
        bk.guard = fin_g
        for b_, fin in zip(boxes, fin_b):
            b_[:] = list(fin)
        for e in self.ENGS:
            self.q[e].append(("__percore__", (streams, e), {}, (), ()))

    def barrier_tokens(self):
        return tuple(self.now(s) for s in list(self.cnt))

    def emit(self):
        blk = self.stack.enter_context(self.nc.Block())
        sems = self.sems
        devmap = DEVMAP[0]
        body_to_core = {}
        if devmap is not None:
            for lg, k2 in enumerate(devmap):
                body_to_core[k2] = lg

        def runner(ops):
            def replay(h, seq):
                for meth, args, kw, waits, incs in seq:
                    for (sn, v) in waits:
                        h.wait_ge(sems[sn], v)
                    inst = getattr(h, meth)(*args, **kw)
                    for (sn, k) in incs:
                        inst.then_inc(sems[sn], k)

            def body(h):
                for entry in ops:
                    if entry[0] == "__percore__":
                        streams, eng = entry[1]
                        k = 0
                        for arm in h.switch_core_id(128):
                            core = body_to_core.get(k)
                            if core is None:
                                lg = getattr(arm, "logical", None)
                                if lg is not None and 0 <= lg < NC:
                                    core = lg
                            if core is not None:
                                if SIM_CORE_OVERRIDE[0] is not None:
                                    core = SIM_CORE_OVERRIDE[0]
                                replay(h, streams[core][eng])
                            k += 1
                    else:
                        replay(h, [entry])
            return body

        blk.tensor(runner(self.q["tensor"]))
        blk.vector(runner(self.q["vector"]))
        blk.scalar(runner(self.q["scalar"]))
        blk.gpsimd(runner(self.q["gpsimd"]))
        blk.sync(runner(self.q["sync"]))

    def n_instructions(self):
        return {e: len(q) for e, q in self.q.items()}


class Arena:
    def __init__(self, nc, stack, name, nbytes):
        self.t = stack.enter_context(nc.sbuf_tensor(name, [128, nbytes], U8))
        self.off = 0
        self.marks = []
        self.hi = 0

    def alloc(self, shape, dtype, align=32):
        assert shape[0] <= 128
        esz = mybir.dt.size(dtype)
        self.off = (self.off + align - 1) // align * align
        nfree = int(np.prod(shape[1:])) if len(shape) > 1 else 1
        nb = nfree * esz
        ap = self.t[0:shape[0], self.off:self.off + nb].bitcast(dtype)
        self.off += nb
        self.hi = max(self.hi, self.off)
        if len(shape) == 3:
            ap = ap.rearrange("p (a b) -> p a b", a=shape[1], b=shape[2])
        return ap

    def push(self):
        self.marks.append(self.off)

    def pop(self):
        self.off = self.marks.pop()


class Banks:

    def __init__(self, nc, stack):
        self.b = [stack.enter_context(nc.psum_tensor(f"bk{i}", [128, 512], F32))
                  for i in range(8)]
        self.guard = [None] * 8

    def ap(self, i):
        return self.b[i]

    def w(self, i):
        g = self.guard[i]
        self.guard[i] = None
        return (g,) if g else ()

    def r(self, i, tok):
        self.guard[i] = tok


class B:
    pass


class LNH:
    def __init__(self, p, b, ar):
        self.p, self.b = p, b
        b.sq2 = ar.alloc([128, 512], BF16)
        self.vm = ar.alloc([1, 512], F32)
        self.vrs = ar.alloc([1, 512], F32)
        self.vsd = ar.alloc([1, 512], F32)
        self.vneg = ar.alloc([1, 512], F32)
        self.rs_t = ar.alloc([128, 512], F32)
        self.c_t = ar.alloc([128, 512], F32)
        self.prev_rows = None
        self.prev_bc = None

    def stats(self, p, bk, srcs, s1, s2, need_sd=False):
        b = self.b
        w1 = bk.w(s1)
        w2 = bk.w(s2)
        sqb = (b.sq, b.sq2)
        n = len(srcs)
        ats = [None] * n
        mm2 = [None] * n
        for c, (src, wt) in enumerate(srcs):
            ats[c] = p.do("scalar", "activation", sqb[c % 2][:, :], src,
                          AF.Square,
                          wait=((wt,) if wt else ())
                          + ((mm2[c - 2],) if c >= 2 else ()),
                          inc="AC")
            p.do("tensor", "matmul", bk.ap(s1)[0:1, :], b.ones_col_bf[:, :], src,
                 wait=((wt,) if wt else ()) + (w1 if c == 0 else ()),
                 start=(c == 0), stop=(c == n - 1),
                 inc="PE" if c == n - 1 else None)
            if c >= 1:
                mm2[c - 1] = p.do("tensor", "matmul", bk.ap(s2)[0:1, :],
                                  b.ones_col_bf[:, :], sqb[(c - 1) % 2][:, :],
                                  wait=(ats[c - 1],) + (w2 if c == 1 else ()),
                                  start=(c == 1), stop=False, inc="PE")
        mm2[n - 1] = p.do("tensor", "matmul", bk.ap(s2)[0:1, :],
                          b.ones_col_bf[:, :], sqb[(n - 1) % 2][:, :],
                          wait=(ats[n - 1],), start=False, stop=True, inc="PE")
        pe = p.now("PE")
        d1 = p.do("vector", "tensor_scalar", self.vm[:, :], bk.ap(s1)[0:1, :],
                  1.0 / D, None, op0=ALU.mult, wait=(pe,), inc="DV")
        d2 = p.do("vector", "tensor_scalar", self.vrs[:, :], bk.ap(s2)[0:1, :],
                  1.0 / D, None, op0=ALU.mult, wait=(pe,), inc="DV")
        bk.r(s1, d1)
        bk.r(s2, d2)
        d3 = p.do("vector", "tensor_tensor", self.vsd[:, :], self.vm[:, :],
                  self.vm[:, :], op=ALU.mult, wait=(d1,), inc="DV")
        d4 = p.do("vector", "tensor_tensor", self.vrs[:, :], self.vrs[:, :],
                  self.vsd[:, :], op=ALU.subtract, wait=(d2, d3), inc="DV")
        if need_sd:
            a1 = p.do("scalar", "activation", self.vsd[:, :], self.vrs[:, :],
                      AF.Sqrt, bias=b.epscol[0:1, :], scale=1.0, wait=(d4,),
                      inc="AC")
            d5 = p.do("vector", "reciprocal", self.vrs[:, :], self.vsd[:, :],
                      wait=(a1,), inc="DV")
        else:
            a1 = p.do("scalar", "activation", self.vsd[:, :], self.vrs[:, :],
                      AF.Sqrt, bias=b.epscol[0:1, :], scale=1.0, wait=(d4,),
                      inc="AC")
            d5 = p.do("vector", "reciprocal", self.vrs[:, :], self.vsd[:, :],
                      wait=(a1,), inc="DV")
        d6 = p.do("vector", "tensor_scalar", self.vneg[:, :], self.vm[:, :],
                  -1.0, None, op0=ALU.mult, wait=(d1,), inc="DV")
        self.rows = (d5, d6)
        return d5, d6

    def bcast(self, p, bk, b1, b2):
        b = self.b
        d5, d6 = self.rows
        d7 = p.do("vector", "tensor_tensor", self.vm[:, :], self.vneg[:, :],
                  self.vrs[:, :], op=ALU.mult, wait=(d5, d6), inc="DV")
        t1 = p.do("tensor", "matmul", bk.ap(b1)[:, :], b.ones_row_f[:, :],
                  self.vrs[:, :], start=True, stop=True,
                  wait=(d5,) + bk.w(b1), inc="PE")
        t2 = p.do("tensor", "matmul", bk.ap(b2)[:, :], b.ones_row_f[:, :],
                  self.vm[:, :], start=True, stop=True,
                  wait=(d7,) + bk.w(b2), inc="PE")
        a1 = p.do("scalar", "copy", self.rs_t[:, :], bk.ap(b1)[:, :],
                  wait=(t1,), inc="AC")
        a2 = p.do("scalar", "copy", self.c_t[:, :], bk.ap(b2)[:, :],
                  wait=(t2,), inc="AC")
        bk.r(b1, a1)
        bk.r(b2, a2)
        return a1, a2


HB0 = 5632
HC0 = 44


def build_launch1():
    nc = bass.Bass("TRN2", target_bir_lowering=False, debug=False)
    stack = ExitStack()
    nc._stk = stack

    xT = nc.dram_tensor("xT", [NCH, 128, T], F32, kind="ExternalInput").ap()
    xown = nc.dram_tensor("xown", [NCH, 128, TC], F32, kind="ExternalInput").ap()
    wq = nc.dram_tensor("wq", [128, NCH * 128], F32, kind="ExternalInput").ap()
    wk = nc.dram_tensor("wk", [128, NCH * 128], F32, kind="ExternalInput").ap()
    wv = nc.dram_tensor("wv", [128, NCH * 128], F32, kind="ExternalInput").ap()
    wo = nc.dram_tensor("wo", [128, D], F32, kind="ExternalInput").ap()
    bq = nc.dram_tensor("bq", [128, 1], F32, kind="ExternalInput").ap()
    bk_in = nc.dram_tensor("bk", [128, 1], F32, kind="ExternalInput").ap()
    bv = nc.dram_tensor("bv", [1, 128], F32, kind="ExternalInput").ap()
    bo = nc.dram_tensor("bo", [128, NCH], F32, kind="ExternalInput").ap()
    tri = nc.dram_tensor("tri", [128, 4 * 512], F32, kind="ExternalInput").ap()
    kbias = nc.dram_tensor("kbias", [128, 128], F32, kind="ExternalInput").ap()
    wg_img = nc.dram_tensor("wg_img", [HC0, 128, D], U8, kind="ExternalInput").ap()
    wu_img = nc.dram_tensor("wu_img", [HC0, 128, D], U8, kind="ExternalInput").ap()
    wd_img = nc.dram_tensor("wd_img", [NCH, 128, HB0], U8, kind="ExternalInput").ap()
    sg = nc.dram_tensor("sg", [128, HC0], F32, kind="ExternalInput").ap()
    su = nc.dram_tensor("su", [128, HC0], F32, kind="ExternalInput").ap()
    sdn = nc.dram_tensor("sdn", [128, NCH], F32, kind="ExternalInput").ap()
    wk1 = nc.dram_tensor("wk1", [128, NCH * 512], F32, kind="ExternalInput").ap()
    wv1 = nc.dram_tensor("wv1", [128, NCH * 512], F32, kind="ExternalInput").ap()

    x_out = nc.dram_tensor("x_out", [NCH, 128, TC], F32, kind="ExternalOutput").ap()
    xn1_out = nc.dram_tensor("xn1_out", [NCH, 128, TC], BF16, kind="ExternalOutput").ap()
    k1_out = nc.dram_tensor("k1_out", [4, 128, TC], BF16, kind="ExternalOutput").ap()
    v1_out = nc.dram_tensor("v1_out", [TC, 512], BF16, kind="ExternalOutput").ap()

    with stack:
        p = Prog(nc, stack)
        ar = Arena(nc, stack, "arena", 206 * 1024)
        bk = Banks(nc, stack)
        b = B()

        b.x1 = ar.alloc([128, NCH, TC], F32)
        b.ones_col_bf = ar.alloc([128, 1], BF16)
        b.epscol = ar.alloc([128, 1], F32)
        b.shiftcol = ar.alloc([128, 1], F32)
        b.ones_row_f = ar.alloc([1, 128], F32)
        b.ones_row_bf = ar.alloc([1, 128], BF16)
        b.sq = ar.alloc([128, 512], BF16)
        b.idf = ar.alloc([128, 128], F32)
        b.bqc = ar.alloc([128, 1], F32)
        b.bkc = ar.alloc([128, 1], F32)
        b.bvr = ar.alloc([1, 128], F32)
        b.bv_bf = ar.alloc([1, 128], BF16)
        b.bos = ar.alloc([128, NCH], F32)
        b.csq = ar.alloc([1, 128], BF16)
        b.csk = ar.alloc([1, 128], BF16)
        b.csv = ar.alloc([1, 128], BF16)
        b.bq_row = ar.alloc([1, 128], BF16)
        b.bk_row = ar.alloc([1, 128], BF16)
        b.rcol = ar.alloc([128, 8], F32)
        b.negm_bf = ar.alloc([1, 512], BF16)
        b.sd_bf = ar.alloc([1, 512], BF16)
        b.t1 = ar.alloc([128, 512], F32)

        ln = LNH(p, b, ar)

        p.load(b.bqc[:, :], bq[:, :], sem="LC")
        p.load(b.bkc[:, :], bk_in[:, :], sem="LC")
        p.load(b.bvr[:, :], bv[:, :], sem="LC")
        p.load(b.bos[:, :], bo[:, :], sem="LC")
        ident_dram = nc.inline_tensor(np.eye(128, dtype=np.float32), "idf_c").ap()
        p.load(b.idf[:, :], ident_dram[:, :], sem="LC")
        p.do("gpsimd", "memset", b.ones_col_bf[:, :], 1.0, inc="GP")
        p.do("gpsimd", "memset", b.ones_row_f[:, :], 1.0, inc="GP")
        p.do("gpsimd", "memset", b.ones_row_bf[:, :], 1.0, inc="GP")
        p.do("gpsimd", "memset", b.epscol[:, :], 1e-5, inc="GP")
        p.do("gpsimd", "memset", b.shiftcol[:, :], SHIFT, inc="GP")
        CGP = p.now("GP")

        def colsum_biasrow(wfull, dst_row, bias_col, dst_bias_row, first):
            for c in range(NCH):
                p.do("tensor", "matmul", bk.ap(4)[0:1, 0:128], b.ones_col_bf[:, :],
                     wfull[:, c * 128:(c + 1) * 128], start=(c == 0),
                     stop=(c == NCH - 1),
                     wait=((CLD, CGP) if (c == 0 and first) else ()) + (bk.w(4) if c == 0 else ()),
                     inc="PE" if c == NCH - 1 else None)
            tpe = p.now("PE")
            a = p.do("scalar", "copy", dst_row[:, :], bk.ap(4)[0:1, 0:128],
                     wait=(tpe,), inc="AC")
            bk.r(4, a)
            if bias_col is not None:
                tt = p.do("tensor", "transpose", bk.ap(5)[0:1, 0:128],
                          bias_col[:, :], b.idf[:, :],
                          wait=bk.w(5) + ((CLD,) if first else ()), inc="PE")
                a2 = p.do("scalar", "copy", dst_bias_row[:, :],
                          bk.ap(5)[0:1, 0:128], wait=(tt,), inc="AC")
                bk.r(5, a2)

        ar.push()
        b.wqs = ar.alloc([128, NCH * 128], BF16)
        b.wks = ar.alloc([128, NCH * 128], BF16)
        b.wvs = ar.alloc([128, NCH * 128], BF16)
        b.wos = ar.alloc([128, D], BF16)
        b.kbias = ar.alloc([128, 128], F32)
        b.tri = ar.alloc([128, 4 * 512], F32)
        b.recip = ar.alloc([1, 512], F32)
        b.rcp_t = ar.alloc([128, 512], F32)
        p.load(b.wqs[:, :], wq[:, :], sem="LC")
        p.load(b.wks[:, :], wk[:, :], sem="LC")
        p.load(b.wvs[:, :], wv[:, :], sem="LC")
        p.load(b.wos[:, :], wo[:, :], sem="LC")
        p.load(b.kbias[:, :], kbias[:, :], sem="LC")
        p.load(b.tri[:, :], tri[:, :], sem="LC")
        CLD = p.now("LC")
        colsum_biasrow(b.wqs, b.csq, b.bqc, b.bq_row, True)
        colsum_biasrow(b.wks, b.csk, b.bkc, b.bk_row, False)
        colsum_biasrow(b.wvs, b.csv, None, None, False)
        p.do("vector", "tensor_copy", b.bv_bf[:, :], b.bvr[:, :], wait=(CLD,), inc="DV")
        b.Kf = ar.alloc([128, T], BF16)
        b.Vt = ar.alloc([128, T], BF16)
        b.Qf = ar.alloc([128, TC], BF16)
        b.Kown = ar.alloc([128, TC], BF16)
        b.Vown = ar.alloc([128, TC], BF16)
        b.xb = [ar.alloc([128, NCH * 512], BF16) for _ in range(2)]
        b.ybf = ar.alloc([128, NCH * 512], BF16)
        b.attn = ar.alloc([128, 2 * 512], BF16)
        b.pt = ar.alloc([128, 1024], BF16)
        xbfree = [None, None]
        vfree = [None, None]

        def rs_cols_and_rows(nsub):
            d5, d6 = ln.rows
            lasta = None
            for j in range(nsub):
                bt = 4 + (j % 2)
                tt = p.do("tensor", "transpose", bk.ap(bt)[:, 0:1],
                          ln.vrs[:, j * 128:(j + 1) * 128], b.idf[0:1, 0:1],
                          wait=(d5,) + bk.w(bt), inc="PE")
                lasta = p.do("scalar", "copy", b.rcol[:, j:j + 1],
                             bk.ap(bt)[:, 0:1], wait=(tt,), inc="AC")
                bk.r(bt, lasta)
            nb = p.do("vector", "tensor_copy", b.negm_bf[:, :], ln.vneg[:, :],
                      wait=(d6,), inc="DV")
            sb = p.do("vector", "tensor_copy", b.sd_bf[:, :], ln.vsd[:, :],
                      wait=(d5,), inc="DV")
            return nb, sb, lasta

        def proj_fm(dst, wfull, xb_t, bi, csum_row, bias_row, nbsb):
            nb, sb = nbsb[0], nbsb[1]
            for c in range(NCH):
                p.do("tensor", "matmul", bk.ap(bi)[:, :],
                     wfull[:, c * 128:(c + 1) * 128],
                     xb_t[:, c * 512:(c + 1) * 512],
                     start=(c == 0), stop=False,
                     wait=bk.w(bi) if c == 0 else ())
            p.do("tensor", "matmul", bk.ap(bi)[:, :], csum_row[:, :],
                 b.negm_bf[:, :], start=False, stop=False, wait=(nb,))
            tpe = p.do("tensor", "matmul", bk.ap(bi)[:, :], bias_row[:, :],
                       b.sd_bf[:, :], start=False, stop=True, wait=(sb,),
                       inc="PE")
            dv = p.do("vector", "tensor_tensor", dst, bk.ap(bi)[:, :],
                      ln.rs_t[:, :], op=ALU.mult, wait=(tpe,), inc="DV")
            bk.r(bi, dv)
            return dv

        def proj_tm(dst, xb_t, wfull, csum_row, bias_row, nbsb):
            nb, sb, rct = nbsb
            last = None
            for j in range(4):
                bi = 2 + (j % 2)
                for c in range(NCH):
                    p.do("tensor", "matmul", bk.ap(bi)[:, 0:128],
                         xb_t[:, c * 512 + j * 128: c * 512 + (j + 1) * 128],
                         wfull[:, c * 128:(c + 1) * 128],
                         start=(c == 0), stop=False,
                         wait=bk.w(bi) if c == 0 else ())
                p.do("tensor", "matmul", bk.ap(bi)[:, 0:128],
                     b.negm_bf[:, j * 128:(j + 1) * 128], csum_row[:, :],
                     start=False, stop=False, wait=(nb,))
                tpe = p.do("tensor", "matmul", bk.ap(bi)[:, 0:128],
                           b.ones_row_bf[:, :], bias_row[:, :],
                           start=False, stop=True, wait=(sb,), inc="PE")
                a = p.do("scalar", "activation", dst[:, j * 128:(j + 1) * 128],
                         bk.ap(bi)[:, 0:128], AF.Identity, bias=0.0,
                         scale=b.rcol[:, j:j + 1], wait=(tpe, rct), inc="AC")
                bk.r(bi, a)
                last = a
            return last

        for ch in range(NCH):
            p.load(b.x1[:, ch, :], xown[ch, :, :], eng="sync", sem="X1")
        X1LD = p.now("X1")

        for src_dram, dsts in ((xown, "own"), (xT, "full")):
            nt = 2 if dsts == "own" else NCH
            for tt in range(nt):
                par = tt % 2
                xb_t = b.xb[par]
                lds = []
                for ch in range(NCH):
                    w = (xbfree[par],) if (ch == 0 and xbfree[par]) else ()
                    srcap = src_dram[ch, :, tt * 512:(tt + 1) * 512]
                    lds.append(p.load(xb_t[:, ch * 512:(ch + 1) * 512], srcap,
                                      wait=w, sem=f"XB{par}"))
                alltok = lds[-1]
                srcs = [(xb_t[:, c * 512:(c + 1) * 512], alltok) for c in range(NCH)]
                ln.stats(p, bk, srcs, 6, 7, need_sd=True)
                ln.bcast(p, bk, 4, 5)
                nbsb = rs_cols_and_rows(4)
                if dsts == "own":
                    proj_fm(b.Qf[:, tt * 512:(tt + 1) * 512], b.wqs, xb_t, 0,
                            b.csq, b.bq_row, nbsb)
                    proj_fm(b.Kown[:, tt * 512:(tt + 1) * 512], b.wks, xb_t, 1,
                            b.csk, b.bk_row, nbsb)
                    xbfree[par] = proj_tm(b.Vown[:, tt * 512:(tt + 1) * 512],
                                          xb_t, b.wvs, b.csv, b.bv_bf, nbsb)
                else:
                    proj_fm(b.Kf[:, tt * 512:(tt + 1) * 512], b.wks, xb_t, 1,
                            b.csk, b.bk_row, nbsb)
                    xbfree[par] = proj_tm(b.Vt[:, tt * 512:(tt + 1) * 512],
                                          xb_t, b.wvs, b.csv, b.bv_bf, nbsb)

        A1DV = p.now("DV")
        A1AC = p.now("AC")
        scale0 = 1.0 / np.sqrt(128.0)

        def build_a2(core):
            a_c, b_c = zz_chunks(core)
            vis = ([(0, "main", kc) for kc in range(4 * a_c)]
                   + [(0, "diag", j) for j in range(4)]
                   + [(1, "main", kc) for kc in range(4 * b_c)]
                   + [(1, "diag", j) for j in range(4)])
            assert len(vis) == 68
            nv0 = 4 * a_c + 4
            prev_ae = [None, None]
            wN0 = bk.w(4)
            wA0 = bk.w(2)
            wN1 = bk.w(5)
            wA1 = bk.w(3)
            vi_box = [0]
            pend = None

            def navs1(v, ae):
                qt, kind, kc = v
                vi = vi_box[0]
                half = (vi % 2) * 512
                if kind == "main":
                    vap = b.Vt[:, kc * 128:(kc + 1) * 128]
                else:
                    vap = b.Vown[:, qt * 512 + kc * 128: qt * 512 + (kc + 1) * 128]
                first = (vi == 0) if qt == 0 else (vi == nv0)
                last = (vi == nv0 - 1) if qt == 0 else (vi == 67)
                bN = 4 if qt == 0 else 5
                bA = 2 if qt == 0 else 3
                wN_ = (wN0 if qt == 0 else wN1) if first else ()
                wA_ = (wA0 if qt == 0 else wA1) if first else ()
                p.do("tensor", "matmul", bk.ap(bN)[0:1, :], b.ones_col_bf[:, :],
                     b.pt[:, half:half + 512],
                     wait=(ae,) + wN_, start=first, stop=last)
                p.do("tensor", "matmul", bk.ap(bA)[:, :], vap,
                     b.pt[:, half:half + 512],
                     wait=wA_, start=first, stop=last,
                     inc="PE" if vi == 67 else None)
                vi_box[0] = vi + 1

            for si, v in enumerate(vis):
                qt, kind, kc = v
                sb_i = si % 2
                half = (si % 2) * 512
                if kind == "main":
                    kap = b.Kf[:, kc * 128:(kc + 1) * 128]
                else:
                    kap = b.Kown[:, qt * 512 + kc * 128: qt * 512 + (kc + 1) * 128]
                ew = (A1DV, A1AC) if si == 0 else ()
                ts = p.do("tensor", "matmul", bk.ap(sb_i)[:, :], kap,
                          b.Qf[:, qt * 512:(qt + 1) * 512], start=True,
                          stop=True,
                          wait=((prev_ae[sb_i],) if prev_ae[sb_i] else ())
                          + bk.w(sb_i) + ew,
                          inc="PE")
                if kind == "diag":
                    ts = p.do("vector", "tensor_tensor", bk.ap(sb_i)[:, :],
                              bk.ap(sb_i)[:, :],
                              b.tri[:, kc * 512:(kc + 1) * 512],
                              op=ALU.add, wait=(ts,), inc="DV")
                ae = p.do("scalar", "activation", b.pt[:, half:half + 512],
                          bk.ap(sb_i)[:, :], AF.Exp, bias=b.shiftcol[:, :],
                          scale=scale0, wait=(ts,), inc="AC")
                prev_ae[sb_i] = ae
                bk.r(sb_i, ae)
                if pend is not None:
                    navs1(*pend)
                pend = (v, ae)
            navs1(*pend)
            for qt in range(2):
                bN = 4 if qt == 0 else 5
                bA = 2 if qt == 0 else 3
                pe = p.now("PE")
                r1 = p.do("vector", "reciprocal", b.recip[:, :],
                          bk.ap(bN)[0:1, :], wait=(pe,), inc="DV")
                bk.r(bN, r1)
                t1 = p.do("tensor", "matmul", bk.ap(6)[:, :],
                          b.ones_row_f[:, :], b.recip[:, :], start=True,
                          stop=True, wait=(r1,) + bk.w(6), inc="PE")
                a1 = p.do("scalar", "copy", b.rcp_t[:, :], bk.ap(6)[:, :],
                          wait=(t1,) + ((prev_rcp[0],) if prev_rcp[0] else ()),
                          inc="AC")
                bk.r(6, a1)
                dv = p.do("vector", "tensor_tensor",
                          b.attn[:, qt * 512:(qt + 1) * 512],
                          bk.ap(bA)[:, :], b.rcp_t[:, :], op=ALU.mult,
                          wait=(a1,), inc="DV")
                prev_rcp[0] = dv
                bk.r(bA, dv)

        prev_rcp = [None]
        p.percore(bk, build_a2, boxes=(prev_rcp,))
        ATN = p.now("DV")

        prev_ybf_use = None
        for ot in range(2):
            ytoks = []
            for ec in range(NCH):
                bi = ec % 2
                tpe = p.do("tensor", "matmul", bk.ap(bi)[:, :],
                           b.wos[:, ec * 128:(ec + 1) * 128],
                           b.attn[:, ot * 512:(ot + 1) * 512],
                           start=True, stop=True,
                           wait=(ATN,) + bk.w(bi), inc="PE")
                a = p.do("scalar", "activation",
                         b.ybf[:, ec * 512:(ec + 1) * 512],
                         bk.ap(bi)[:, :], AF.Identity, bias=b.bos[:, ec:ec + 1],
                         scale=1.0,
                         wait=(tpe,) + ((prev_ybf_use,) if (ec == 0 and prev_ybf_use) else ()),
                         inc="AC")
                bk.r(bi, a)
                ytoks.append(a)
            srcs = [(b.ybf[:, c * 512:(c + 1) * 512], ytoks[c])
                    for c in range(NCH)]
            ln.stats(p, bk, srcs, 6, 7)
            a1, a2 = ln.bcast(p, bk, 4, 5)
            t1prev = None
            for c in range(NCH):
                d1 = p.do("vector", "tensor_tensor", b.t1[:, :],
                          b.ybf[:, c * 512:(c + 1) * 512],
                          ln.rs_t[:, :], op=ALU.mult,
                          wait=(a1, a2) + ((t1prev,) if t1prev else ()), inc="DV")
                d2 = p.do("vector", "tensor_tensor", b.t1[:, :], b.t1[:, :],
                          ln.c_t[:, :], op=ALU.add, wait=(d1,), inc="DV")
                xu = p.do("vector", "tensor_tensor",
                          b.x1[:, c, ot * 512:(ot + 1) * 512],
                          b.x1[:, c, ot * 512:(ot + 1) * 512], b.t1[:, :],
                          op=ALU.add, wait=(d2, X1LD), inc="DV")
                t1prev = xu
            prev_ybf_use = p.now("DV")
        ar.pop()

        ar.push()
        b.xn = ar.alloc([128, NCH * 512], BF16)
        b.h = ar.alloc([128, HC0 * 512], BF16)
        b.yb = ar.alloc([128, NCH * 512], BF16)
        b.gsil = ar.alloc([128, 512], BF16)
        b.usc = ar.alloc([128, 512], BF16)
        b.wG = [ar.alloc([128, D], BF16) for _ in range(2)]
        b.wU = [ar.alloc([128, D], BF16) for _ in range(2)]
        b.wD = [ar.alloc([128, HB0], BF16) for _ in range(2)]
        b.sg = ar.alloc([128, HC0], F32)
        b.su = ar.alloc([128, HC0], F32)
        b.sdn = ar.alloc([128, NCH], F32)
        b.corr = ar.alloc([1, 512], BF16)
        b.corr2 = ar.alloc([1, 512], BF16)

        PHB = (p.now("PE"), p.now("DV"), p.now("AC"),
               p.now("XB0"), p.now("XB1"))
        p.load(b.sg[:, :], sg[:, :], sem="SC", wait=PHB)
        p.load(b.su[:, :], su[:, :], sem="SC")
        p.load(b.sdn[:, :], sdn[:, :], sem="SC")
        sctok = p.now("SC")
        sld = [sctok, sctok, sctok]
        wfree = {"G": [None, None], "U": [None, None], "D": [None, None]}

        for ot in range(2):
            PEB = p.now("PE")
            ACB = p.now("AC")
            casts = []
            cprev = None
            for c in range(NCH):
                ct = p.do("vector", "tensor_copy", b.xn[:, c * 512:(c + 1) * 512],
                          b.x1[:, c, ot * 512:(ot + 1) * 512],
                          wait=((PEB, ACB) if c == 0 else ()), inc="DV")
                casts.append(ct)
            srcs = [(b.xn[:, c * 512:(c + 1) * 512], casts[c]) for c in range(NCH)]
            ln.stats(p, bk, srcs, 6, 7)
            a1, a2 = ln.bcast(p, bk, 4, 5)
            xtoks = []
            t1prev = None
            for c in range(NCH):
                d1 = p.do("vector", "tensor_tensor", b.t1[:, :],
                          b.x1[:, c, ot * 512:(ot + 1) * 512], ln.rs_t[:, :],
                          op=ALU.mult,
                          wait=(a1, a2) + ((t1prev,) if t1prev else ()), inc="DV")
                d2 = p.do("vector", "tensor_tensor",
                          b.xn[:, c * 512:(c + 1) * 512], b.t1[:, :],
                          ln.c_t[:, :], op=ALU.add, wait=(d1,), inc="DV")
                t1prev = d2
                xtoks.append(d2)
            w6 = bk.w(6)
            for c in range(NCH):
                p.do("tensor", "matmul", bk.ap(6)[0:1, :], b.ones_col_bf[:, :],
                     b.xn[:, c * 512:(c + 1) * 512], start=(c == 0),
                     stop=(c == NCH - 1), wait=(xtoks[c],) + (w6 if c == 0 else ()),
                     inc="PE" if c == NCH - 1 else None)
            tpe = p.now("PE")
            ccor = p.do("scalar", "activation", b.corr[:, :], bk.ap(6)[0:1, :],
                        AF.Identity, bias=0.0, scale=-128.0, wait=(tpe,), inc="AC")
            bk.r(6, ccor)
            w3 = bk.w(3)
            for hc in range(HC0):
                pb = hc % 2
                wgl = p.load(b.wG[pb][:, :], wg_img[hc, :, :], sem=f"WG{pb}",
                             wait=(wfree["G"][pb],) if wfree["G"][pb] else ())
                wul = p.load(b.wU[pb][:, :], wu_img[hc, :, :], sem=f"WU{pb}",
                             wait=(wfree["U"][pb],) if wfree["U"][pb] else ())
                w0 = bk.w(0)
                for c in range(NCH):
                    p.do("tensor", "matmul", bk.ap(0)[:, :],
                         b.wG[pb][:, c * 128:(c + 1) * 128],
                         b.xn[:, c * 512:(c + 1) * 512], start=(c == 0),
                         stop=False, wait=((wgl,) + w0) if c == 0 else ())
                g_pe = p.do("tensor", "matmul", bk.ap(0)[:, :], b.ones_row_bf[:, :],
                            b.corr[:, :], start=False, stop=True, wait=(ccor,),
                            inc="PE")
                w1 = bk.w(1)
                for c in range(NCH):
                    p.do("tensor", "matmul", bk.ap(1)[:, :],
                         b.wU[pb][:, c * 128:(c + 1) * 128],
                         b.xn[:, c * 512:(c + 1) * 512], start=(c == 0),
                         stop=False, wait=((wul,) + w1) if c == 0 else ())
                u_pe = p.do("tensor", "matmul", bk.ap(1)[:, :], b.ones_row_bf[:, :],
                            b.corr[:, :], start=False, stop=True, inc="PE")
                ga = p.do("scalar", "activation", b.gsil[:, :], bk.ap(0)[:, :],
                          AF.Silu, bias=0.0, scale=b.sg[:, hc:hc + 1],
                          wait=(g_pe, sld[0]) if hc == 0 else (g_pe,), inc="AC")
                bk.r(0, ga)
                wfree["G"][pb] = ga
                ua = p.do("scalar", "activation", b.usc[:, :], bk.ap(1)[:, :],
                          AF.Identity, bias=0.0, scale=b.su[:, hc:hc + 1],
                          wait=(u_pe, sld[1]) if hc == 0 else (u_pe,), inc="AC")
                bk.r(1, ua)
                wfree["U"][pb] = ua
                hd = p.do("vector", "tensor_tensor",
                          b.h[:, hc * 512:(hc + 1) * 512], b.gsil[:, :],
                          b.usc[:, :], op=ALU.mult, wait=(ga, ua), inc="DV")
                p.do("tensor", "matmul", bk.ap(3)[0:1, :], b.ones_col_bf[:, :],
                     b.h[:, hc * 512:(hc + 1) * 512], start=(hc == 0),
                     stop=(hc == HC0 - 1), wait=(hd,) + (w3 if hc == 0 else ()),
                     inc="PE" if hc == HC0 - 1 else None)
            tpe = p.now("PE")
            ccor2 = p.do("scalar", "activation", b.corr2[:, :], bk.ap(3)[0:1, :],
                         AF.Identity, bias=0.0, scale=-128.0, wait=(tpe,), inc="AC")
            bk.r(3, ccor2)
            ytoks = []
            for ec in range(NCH):
                pb = ec % 2
                wdl = p.load(b.wD[pb][:, :], wd_img[ec, :, :], sem=f"WD{pb}",
                             wait=(wfree["D"][pb],) if wfree["D"][pb] else ())
                w2 = bk.w(2)
                for hc in range(HC0):
                    p.do("tensor", "matmul", bk.ap(2)[:, :],
                         b.wD[pb][:, hc * 128:(hc + 1) * 128],
                         b.h[:, hc * 512:(hc + 1) * 512], start=(hc == 0),
                         stop=False, wait=((wdl,) + w2) if hc == 0 else ())
                d_pe = p.do("tensor", "matmul", bk.ap(2)[:, :], b.ones_row_bf[:, :],
                            b.corr2[:, :], start=False, stop=True, wait=(ccor2,),
                            inc="PE")
                ya = p.do("scalar", "activation",
                          b.yb[:, ec * 512:(ec + 1) * 512], bk.ap(2)[:, :],
                          AF.Identity, bias=0.0, scale=b.sdn[:, ec:ec + 1],
                          wait=(d_pe, sld[2]) if ec == 0 else (d_pe,), inc="AC")
                bk.r(2, ya)
                wfree["D"][pb] = ya
                ytoks.append(ya)
            srcs = [(b.yb[:, c * 512:(c + 1) * 512], ytoks[c]) for c in range(NCH)]
            ln.stats(p, bk, srcs, 6, 7)
            a1, a2 = ln.bcast(p, bk, 4, 5)
            t1prev = None
            for c in range(NCH):
                d1 = p.do("vector", "tensor_tensor", b.t1[:, :],
                          b.yb[:, c * 512:(c + 1) * 512], ln.rs_t[:, :],
                          op=ALU.mult, wait=(a1, a2) + ((t1prev,) if t1prev else ()),
                          inc="DV")
                d2 = p.do("vector", "tensor_tensor", b.t1[:, :], b.t1[:, :],
                          ln.c_t[:, :], op=ALU.add, wait=(d1,), inc="DV")
                xu = p.do("vector", "tensor_tensor",
                          b.x1[:, c, ot * 512:(ot + 1) * 512],
                          b.x1[:, c, ot * 512:(ot + 1) * 512], b.t1[:, :],
                          op=ALU.add, wait=(d2,), inc="DV")
                t1prev = xu
                p.load(x_out[c, :, ot * 512:(ot + 1) * 512],
                       b.x1[:, c, ot * 512:(ot + 1) * 512], eng="gpsimd",
                       sem="ST", wait=(xu,))
        ar.pop()

        ar.push()
        b.xn1 = ar.alloc([128, NCH * 512], BF16)
        b.wk1s = ar.alloc([128, NCH * 512], BF16)
        b.wv1s = ar.alloc([128, NCH * 512], BF16)
        b.kv1 = [ar.alloc([128, 512], BF16) for _ in range(2)]
        PHC = (p.now("PE"), p.now("DV"), p.now("AC"),
               p.now("WG0"), p.now("WG1"), p.now("WU0"), p.now("WU1"),
               p.now("WD0"), p.now("WD1"), p.now("SC"))
        wl1 = p.load(b.wk1s[:, :], wk1[:, :], sem="WK1", wait=PHC)
        wl2 = p.load(b.wv1s[:, :], wv1[:, :], sem="WK1")
        wl1 = wl2
        kvst = [None, None]
        for ot in range(2):
            PEB = p.now("PE")
            ACB = p.now("AC")
            STB = p.now("ST")
            casts = []
            for c in range(NCH):
                ct = p.do("vector", "tensor_copy",
                          b.xn1[:, c * 512:(c + 1) * 512],
                          b.x1[:, c, ot * 512:(ot + 1) * 512],
                          wait=((PEB, ACB, STB) if c == 0 else ()), inc="DV")
                casts.append(ct)
            srcs = [(b.xn1[:, c * 512:(c + 1) * 512], casts[c]) for c in range(NCH)]
            ln.stats(p, bk, srcs, 6, 7)
            a1, a2 = ln.bcast(p, bk, 4, 5)
            xtoks = []
            t1prev = None
            for c in range(NCH):
                d1 = p.do("vector", "tensor_tensor", b.t1[:, :],
                          b.x1[:, c, ot * 512:(ot + 1) * 512], ln.rs_t[:, :],
                          op=ALU.mult,
                          wait=(a1, a2) + ((t1prev,) if t1prev else ()), inc="DV")
                d2 = p.do("vector", "tensor_tensor",
                          b.xn1[:, c * 512:(c + 1) * 512], b.t1[:, :],
                          ln.c_t[:, :], op=ALU.add, wait=(d1,), inc="DV")
                t1prev = d2
                xtoks.append(d2)
                p.load(xn1_out[c, :, ot * 512:(ot + 1) * 512],
                       b.xn1[:, c * 512:(c + 1) * 512], eng="gpsimd", sem="ST",
                       wait=(d2,))
            for g in range(4):
                pb = g % 2
                w0 = bk.w(0 + pb)
                for c in range(NCH):
                    p.do("tensor", "matmul", bk.ap(0 + pb)[:, :],
                         b.wk1s[:, c * 512 + g * 128: c * 512 + (g + 1) * 128],
                         b.xn1[:, c * 512:(c + 1) * 512], start=(c == 0),
                         stop=(c == NCH - 1),
                         wait=((wl1, xtoks[c]) + w0) if c == 0 else (xtoks[c],),
                         inc="PE" if c == NCH - 1 else None)
                tpe = p.now("PE")
                ka = p.do("vector", "tensor_copy", b.kv1[pb][:, :],
                          bk.ap(0 + pb)[:, :],
                          wait=(tpe,) + ((kvst[pb],) if kvst[pb] else ()), inc="DV")
                bk.r(0 + pb, ka)
                kvst[pb] = p.load(k1_out[g, :, ot * 512:(ot + 1) * 512],
                                  b.kv1[pb][:, :], eng="gpsimd", sem=f"KV{pb}",
                                  wait=(ka,))
            for j in range(4):
                pb = j % 2
                w2 = bk.w(2 + pb)
                for c in range(NCH):
                    p.do("tensor", "matmul", bk.ap(2 + pb)[:, :],
                         b.xn1[:, c * 512 + j * 128: c * 512 + (j + 1) * 128],
                         b.wv1s[:, c * 512:(c + 1) * 512], start=(c == 0),
                         stop=(c == NCH - 1),
                         wait=((wl2,) + w2) if c == 0 else (),
                         inc="PE" if c == NCH - 1 else None)
                tpe = p.now("PE")
                va = p.do("vector", "tensor_copy", b.kv1[pb][:, :],
                          bk.ap(2 + pb)[:, :],
                          wait=(tpe,) + ((kvst[pb],) if kvst[pb] else ()), inc="DV")
                bk.r(2 + pb, va)
                kvst[pb] = p.load(
                    v1_out[ot * 512 + j * 128: ot * 512 + (j + 1) * 128, :],
                    b.kv1[pb][:, :], eng="gpsimd", sem=f"KV{pb}", wait=(va,))
        ar.pop()

        p.do("gpsimd", "nop", wait=(p.now("ST"), p.now("KV0"), p.now("KV1")))
        p.emit()
    return nc


def _img_dmaj(wT):
    wT = np.asarray(wT)
    out = wT.shape[1]
    return np.ascontiguousarray(
        wT.reshape(NCH, 128, out).transpose(1, 0, 2)).reshape(128, NCH * out)


def prep_launch1(I):
    x0 = np.ascontiguousarray(I["L0_attn_x"].reshape(T, D)).astype(np.float32, copy=False)
    xT = np.ascontiguousarray(x0.T).reshape(NCH, 128, T)
    com = dict(
        xT=xT,
        wq=_img_dmaj(I["L0_attn_Wq"]),
        wk=_img_dmaj(I["L0_attn_Wk"]),
        wv=_img_dmaj(I["L0_attn_Wv"]),
        wo=np.ascontiguousarray(I["L0_attn_Wo"]),
        bq=np.ascontiguousarray(I["L0_attn_bq"]).reshape(128, 1),
        bk=np.ascontiguousarray(I["L0_attn_bk"]).reshape(128, 1),
        bv=np.ascontiguousarray(I["L0_attn_bv"]).reshape(1, 128),
        bo=np.ascontiguousarray(I["L0_attn_bo"].reshape(NCH, 128).T),
        sg=np.ascontiguousarray(I["L0_ffn_s_gate"].reshape(HC0, 128).T),
        su=np.ascontiguousarray(I["L0_ffn_s_up"].reshape(HC0, 128).T),
        sdn=np.ascontiguousarray(I["L0_ffn_s_down"].reshape(NCH, 128).T),
    )

    def img_gu(wq_):
        u8 = wq_.astype(np.uint8)
        wTl = u8.T
        return np.ascontiguousarray(
            wTl.reshape(NCH, 128, HC0, 128).transpose(2, 1, 0, 3)).reshape(HC0, 128, D)

    com["wg_img"] = img_gu(I["L0_ffn_wq_gate"])
    com["wu_img"] = img_gu(I["L0_ffn_wq_up"])
    d8 = I["L0_ffn_wq_down"].astype(np.uint8)
    com["wd_img"] = np.ascontiguousarray(
        d8.T.reshape(HC0, 128, NCH, 128).transpose(2, 1, 0, 3)).reshape(NCH, 128, HB0)

    sel = np.concatenate([np.arange(512 * g, 512 * g + 128) for g in range(4)])
    com["wk1"] = _img_dmaj(I["L1_attn_Wk"][sel].T)
    com["wv1"] = _img_dmaj(I["L1_attn_Wv"][sel].T)

    tri = np.zeros((4, 128, 512), np.float32)
    for j in range(4):
        kk = np.arange(128)[:, None] + j * 128
        qq = np.arange(512)[None, :]
        tri[j] = np.where(kk <= qq, 0.0, KILL / (1.0 / np.sqrt(128.0)))
    tri2 = np.ascontiguousarray(tri.transpose(1, 0, 2)).reshape(128, 4 * 512)

    in_maps = []
    for c in range(NC):
        a, bb = zz_chunks(c)
        own = np.concatenate([x0[a * 512:(a + 1) * 512],
                              x0[bb * 512:(bb + 1) * 512]], 0)
        xo = np.ascontiguousarray(own.T).reshape(NCH, 128, TC)
        kb = np.full((128,), KILL, np.float32)
        for qt, g in enumerate((a, bb)):
            nfull = 4 * g
            for kc in range(NV_L0[qt]):
                kb[qt * 64 + kc] = SHIFT if kc < nfull else KILL
        kbias_arr = np.ascontiguousarray(
            np.broadcast_to(kb[None, :], (128, 128)).copy())
        im = dict(com)
        im["xown"] = xo
        im["kbias"] = kbias_arr
        im["tri"] = tri2
        in_maps.append(im)
    return in_maps


def kernel(**inputs):
    raise NotImplementedError("assembly in progress")



HB1 = 8192
HC1 = 64
NHEAD = 16


def build_launch2(upto=9):
    nc = bass.Bass("TRN2", target_bir_lowering=False, debug=False)
    stack = ExitStack()
    nc._stk = stack

    xin = nc.dram_tensor("xin", [NCH, 128, TC], F32, kind="ExternalInput").ap()
    xn1 = nc.dram_tensor("xn1", [NCH, 128, TC], BF16, kind="ExternalInput").ap()
    k1all = nc.dram_tensor("k1all", [4, 128, T], BF16, kind="ExternalInput").ap()
    v1img = nc.dram_tensor("v1img", [4, 128, T], BF16, kind="ExternalInput").ap()
    k1own = nc.dram_tensor("k1own", [4, 128, TC], BF16, kind="ExternalInput").ap()
    v1ownimg = nc.dram_tensor("v1ownimg", [4, 128, TC], BF16, kind="ExternalInput").ap()
    wq1 = nc.dram_tensor("wq1", [NHEAD, 128, D], F32, kind="ExternalInput").ap()
    wp1 = nc.dram_tensor("wp1", [NCH, 128, D], F32, kind="ExternalInput").ap()
    bp1 = nc.dram_tensor("bp1", [128, NCH], F32, kind="ExternalInput").ap()
    kbias = nc.dram_tensor("kbias", [128, 128], F32, kind="ExternalInput").ap()
    tri = nc.dram_tensor("tri", [128, 4 * 512], F32, kind="ExternalInput").ap()
    wup = nc.dram_tensor("wup", [HC1, 128, D], F32, kind="ExternalInput").ap()
    wgt = nc.dram_tensor("wgt", [HC1, 128, D], F32, kind="ExternalInput").ap()
    wdn1 = nc.dram_tensor("wdn1", [NCH, 128, HB1], F32, kind="ExternalInput").ap()
    su_ug = nc.dram_tensor("su_ug", [128, NCH], F32, kind="ExternalInput").ap()
    su_dp = nc.dram_tensor("su_dp", [128, HC1], F32, kind="ExternalInput").ap()
    wsc_up = nc.dram_tensor("wsc_up", [128, HC1], F32, kind="ExternalInput").ap()
    wsc_gt = nc.dram_tensor("wsc_gt", [128, HC1], F32, kind="ExternalInput").ap()
    wsc_dp = nc.dram_tensor("wsc_dp", [128, NCH], F32, kind="ExternalInput").ap()
    h128 = nc.dram_tensor("h128", [128, 128], BF16, kind="ExternalInput").ap()
    wq2 = nc.dram_tensor("wq2", [NHEAD, 128, D], F32, kind="ExternalInput").ap()
    wk2 = nc.dram_tensor("wk2", [NHEAD, 128, D], F32, kind="ExternalInput").ap()
    wv2 = nc.dram_tensor("wv2", [NHEAD, 128, D], F32, kind="ExternalInput").ap()
    cosT = nc.dram_tensor("cosT", [128, TC], F32, kind="ExternalInput").ap()
    sinT = nc.dram_tensor("sinT", [128, TC], F32, kind="ExternalInput").ap()
    qdec_t = nc.dram_tensor("qdec_t", [128, NHEAD * 128], F32, kind="ExternalInput").ap()
    kdec_c = nc.dram_tensor("kdec_c", [128, NHEAD], F32, kind="ExternalInput").ap()
    dmask = nc.dram_tensor("dmask", [128, NHEAD * 128], F32, kind="ExternalInput").ap()

    x_out = nc.dram_tensor("x_out", [NCH, 128, TC], F32, kind="ExternalOutput").ap()
    a2p_out = nc.dram_tensor("a2p_out", [NHEAD, 128, TC], BF16, kind="ExternalOutput").ap()
    qd_out = nc.dram_tensor("qd_out", [NHEAD, 128, TC], BF16, kind="ExternalOutput").ap()
    sum_out = nc.dram_tensor("sum_out", [2, NHEAD, 128, 128], BF16, kind="ExternalOutput").ap()

    RC = _l2_rc()

    with stack:
        p = Prog(nc, stack)
        ar = Arena(nc, stack, "arena", 206 * 1024)
        bk = Banks(nc, stack)
        b = B()

        b.ones_col_bf = ar.alloc([128, 1], BF16)
        b.epscol = ar.alloc([128, 1], F32)
        b.shiftcol = ar.alloc([128, 1], F32)
        b.ones_row_f = ar.alloc([1, 128], F32)
        b.ones_row_bf = ar.alloc([1, 128], BF16)
        b.sq = ar.alloc([128, 512], BF16)
        b.idf = ar.alloc([128, 128], F32)
        b.t1 = ar.alloc([128, 512], F32)
        ln = LNH(p, b, ar)

        b.idb = ar.alloc([128, 128], BF16)
        ident_dram = nc.inline_tensor(np.eye(128, dtype=np.float32), "idf_c").ap()
        identb_dram = nc.inline_tensor(np.eye(128, dtype=BF), "idb_c").ap()
        p.load(b.idf[:, :], ident_dram[:, :], sem="LC")
        p.load(b.idb[:, :], identb_dram[:, :], sem="LC")
        p.do("gpsimd", "memset", b.ones_col_bf[:, :], 1.0, inc="GP")
        p.do("gpsimd", "memset", b.ones_row_f[:, :], 1.0, inc="GP")
        p.do("gpsimd", "memset", b.ones_row_bf[:, :], 1.0, inc="GP")
        p.do("gpsimd", "memset", b.epscol[:, :], 1e-5, inc="GP")
        p.do("gpsimd", "memset", b.shiftcol[:, :], SHIFT, inc="GP")
        CLD = p.now("LC")
        CGP = p.now("GP")

        scale0 = 1.0 / np.sqrt(128.0)

        ar.push()
        b.x1 = ar.alloc([128, NCH, TC], F32)
        b.Qf = ar.alloc([128, NHEAD * TC], BF16)
        b.attn = ar.alloc([128, NHEAD * TC], BF16)
        b.recip = ar.alloc([1, 512], F32)
        b.rcp_t = ar.alloc([128, 512], F32)
        b.kbias = ar.alloc([128, 128], F32)
        b.tri = ar.alloc([128, 4 * 512], F32)
        p.load(b.kbias[:, :], kbias[:, :], sem="LC2")
        p.load(b.tri[:, :], tri[:, :], sem="LC2")
        KB2 = p.now("LC2")
        for ch in range(NCH):
            p.load(b.x1[:, ch, :], xin[ch, :, :], eng="sync", sem="X1")
        X1LD = p.now("X1")

        ar.push()
        b.xn1s = ar.alloc([128, NCH * TC], BF16)
        b.wb = [ar.alloc([128, D], BF16) for _ in range(2)]
        for ch in range(NCH):
            p.load(b.xn1s[:, ch * TC:(ch + 1) * TC], xn1[ch, :, :], sem="XN")
        XNL = p.now("XN")
        wfree = [None, None]
        for e in range(NHEAD):
            pb = e % 2
            wl = p.load(b.wb[pb][:, :], wq1[e, :, :], sem=f"WA{pb}",
                        wait=(wfree[pb],) if wfree[pb] else ())
            for ot in range(2):
                ww = bk.w(pb)
                for c in range(NCH):
                    p.do("tensor", "matmul", bk.ap(pb)[:, :],
                         b.wb[pb][:, c * 128:(c + 1) * 128],
                         b.xn1s[:, c * TC + ot * 512: c * TC + (ot + 1) * 512],
                         start=(c == 0), stop=(c == NCH - 1),
                         wait=((wl, XNL, CGP) + ww) if c == 0 else (),
                         inc="PE" if c == NCH - 1 else None)
                tpe = p.now("PE")
                a = p.do("scalar", "copy",
                         b.Qf[:, e * TC + ot * 512: e * TC + (ot + 1) * 512],
                         bk.ap(pb)[:, :], wait=(tpe,), inc="AC")
                bk.r(pb, a)
                wfree[pb] = a
        ar.pop()

        if upto < 1:
            ar.pop()
            p.do("gpsimd", "nop", wait=(p.now("ST"),))
            p.emit()
            return nc
        ar.push()
        b.Kg = ar.alloc([128, T], BF16)
        b.Vg = ar.alloc([128, T], BF16)
        b.Kow = ar.alloc([128, TC], BF16)
        b.Vow = ar.alloc([128, TC], BF16)
        b.pt2 = ar.alloc([128, 2048], BF16)
        kvfree = None
        prev_rcp = [None]
        PH1 = (p.now("PE"), p.now("AC"))
        for g in range(4):
            lk = p.load(b.Kg[:, :], k1all[g, :, :], sem="KG",
                        wait=((kvfree,) if kvfree else ()) + (PH1 if g == 0 else ()))
            lv = p.load(b.Vg[:, :], v1img[g, :, :], sem="KG")
            lko = p.load(b.Kow[:, :], k1own[g, :, :], sem="KG")
            lvo = p.load(b.Vow[:, :], v1ownimg[g, :, :], sem="KG")
            KGL = p.now("KG")
            def build_g(core):
                a_c, b_c = zz_chunks(core)
                nonlocal_box = None
                for sct in range(4):
                    e = 4 * g + sct
                    vis = ([(0, "main", kc) for kc in range(4 * a_c)]
                           + [(0, "diag", j) for j in range(4)]
                           + [(1, "main", kc) for kc in range(4 * b_c)]
                           + [(1, "diag", j) for j in range(4)])
                    assert len(vis) == 68
                    nv0 = 4 * a_c + 4
                    pairs = [(vis[i], vis[i + 1]) for i in range(0, 68, 2)]
                    prev_ae = [None, None]
                    wN0 = bk.w(5)
                    wA0 = bk.w(4)
                    wN1 = bk.w(7)
                    wA1 = bk.w(6)
                    vi_box = [0]

                    def navs(pr, ae, ppi):
                        half = (ppi % 2) * 1024
                        for sl, v in enumerate(pr):
                            qt, kind, kc = v
                            vi = vi_box[0]
                            if kind == "main":
                                vap = b.Vg[:, kc * 128:(kc + 1) * 128]
                            else:
                                vap = b.Vow[:, qt * 512 + kc * 128:
                                            qt * 512 + (kc + 1) * 128]
                            first = (vi == 0) if qt == 0 else (vi == nv0)
                            last = (vi == nv0 - 1) if qt == 0 else (vi == 67)
                            bN = 5 if qt == 0 else 7
                            bA = 4 if qt == 0 else 6
                            wN_ = (wN0 if qt == 0 else wN1) if first else ()
                            wA_ = (wA0 if qt == 0 else wA1) if first else ()
                            p.do("tensor", "matmul", bk.ap(bN)[0:1, :],
                                 b.ones_col_bf[:, :],
                                 b.pt2[:, half + sl * 512:half + (sl + 1) * 512],
                                 wait=(ae,) + wN_, start=first, stop=last)
                            p.do("tensor", "matmul", bk.ap(bA)[:, :], vap,
                                 b.pt2[:, half + sl * 512:half + (sl + 1) * 512],
                                 wait=wA_, start=first, stop=last,
                                 inc="PE" if vi == 67 else None)
                            vi_box[0] = vi + 1

                    pend = None
                    for pi, pr in enumerate(pairs):
                        sb0 = (pi % 2) * 2
                        half = (pi % 2) * 1024
                        ts_toks = []
                        for sl, v in enumerate(pr):
                            qt, kind, kc = v
                            if kind == "main":
                                kap = b.Kg[:, kc * 128:(kc + 1) * 128]
                            else:
                                kap = b.Kow[:, qt * 512 + kc * 128:
                                            qt * 512 + (kc + 1) * 128]
                            ts = p.do("tensor", "matmul", bk.ap(sb0 + sl)[:, :],
                                      kap,
                                      b.Qf[:, e * TC + qt * 512:
                                           e * TC + (qt + 1) * 512],
                                      start=True, stop=True,
                                      wait=((prev_ae[pi % 2],)
                                            if prev_ae[pi % 2] else ())
                                      + bk.w(sb0 + sl)
                                      + ((KGL, CGP, CLD, KB2)
                                         if (pi == 0 and sct == 0) else ()),
                                      inc="PE")
                            if kind == "diag":
                                ts = p.do("vector", "tensor_tensor",
                                          bk.ap(sb0 + sl)[:, :],
                                          bk.ap(sb0 + sl)[:, :],
                                          b.tri[:, kc * 512:(kc + 1) * 512],
                                          op=ALU.add, wait=(ts,), inc="DV")
                            ts_toks.append(ts)
                        ae = None
                        for sl in range(2):
                            ae = p.do("scalar", "activation",
                                      b.pt2[:, half + sl * 512:
                                            half + (sl + 1) * 512],
                                      bk.ap(sb0 + sl)[:, :], AF.Exp,
                                      bias=b.shiftcol[:, :], scale=scale0,
                                      wait=(ts_toks[sl],) + ((ae,) if ae else ()),
                                      inc="AC")
                        prev_ae[pi % 2] = ae
                        bk.r(sb0, ae)
                        bk.r(sb0 + 1, ae)
                        if pend is not None:
                            navs(*pend)
                        pend = (pr, ae, pi)
                    navs(*pend)
                    for qt in (0, 1):
                        bN = 5 if qt == 0 else 7
                        bA = 4 if qt == 0 else 6
                        bT = 0 if qt == 0 else 1
                        pe = p.now("PE")
                        r1 = p.do("vector", "reciprocal", b.recip[:, :],
                                  bk.ap(bN)[0:1, :], wait=(pe,), inc="DV")
                        bk.r(bN, r1)
                        t1 = p.do("tensor", "matmul", bk.ap(bT)[:, :],
                                  b.ones_row_f[:, :], b.recip[:, :],
                                  start=True, stop=True,
                                  wait=(r1,) + bk.w(bT), inc="PE")
                        a1 = p.do("scalar", "copy", b.rcp_t[:, :],
                                  bk.ap(bT)[:, :],
                                  wait=(t1,) + ((prev_rcp[0],)
                                                if prev_rcp[0] else ()),
                                  inc="AC")
                        bk.r(bT, a1)
                        dv = p.do("vector", "tensor_tensor",
                                  b.attn[:, e * TC + qt * 512:
                                         e * TC + (qt + 1) * 512],
                                  bk.ap(bA)[:, :], b.rcp_t[:, :], op=ALU.mult,
                                  wait=(a1,), inc="DV")
                        prev_rcp[0] = dv
                        bk.r(bA, dv)

            p.percore(bk, build_g, boxes=(prev_rcp,))
            kvfree = p.now("DV")
        ar.pop()

        if upto < 2:
            ar.pop()
            p.do("gpsimd", "nop", wait=(p.now("ST"),))
            p.emit()
            return nc
        ar.push()
        b.wpb = [ar.alloc([128, D], BF16) for _ in range(2)]
        b.yb = ar.alloc([128, NCH * TC], BF16)
        b.bp1 = ar.alloc([128, NCH], F32)
        p.load(b.bp1[:, :], bp1[:, :], sem="BP")
        BPL = p.now("BP")
        wfree = [None, None]
        ATN = p.now("DV")
        PH2 = (p.now("PE"), p.now("AC"), ATN)
        for ec in range(NCH):
            pb = ec % 2
            wl = p.load(b.wpb[pb][:, :], wp1[ec, :, :], sem=f"WB{pb}",
                        wait=((wfree[pb],) if wfree[pb] else ()) + (PH2 if ec < 2 else ()))
            for ot in range(2):
                ww = bk.w(pb)
                for hc in range(NCH):
                    p.do("tensor", "matmul", bk.ap(pb)[:, :],
                         b.wpb[pb][:, hc * 128:(hc + 1) * 128],
                         b.attn[:, hc * TC + ot * 512: hc * TC + (ot + 1) * 512],
                         start=(hc == 0), stop=(hc == NCH - 1),
                         wait=((wl, ATN, BPL) + ww) if hc == 0 else (),
                         inc="PE" if hc == NCH - 1 else None)
                tpe = p.now("PE")
                a = p.do("scalar", "activation",
                         b.yb[:, ec * TC + ot * 512: ec * TC + (ot + 1) * 512],
                         bk.ap(pb)[:, :], AF.Identity, bias=b.bp1[:, ec:ec + 1],
                         scale=1.0, wait=(tpe,), inc="AC")
                bk.r(pb, a)
                wfree[pb] = a
        YT2 = p.now("AC")
        for ot in range(2):
            srcs = [(b.yb[:, c * TC + ot * 512: c * TC + (ot + 1) * 512],
                     YT2 if c == 0 else None) for c in range(NCH)]
            ln.stats(p, bk, srcs, 6, 7)
            a1, a2 = ln.bcast(p, bk, 4, 5)
            t1prev = None
            for c in range(NCH):
                d1 = p.do("vector", "tensor_tensor", b.t1[:, :],
                          b.yb[:, c * TC + ot * 512: c * TC + (ot + 1) * 512],
                          ln.rs_t[:, :], op=ALU.mult,
                          wait=(a1, a2) + ((t1prev,) if t1prev else ()), inc="DV")
                d2 = p.do("vector", "tensor_tensor", b.t1[:, :], b.t1[:, :],
                          ln.c_t[:, :], op=ALU.add, wait=(d1,), inc="DV")
                xu = p.do("vector", "tensor_tensor",
                          b.x1[:, c, ot * 512:(ot + 1) * 512],
                          b.x1[:, c, ot * 512:(ot + 1) * 512], b.t1[:, :],
                          op=ALU.add, wait=(d2, X1LD), inc="DV")
                t1prev = xu
                p.load(x_out[c, :, ot * 512:(ot + 1) * 512],
                       b.x1[:, c, ot * 512:(ot + 1) * 512], eng="gpsimd",
                       sem="ST", wait=(xu,))
        ar.pop()
        ar.pop()

        if upto < 3:
            p.do("gpsimd", "nop", wait=(p.now("ST"), p.now("SX0"), p.now("SX1")))
            p.emit()
            return nc
        ar.push()
        b.xn = ar.alloc([128, NCH * 512], BF16)
        b.xh2 = ar.alloc([128, NCH * 512], BF16)
        b.h = ar.alloc([128, HC1 * 512], BF16)
        b.yb = b.xh2
        b.gsil = ar.alloc([128, 512], BF16)
        b.usc = ar.alloc([128, 512], BF16)
        b.wG = [ar.alloc([128, D], BF16) for _ in range(2)]
        b.wU = [ar.alloc([128, D], BF16) for _ in range(2)]
        b.wD = [ar.alloc([128, HB1], BF16) for _ in range(2)]
        b.Hs1 = ar.alloc([128, NCH * 128], BF16)
        b.Hs2 = ar.alloc([128, HC1 * 128], BF16)
        b.Hs2n = ar.alloc([128, HC1 * 128], BF16)
        b.h128 = ar.alloc([128, 128], BF16)
        b.suc = ar.alloc([128, NCH], F32)
        b.sdc = ar.alloc([128, HC1], F32)
        b.scup = ar.alloc([128, HC1], F32)
        b.scgt = ar.alloc([128, HC1], F32)
        b.scdp = ar.alloc([128, NCH], F32)
        b.xf = [ar.alloc([128, 512], F32) for _ in range(2)]

        PH3 = (p.now("PE"), p.now("DV"), p.now("AC"), p.now("ST"),
               p.now("KG"), p.now("WB0"), p.now("WB1"))
        p.load(b.h128[:, :], h128[:, :], sem="SC", wait=PH3)
        p.load(b.suc[:, :], su_ug[:, :], sem="SC")
        p.load(b.sdc[:, :], su_dp[:, :], sem="SC")
        p.load(b.scup[:, :], wsc_up[:, :], sem="SC")
        p.load(b.scgt[:, :], wsc_gt[:, :], sem="SC")
        p.load(b.scdp[:, :], wsc_dp[:, :], sem="SC")
        SCL = p.now("SC")
        s1 = p.do("vector", "tensor_scalar", b.scup[:, :], b.scup[:, :],
                  1.0 / np.sqrt(2048.0), None, op0=ALU.mult, wait=(SCL,), inc="DV")
        s2 = p.do("vector", "tensor_scalar", b.scgt[:, :], b.scgt[:, :],
                  1.0 / np.sqrt(2048.0), None, op0=ALU.mult, wait=(SCL,), inc="DV")
        s3 = p.do("vector", "tensor_scalar", b.scdp[:, :], b.scdp[:, :],
                  1.0 / np.sqrt(8192.0), None, op0=ALU.mult, wait=(SCL,), inc="DV")
        hs_toks = []
        for c in range(NCH):
            tkk = p.do("vector", "tensor_scalar", b.Hs1[:, c * 128:(c + 1) * 128],
                       b.h128[:, :], b.suc[:, c:c + 1], None, op0=ALU.mult,
                       wait=(SCL,), inc="DV")
            hs_toks.append(tkk)
        for c in range(HC1):
            tkk = p.do("vector", "tensor_scalar", b.Hs2[:, c * 128:(c + 1) * 128],
                       b.h128[:, :], b.sdc[:, c:c + 1], None, op0=ALU.mult,
                       wait=(SCL,), inc="DV")
            hs_toks.append(tkk)
            tkk = p.do("vector", "tensor_scalar",
                       b.Hs2n[:, c * 128:(c + 1) * 128],
                       b.h128[:, :], b.sdc[:, c:c + 1], -1.0, op0=ALU.mult,
                       op1=ALU.mult, wait=(SCL,), inc="DV")
            hs_toks.append(tkk)
        HST = p.now("DV")

        wfree = {"G": [None, None], "U": [None, None], "D": [None, None]}
        bfD = [None]
        bfG = [None]
        for ot in range(2):
            PEB = p.now("PE"); ACB = p.now("AC"); STB = p.now("ST")
            SXB = (p.now("SX0"), p.now("SX1"))
            xls = []
            for c in range(NCH):
                xl = p.load(b.xn[:, c * 512:(c + 1) * 512],
                            x_out[c, :, ot * 512:(ot + 1) * 512],
                            sem=f"XF{c % 2}",
                            wait=(xls[c - 2],) if c >= 2 else (PEB, ACB, STB) + SXB)
                xls.append(xl)
            srcs = [(b.xn[:, c * 512:(c + 1) * 512], xls[c]) for c in range(NCH)]
            ln.stats(p, bk, srcs, 6, 7)
            a1, a2 = ln.bcast(p, bk, 4, 5)
            xtoks = []
            t1prev = None
            for c in range(NCH):
                xl = p.load(b.xf[c % 2][:, :], x_out[c, :, ot * 512:(ot + 1) * 512],
                            sem=f"XF{c % 2}",
                            wait=(xtoks[c - 2],) if c >= 2 else (a1, a2))
                d1 = p.do("vector", "tensor_tensor", b.t1[:, :], b.xf[c % 2][:, :],
                          ln.rs_t[:, :], op=ALU.mult,
                          wait=(xl, a1, a2) + ((t1prev,) if t1prev else ()), inc="DV")
                d2 = p.do("vector", "tensor_tensor",
                          b.xn[:, c * 512:(c + 1) * 512], b.t1[:, :],
                          ln.c_t[:, :], op=ALU.add, wait=(d1,), inc="DV")
                t1prev = d2
                xtoks.append(d2)
            sat1 = [None] * NCH
            for c in range(NCH):
                bi = c % 4
                tpe = p.do("tensor", "matmul", bk.ap(bi)[:, :],
                           b.Hs1[:, c * 128:(c + 1) * 128],
                           b.xn[:, c * 512:(c + 1) * 512], start=True, stop=True,
                           wait=(xtoks[c], HST) + bk.w(bi), inc="PE")
                a = p.do("scalar", "copy", b.xh2[:, c * 512:(c + 1) * 512],
                         bk.ap(bi)[:, :], wait=(tpe,), inc="AC")
                bk.r(bi, a)
                sat1[c] = a
            dprev = None
            for hh in (1, 2, 4, 8):
                W = hh * 512
                for q in range(NCH // (2 * hh)):
                    base = q * 2 * W
                    a0, b0 = base, base + W
                    wv = ((sat1[2 * q * hh + 2 * hh - 1],) if hh == 1 else ()) + \
                        ((dprev,) if dprev else ())
                    if hh == 1:
                        wv = (sat1[2 * q + 1],) + ((dprev,) if dprev else ())
                    e1 = p.do("vector", "tensor_tensor",
                              b.xh2[:, a0:a0 + W], b.xh2[:, a0:a0 + W],
                              b.xh2[:, b0:b0 + W], op=ALU.add,
                              wait=wv, inc="DV")
                    dprev = p.do("vector", "scalar_tensor_tensor",
                                 b.xh2[:, b0:b0 + W],
                                 b.xh2[:, b0:b0 + W], -2.0,
                                 b.xh2[:, a0:a0 + W],
                                 op0=ALU.mult, op1=ALU.add, wait=(e1,), inc="DV")
            XH = b.xh2
            XHT = p.now("DV")
            XHTG = p.now("GP")
            satok = [None] * HC1
            bft = {}

            def emit_bf(hh, q):
                W = hh * 512
                wDc = W
                wGc = 0
                base = q * 2 * W
                a0, b0 = base, base + W
                if hh == 4:
                    wD_ = (satok[8 * q + 7],)
                    wG_ = ()
                else:
                    s0 = bft[(hh // 2, 2 * q)]
                    s1 = bft[(hh // 2, 2 * q + 1)]
                    wD_ = tuple(t for t in (s0[1], s1[1]) if t is not None)
                    wG_ = (s0[0], s1[0])
                if bfD[0] is not None:
                    wD_ = wD_ + (bfD[0],)
                e1 = p.do("vector", "tensor_tensor", b.h[:, a0:a0 + wDc],
                          b.h[:, a0:a0 + wDc], b.h[:, b0:b0 + wDc],
                          op=ALU.add, wait=wD_, inc="DV")
                dT = p.do("vector", "scalar_tensor_tensor", b.h[:, b0:b0 + wDc],
                          b.h[:, b0:b0 + wDc], -2.0, b.h[:, a0:a0 + wDc],
                          op0=ALU.mult, op1=ALU.add, wait=(e1,), inc="DV")
                gT = None
                if wGc:
                    if bfG[0] is not None:
                        wG_ = wG_ + (bfG[0],)
                    f0 = p.do("gpsimd", "tensor_copy", b.tmpG[:, 0:wGc],
                              b.h[:, a0 + wDc:a0 + W], wait=wG_, inc="GP")
                    f1 = p.do("gpsimd", "tensor_tensor", b.h[:, a0 + wDc:a0 + W],
                              b.tmpG[:, 0:wGc], b.h[:, b0 + wDc:b0 + W],
                              op=ALU.add, wait=(f0,), inc="GP")
                    gT = p.do("gpsimd", "tensor_tensor",
                              b.h[:, b0 + wDc:b0 + W], b.tmpG[:, 0:wGc],
                              b.h[:, b0 + wDc:b0 + W], op=ALU.subtract,
                              wait=(f1,), inc="GP")
                    bfG[0] = gT
                bft[(hh, q)] = (dT, gT)
                bfD[0] = dT

            prev_hd = None
            hmt = []
            for hc in range(HC1):
                pb = hc % 2
                wgl = p.load(b.wG[pb][:, :], wgt[hc, :, :], sem=f"WG{pb}",
                             wait=(wfree["G"][pb],) if wfree["G"][pb] else ())
                wul = p.load(b.wU[pb][:, :], wup[hc, :, :], sem=f"WU{pb}",
                             wait=(wfree["U"][pb],) if wfree["U"][pb] else ())
                w0 = bk.w(0)
                for c in range(NCH):
                    p.do("tensor", "matmul", bk.ap(0)[:, :],
                         b.wG[pb][:, c * 128:(c + 1) * 128],
                         XH[:, c * 512:(c + 1) * 512], start=(c == 0),
                         stop=(c == NCH - 1),
                         wait=((wgl, XHT, XHTG) + w0) if c == 0 else (),
                         inc="PE" if c == NCH - 1 else None)
                g_pe = p.now("PE")
                w1 = bk.w(1)
                for c in range(NCH):
                    p.do("tensor", "matmul", bk.ap(1)[:, :],
                         b.wU[pb][:, c * 128:(c + 1) * 128],
                         XH[:, c * 512:(c + 1) * 512], start=(c == 0),
                         stop=(c == NCH - 1),
                         wait=((wul,) + w1) if c == 0 else (),
                         inc="PE" if c == NCH - 1 else None)
                u_pe = p.now("PE")
                ga = p.do("scalar", "activation", b.gsil[:, :], bk.ap(0)[:, :],
                          AF.Silu, bias=0.0, scale=b.scgt[:, hc:hc + 1],
                          wait=(g_pe, s2) + ((prev_hd,) if prev_hd else ()),
                          inc="AC")
                bk.r(0, ga)
                wfree["G"][pb] = ga
                ua = p.do("scalar", "activation", b.usc[:, :], bk.ap(1)[:, :],
                          AF.Identity, bias=0.0, scale=b.scup[:, hc:hc + 1],
                          wait=(u_pe, s1), inc="AC")
                bk.r(1, ua)
                wfree["U"][pb] = ua
                prev_hd = p.do("vector", "tensor_tensor",
                               b.h[:, hc * 512:(hc + 1) * 512],
                               b.gsil[:, :], b.usc[:, :], op=ALU.mult,
                               wait=(ga, ua), inc="DV")
                hmt.append(prev_hd)
            HDONE = p.now("DV")
            H4 = ((1, 1, 1, 1), (1, -1, 1, -1), (1, 1, -1, -1), (1, -1, -1, 1))
            gbanks = (0, 1, 3, 4)
            for g4 in range(HC1 // 4):
                toks = []
                for r in range(4):
                    bi = gbanks[r]
                    ww = bk.w(bi)
                    for i in range(4):
                        ic = 4 * g4 + i
                        stat = b.Hs2 if H4[r][i] > 0 else b.Hs2n
                        p.do("tensor", "matmul", bk.ap(bi)[:, :],
                             stat[:, ic * 128:(ic + 1) * 128],
                             b.h[:, ic * 512:(ic + 1) * 512],
                             start=(i == 0), stop=(i == 3),
                             wait=((hmt[4 * g4 + 3], HST) + ww) if i == 0 else (),
                             inc="PE" if i == 3 else None)
                pe4 = p.now("PE")
                for r in range(4):
                    oc = 4 * g4 + r
                    av = p.do("scalar", "copy", b.h[:, oc * 512:(oc + 1) * 512],
                              bk.ap(gbanks[r])[:, :], wait=(pe4,), inc="AC")
                    bk.r(gbanks[r], av)
                    satok[oc] = av
            for hh in (4, 8, 16, 32):
                for q in range(HC1 // (2 * hh)):
                    emit_bf(hh, q)
            ytoks = []
            for ec in range(NCH):
                pb = ec % 2
                wdl = p.load(b.wD[pb][:, :], wdn1[ec, :, :], sem=f"WD{pb}",
                             wait=(wfree["D"][pb],) if wfree["D"][pb] else ())
                w2 = bk.w(2)
                for hc in range(HC1):
                    p.do("tensor", "matmul", bk.ap(2)[:, :],
                         b.wD[pb][:, hc * 128:(hc + 1) * 128],
                         b.h[:, hc * 512:(hc + 1) * 512], start=(hc == 0),
                         stop=(hc == HC1 - 1),
                         wait=((wdl,) + bft[(32, 0)] + w2) if hc == 0 else (),
                         inc="PE" if hc == HC1 - 1 else None)
                d_pe = p.now("PE")
                ya = p.do("scalar", "activation",
                          b.yb[:, ec * 512:(ec + 1) * 512], bk.ap(2)[:, :],
                          AF.Identity, bias=0.0, scale=b.scdp[:, ec:ec + 1],
                          wait=(d_pe, s3), inc="AC")
                bk.r(2, ya)
                wfree["D"][pb] = ya
                ytoks.append(ya)
            srcs = [(b.yb[:, c * 512:(c + 1) * 512], ytoks[c]) for c in range(NCH)]
            ln.stats(p, bk, srcs, 6, 7)
            a1, a2 = ln.bcast(p, bk, 4, 5)
            t1prev = None
            sttoks = []
            for c in range(NCH):
                xl = p.load(b.xf[c % 2][:, :], x_out[c, :, ot * 512:(ot + 1) * 512],
                            sem=f"XF{c % 2}",
                            wait=(sttoks[c - 2],) if c >= 2 else (a1, a2))
                d1 = p.do("vector", "tensor_tensor", b.t1[:, :],
                          b.yb[:, c * 512:(c + 1) * 512], ln.rs_t[:, :],
                          op=ALU.mult, wait=(a1, a2) + ((t1prev,) if t1prev else ()),
                          inc="DV")
                d2 = p.do("vector", "tensor_tensor", b.t1[:, :], b.t1[:, :],
                          ln.c_t[:, :], op=ALU.add, wait=(d1,), inc="DV")
                xu = p.do("vector", "tensor_tensor", b.xf[c % 2][:, :],
                          b.xf[c % 2][:, :], b.t1[:, :], op=ALU.add,
                          wait=(d2, xl), inc="DV")
                t1prev = xu
                stt = p.load(x_out[c, :, ot * 512:(ot + 1) * 512],
                             b.xf[c % 2][:, :], eng="gpsimd", sem=f"SX{c % 2}",
                             wait=(xu,))
                sttoks.append(stt)
        ar.pop()

        if upto < 4:
            p.do("gpsimd", "nop", wait=(p.now("ST"), p.now("SX0"), p.now("SX1")))
            p.emit()
            return nc
        ar.push()
        b.xn2 = ar.alloc([128, NCH * TC], BF16)
        b.Vall = ar.alloc([128, 8 * D], BF16)
        b.kv2 = [ar.alloc([128, 128], F32) for _ in range(2)]
        b.kvbf2 = [ar.alloc([128, 128], BF16) for _ in range(2)]
        b.sm4 = [[ar.alloc([128, 128], BF16) for _ in range(2)]
                 for _ in range(2)]
        b.ra = ar.alloc([128, TC], F32)
        b.rb = ar.alloc([128, TC], F32)
        b.rsw = ar.alloc([128, TC], BF16)
        b.rtp = ar.alloc([128, TC], BF16)
        b.rt2 = ar.alloc([128, TC], F32)
        b.cosT = ar.alloc([128, TC], F32)
        b.sinT = ar.alloc([128, TC], F32)
        b.qdt = ar.alloc([128, NHEAD * 128], F32)
        b.kdc = ar.alloc([128, NHEAD], F32)
        b.dmk = ar.alloc([128, NHEAD * 128], F32)
        b.xf2 = [ar.alloc([128, 512], F32) for _ in range(2)]

        PH4 = (p.now("PE"), p.now("DV"), p.now("AC"), p.now("ST"),
               p.now("WG0"), p.now("WG1"), p.now("WU0"), p.now("WU1"),
               p.now("WD0"), p.now("WD1"), p.now("SC"), p.now("SX0"), p.now("SX1"),
               p.now("XF0"), p.now("XF1"))
        p.load(b.cosT[:, :], cosT[:, :], sem="C2", wait=PH4)
        p.load(b.sinT[:, :], sinT[:, :], sem="C2")
        p.load(b.qdt[:, :], qdec_t[:, :], sem="C2")
        p.load(b.kdc[:, :], kdec_c[:, :], sem="C2")
        p.load(b.dmk[:, :], dmask[:, :], sem="C2")
        C2L = p.now("C2")

        ar.push()
        b.wvA = ar.alloc([128, NCH * D], BF16)
        VB = (p.now("PE"), p.now("AC"), p.now("GP"), p.now("ST"), p.now("DV"))
        vls = []
        for c in range(NCH):
            vl = p.load(b.wvA[:, c * D:(c + 1) * D], wv2[c, :, :], sem="W2V",
                        wait=VB if c == 0 else ())
            vls.append(vl)

        for ot in range(2):
            OTB = PH4 if ot == 0 else (p.now("DV"),)
            xls = []
            for c in range(NCH):
                xl = p.load(b.xn2[:, c * TC + ot * 512: c * TC + (ot + 1) * 512],
                            x_out[c, :, ot * 512:(ot + 1) * 512],
                            sem=f"XG{c % 2}",
                            wait=(xls[c - 2],) if c >= 2 else OTB)
                xls.append(xl)
            srcs = [(b.xn2[:, c * TC + ot * 512: c * TC + (ot + 1) * 512],
                     xls[c]) for c in range(NCH)]
            ln.stats(p, bk, srcs, 6, 7)
            a1, a2 = ln.bcast(p, bk, 4, 5)
            xtoks = []
            t1prev = None
            for c in range(NCH):
                xl = p.load(b.xf2[c % 2][:, :], x_out[c, :, ot * 512:(ot + 1) * 512],
                            sem=f"XG{c % 2}",
                            wait=(xtoks[c - 2],) if c >= 2 else (a1, a2))
                d1 = p.do("vector", "tensor_tensor", b.t1[:, :], b.xf2[c % 2][:, :],
                          ln.rs_t[:, :], op=ALU.mult,
                          wait=(xl, a1, a2) + ((t1prev,) if t1prev else ()),
                          inc="DV")
                d2 = p.do("vector", "tensor_tensor",
                          b.xn2[:, c * TC + ot * 512: c * TC + (ot + 1) * 512],
                          b.t1[:, :], ln.c_t[:, :], op=ALU.add, wait=(d1,),
                          inc="DV")
                t1prev = d2
                xtoks.append(d2)
        XN2 = p.now("DV")

        for js in range(32):
            j, sx = js // 4, js % 4
            bi = 1 + (js % 2)
            ww = bk.w(bi)
            for c in range(NCH):
                if js == 0:
                    mmw = ((vls[-1], XN2) + ww) if c == 0 else ()
                else:
                    mmw = ww if c == 0 else ()
                p.do("tensor", "matmul", bk.ap(bi)[:, :],
                     b.xn2[:, c * TC + j * 128:c * TC + (j + 1) * 128],
                     b.wvA[:, c * D + sx * 512:c * D + (sx + 1) * 512],
                     start=(c == 0), stop=(c == NCH - 1),
                     wait=mmw, inc="PE" if c == NCH - 1 else None)
            tpe = p.now("PE")
            a = p.do("scalar", "copy",
                     b.Vall[:, j * D + sx * 512:j * D + (sx + 1) * 512],
                     bk.ap(bi)[:, :], wait=(tpe,), inc="AC")
            bk.r(bi, a)
        VALL = p.now("AC")
        ar.pop()

        ar.push()
        b.wq4 = [ar.alloc([128, D], BF16) for _ in range(4)]
        b.wk4 = [ar.alloc([128, D], BF16) for _ in range(4)]
        b.Qh4 = [ar.alloc([128, TC], BF16) for _ in range(4)]
        b.Kh4 = [ar.alloc([128, TC], BF16) for _ in range(4)]
        b.Ktm4 = [ar.alloc([128, TC], BF16) for _ in range(4)]
        b.qdh4 = [ar.alloc([128, TC], F32) for _ in range(4)]
        b.a2h2 = [ar.alloc([128, TC], BF16) for _ in range(2)]

        wfree2 = [None, None]
        VBAR = p.now("PE")
        qfree = [VBAR] * 4
        qd_st4 = [None] * 4
        a2_st2 = [None, None]
        sum_st2 = [None, None]
        kvlast = [None, None]
        rotch = {"dma": None, "last": ()}

        def prep(i, res):
            pi = i % 2
            for s in (0, 1):
                h = 2 * i + s
                bi4 = 2 * pi + s
                wlq = p.load(b.wq4[bi4][:, :], wq2[h, :, :], sem=f"W2{bi4}",
                             wait=(qfree[bi4],) if qfree[bi4] else ())
                wlk = p.load(b.wk4[bi4][:, :], wk2[h, :, :], sem=f"W2{bi4}")
                res[("WL", s)] = (wlq, wlk)
            for s in (0, 1):
                h = 2 * i + s
                bi4 = 2 * pi + s
                wlq, wlk = res[("WL", s)]
                for ot in range(2):
                    ww = bk.w(6)
                    for c in range(NCH):
                        p.do("tensor", "matmul", bk.ap(6)[:, :],
                             b.wq4[bi4][:, c * 128:(c + 1) * 128],
                             b.xn2[:, c * TC + ot * 512: c * TC + (ot + 1) * 512],
                             start=(c == 0), stop=(c == NCH - 1),
                             wait=((wlk, XN2) + ww) if c == 0 else (),
                             inc="PE" if c == NCH - 1 else None)
                    tpe = p.now("PE")
                    a = p.do("scalar", "copy",
                             b.Qh4[bi4][:, ot * 512:(ot + 1) * 512],
                             bk.ap(6)[:, :],
                             wait=(tpe,) + ((qfree[bi4],)
                                            if (qfree[bi4] and ot == 0) else ()),
                             inc="AC")
                    bk.r(6, a)
                res[("QEV", s)] = p.now("AC")
            for s in (0, 1):
                h = 2 * i + s
                bi4 = 2 * pi + s
                for ot in range(2):
                    ww = bk.w(6)
                    for c in range(NCH):
                        p.do("tensor", "matmul", bk.ap(6)[:, :],
                             b.wk4[bi4][:, c * 128:(c + 1) * 128],
                             b.xn2[:, c * TC + ot * 512: c * TC + (ot + 1) * 512],
                             start=(c == 0), stop=(c == NCH - 1),
                             wait=ww if c == 0 else (),
                             inc="PE" if c == NCH - 1 else None)
                    tpe = p.now("PE")
                    a = p.do("scalar", "copy",
                             b.Kh4[bi4][:, ot * 512:(ot + 1) * 512],
                             bk.ap(6)[:, :],
                             wait=(tpe,) + ((qfree[bi4],)
                                            if (qfree[bi4] and ot == 0) else ()),
                             inc="AC")
                    bk.r(6, a)
                res[("KEV", s)] = p.now("AC")

        def prepB(i, res):
            pi = i % 2
            if upto < 5:
                return
            bufs = [b.Qh4[2 * pi + 0], b.Kh4[2 * pi + 0],
                    b.Qh4[2 * pi + 1], b.Kh4[2 * pi + 1]]
            ing = []
            for k, buf in enumerate(bufs):
                sK = res[("KEV", k // 2)]
                w0 = (sK, C2L) + rotch["last"] + \
                    (((rotch["dma"],)) if (rotch["dma"] and k == 0) else ())
                ing.append(p.do("gpsimd", "dma_start",
                                out=b.rsw[32 * k:32 * k + 32, :],
                                in_=buf[64:96, :], inc="RSW", inck=16, wait=w0))
                ing.append(p.do("gpsimd", "dma_start",
                                out=b.rtp[32 * k:32 * k + 32, :],
                                in_=buf[0:32, :], inc="RSW", inck=16))
            GIN = p.now("RSW")
            u1 = p.do("vector", "tensor_tensor", b.ra[:, :], b.rtp[:, :],
                      b.cosT[:, :], op=ALU.mult,
                      wait=(GIN, C2L) + rotch["last"], inc="DV")
            u2 = p.do("vector", "tensor_tensor", b.rt2[:, :], b.rsw[:, :],
                      b.sinT[:, :], op=ALU.mult, wait=(GIN,), inc="DV")
            y1 = p.do("vector", "tensor_tensor", b.ra[:, :], b.ra[:, :],
                      b.rt2[:, :], op=ALU.add, wait=(u1, u2), inc="DV")
            v1 = p.do("vector", "tensor_tensor", b.rb[:, :], b.rsw[:, :],
                      b.cosT[:, :], op=ALU.mult, wait=(u2,), inc="DV")
            v2 = p.do("vector", "tensor_tensor", b.rt2[:, :], b.rtp[:, :],
                      b.sinT[:, :], op=ALU.mult, wait=(y1,), inc="DV")
            y2 = p.do("vector", "tensor_tensor", b.rb[:, :], b.rb[:, :],
                      b.rt2[:, :], op=ALU.subtract, wait=(v1, v2), inc="DV")
            dlast = None
            for k, buf in enumerate(bufs):
                dlast = p.do("gpsimd", "dma_start", out=buf[0:32, :],
                             in_=b.ra[32 * k:32 * k + 32, :], inc="RSW",
                             inck=16, wait=(y1, v2) if k == 0 else ())
                dlast = p.do("gpsimd", "dma_start", out=buf[64:96, :],
                             in_=b.rb[32 * k:32 * k + 32, :], inc="RSW",
                             inck=16, wait=(y2,) if k == 0 else ())
            rotch["dma"] = dlast
            rotch["last"] = (y2, v2)
            res["ROT"] = p.now("DV")
            res["ROTD"] = p.now("RSW")
            if upto < 6:
                return
            for s in (0, 1):
                h = 2 * i + s
                bi4 = 2 * pi + s
                qv = b.qdt[:, h * 128:(h + 1) * 128]
                qv3 = qv.unsqueeze(1).broadcast_to([128, 8, 128])
                qd1 = p.do("vector", "tensor_tensor",
                           b.qdh4[bi4][:, :].rearrange("p (a b) -> p a b", a=8, b=128),
                           b.Qh4[bi4][:, :].rearrange("p (a b) -> p a b", a=8, b=128),
                           qv3, op=ALU.mult,
                           wait=(res["ROT"], res["ROTD"], C2L)
                           + ((qd_st4[bi4],) if qd_st4[bi4] else ()), inc="DV")
                res[("QD1", s)] = qd1
                qd_st4[bi4] = p.load(qd_out[h, :, :], b.qdh4[bi4][:, :],
                                     eng="gpsimd", sem=f"SQ{bi4}", wait=(qd1,))
                if upto < 7:
                    continue
                for j in range(8):
                    bt = 6 + (j % 2)
                    ps_t = bk.ap(bt).bitcast(BF16)
                    tt = p.do("tensor", "transpose", ps_t[:, 0:128],
                              b.Kh4[bi4][:, j * 128:(j + 1) * 128], b.idb[:, :],
                              wait=(res["ROT"], res["ROTD"]) + bk.w(bt), inc="PE")
                    a = p.do("scalar", "activation",
                             b.Ktm4[bi4][:, j * 128:(j + 1) * 128],
                             ps_t[:, 0:128], AF.Identity, bias=0.0,
                             scale=b.kdc[:, h:h + 1],
                             wait=(tt,) + ((qfree[bi4],)
                                           if (qfree[bi4] and j == 0) else ()),
                             inc="AC")
                    bk.r(bt, a)
                res[("KTM", s)] = p.now("AC")

        def scan(i, res):
            pi = i % 2
            kvtok = [None, None]
            for s in (0, 1):
                kvtok[s] = p.do("gpsimd", "memset", b.kv2[s][:, :], 0.0,
                                wait=(res["ROT"],)
                                + ((kvlast[s],) if kvlast[s] else ()),
                                inc="GP")
            spool = ((0, 1), (3, 4))
            avb = (2, 5)
            smtok = [[None, None], [None, None]]
            smev = [[None, None], [None, None]]

            def emit_ts(j):
                for s in (0, 1):
                    h = 2 * i + s
                    bi4 = 2 * pi + s
                    Sb = spool[s][j % 2]
                    ts = p.do("tensor", "matmul", bk.ap(Sb)[:, :128],
                              b.Kh4[bi4][:, j * 128:(j + 1) * 128],
                              b.Qh4[bi4][:, j * 128:(j + 1) * 128],
                              start=True, stop=True,
                              wait=(res["ROT"], res["ROTD"]) + bk.w(Sb),
                              inc="PE")
                    dm = b.dmk[:, h * 128:(h + 1) * 128]
                    sme = p.do("vector", "tensor_tensor", b.sm4[s][j % 2][:, :],
                               bk.ap(Sb)[:, :128], dm, op=ALU.mult,
                               wait=(ts, C2L)
                               + ((smtok[s][j % 2],) if smtok[s][j % 2] else ()),
                               inc="DV")
                    bk.r(Sb, sme)
                    smev[s][j % 2] = sme

            emit_ts(0)
            for j in range(8):
                if j + 1 < 8:
                    emit_ts(j + 1)
                for s in (0, 1):
                    h = 2 * i + s
                    bi4 = 2 * pi + s
                    Ab = avb[s]
                    sme = smev[s][j % 2]
                    wv_ = bk.w(Ab)
                    t1_ = p.do("tensor", "matmul", bk.ap(Ab)[:, :128],
                               b.Vall[:, j * D + h * 128:j * D + (h + 1) * 128],
                               b.sm4[s][j % 2][:, :], start=True, stop=False,
                               wait=(sme,) + wv_, inc="PE")
                    smtok[s][j % 2] = t1_
                    t2_ = p.do("tensor", "matmul", bk.ap(Ab)[:, :128],
                               b.kv2[s][:, :],
                               b.qdh4[bi4][:, j * 128:(j + 1) * 128],
                               start=False, stop=True,
                               wait=(kvtok[s], res[("QD1", s)]), inc="PE")
                    aev = p.do("scalar", "copy",
                               b.a2h2[s][:, j * 128:(j + 1) * 128],
                               bk.ap(Ab)[:, :128],
                               wait=(t2_,) + ((a2_st2[s],)
                                              if (j == 0 and a2_st2[s]) else ()),
                               inc="AC")
                    bk.r(Ab, aev)
                    Kb = spool[s][j % 2]
                    wu_ = bk.w(Kb)
                    tu = p.do("tensor", "matmul", bk.ap(Kb)[:, :128],
                              b.Ktm4[bi4][:, j * 128:(j + 1) * 128],
                              b.Vall[:, j * D + h * 128:j * D + (h + 1) * 128],
                              start=True, stop=True,
                              wait=(res[("KTM", s)],) + wu_, inc="PE")
                    kvtok[s] = p.do("vector", "scalar_tensor_tensor",
                                    b.kv2[s][:, :], b.kv2[s][:, :],
                                    float(RC[h]), bk.ap(Kb)[:, :128],
                                    op0=ALU.mult, op1=ALU.add,
                                    wait=(tu,)
                                    + ((sum_st2[s],) if (j == 4 and sum_st2[s])
                                       else ()),
                                    inc="DV")
                    bk.r(Kb, kvtok[s])
                    if j == 3 or j == 7:
                        seg = j // 4
                        sum_st2[s] = p.load(sum_out[seg, h, :, :],
                                            b.kv2[s][:, :], eng="gpsimd",
                                            sem=f"SS{s}", wait=(kvtok[s],))
                        if j == 3:
                            kvtok[s] = p.do("gpsimd", "memset", b.kv2[s][:, :],
                                            0.0, wait=(sum_st2[s],), inc="GP")
            for s in (0, 1):
                h = 2 * i + s
                a2_st2[s] = p.load(a2p_out[h, :, :], b.a2h2[s][:, :],
                                   eng="gpsimd", sem=f"SA{s}",
                                   wait=(p.now("AC"),))
                kvlast[s] = sum_st2[s]
            pe_done = p.now("PE")
            qfree[2 * pi] = pe_done
            qfree[2 * pi + 1] = pe_done

        res_all = [dict() for _ in range(NHEAD // 2)]
        prep(0, res_all[0])
        prepB(0, res_all[0])
        for i in range(NHEAD // 2):
            if i + 1 < NHEAD // 2:
                prep(i + 1, res_all[i + 1])
            if upto >= 8:
                scan(i, res_all[i])
            if i + 1 < NHEAD // 2:
                prepB(i + 1, res_all[i + 1])
        ar.pop()
        p.do("gpsimd", "nop", wait=(p.now("ST"), p.now("SX0"), p.now("SX1"),
                                    p.now("SS0"), p.now("SS1"),
                                    p.now("SQ0"), p.now("SQ1"), p.now("SQ2"),
                                    p.now("SQ3"), p.now("SA0"), p.now("SA1")))
        p.emit()
    return nc


def _l2_slopes():
    n = NHEAD
    start = 2.0 ** (-(2.0 ** (-(np.log2(n) - 3))))
    return np.array([start * start ** i for i in range(n)], np.float64)


def _l2_rc():
    r = np.exp(-_l2_slopes())
    return (r ** 128.0).astype(np.float64)


def prep_launch2(I, x2_percore, xn1_percore, k1_percore, v1_percore):
    com = {}
    wq1 = np.empty((NHEAD, 128, D), np.float32)
    for g in range(4):
        for sct in range(4):
            e = 4 * g + sct
            rows = I["L1_attn_Wq"][512 * g + 128 * sct: 512 * g + 128 * (sct + 1)]
            wq1[e] = _img_dmaj(rows.T).reshape(128, NCH, 128).transpose(1, 0, 2).reshape(128, D) if False else _img_dmaj(rows.T)
    com["wq1"] = np.ascontiguousarray(wq1)
    wpT = I["L1_attn_Wp"].T
    wp1 = np.empty((NCH, 128, D), np.float32)
    for ec in range(NCH):
        wp1[ec] = _img_dmaj(wpT[:, ec * 128:(ec + 1) * 128])
    com["wp1"] = np.ascontiguousarray(wp1)
    com["bp1"] = np.ascontiguousarray(I["L1_attn_bp"].reshape(NCH, 128).T)
    wupT = I["L1_ffn_W_up"].T
    wgtT = I["L1_ffn_W_gate"].T
    com["wup"] = np.ascontiguousarray(
        wupT.reshape(NCH, 128, HC1, 128).transpose(2, 1, 0, 3)).reshape(HC1, 128, D)
    com["wgt"] = np.ascontiguousarray(
        wgtT.reshape(NCH, 128, HC1, 128).transpose(2, 1, 0, 3)).reshape(HC1, 128, D)
    wdnT = I["L1_ffn_W_down"].T
    com["wdn1"] = np.ascontiguousarray(
        wdnT.reshape(HC1, 128, NCH, 128).transpose(2, 1, 0, 3)).reshape(NCH, 128, HB1)
    com["su_ug"] = np.ascontiguousarray(I["L1_ffn_SU_ug"].reshape(NCH, 128).T)
    com["su_dp"] = np.ascontiguousarray(I["L1_ffn_SU_dp"].reshape(HC1, 128).T)
    com["wsc_up"] = np.ascontiguousarray(I["L1_ffn_Wscale_ug"][:HB1].reshape(HC1, 128).T)
    com["wsc_gt"] = np.ascontiguousarray(I["L1_ffn_Wscale_ug"][HB1:].reshape(HC1, 128).T)
    com["wsc_dp"] = np.ascontiguousarray(I["L1_ffn_Wscale_dp"].reshape(NCH, 128).T)
    H = np.array([[1.0]])
    while H.shape[0] < 128:
        H = np.block([[H, H], [H, -H]])
    com["h128"] = H.astype(BF)
    for nm, w in (("wq2", I["L2_attn_Wq"]), ("wk2", I["L2_attn_Wk"]),
                  ("wv2", I["L2_attn_Wv"])):
        img = np.empty((NHEAD, 128, D), np.float32)
        for hh in range(NHEAD):
            img[hh] = _img_dmaj(w[:, hh * 128:(hh + 1) * 128])
        if nm == "wv2":
            img = img.reshape(NHEAD, 128, NCH, 128).transpose(2, 1, 0, 3)
            img = img.reshape(NCH, 128, NHEAD * 128)
        com[nm] = np.ascontiguousarray(img)
    slopes = _l2_slopes()
    r = np.exp(-slopes)
    i = np.arange(128, dtype=np.float64)
    qdec = (r[:, None] ** (i[None, :] + 1.0)).astype(np.float32)
    kdec = (r[:, None] ** (127.0 - i[None, :])).astype(np.float32)
    diff = i[:, None] - i[None, :]
    mask = diff >= 0
    dmk = np.where(mask[None], r[:, None, None] ** np.where(mask, diff, 0.0)[None], 0.0)
    dmkT = dmk.transpose(0, 2, 1).astype(np.float32)
    com["qdec_t"] = np.ascontiguousarray(
        np.broadcast_to(qdec[:, None, :], (NHEAD, 128, 128)).transpose(1, 0, 2).reshape(128, NHEAD * 128).copy())
    com["kdec_c"] = np.ascontiguousarray(kdec.T)
    com["dmask"] = np.ascontiguousarray(
        dmkT.transpose(1, 0, 2)).reshape(128, NHEAD * 128)
    in_maps = []
    k1all = to_global([k.transpose(1, 0) if False else k for k in k1_percore]) if False else None
    k1glob = np.empty((4, 128, T), BF)
    v1glob = np.empty((T, 512), BF)
    for c in range(NC):
        a, bb = zz_chunks(c)
        k1glob[:, :, a * 512:(a + 1) * 512] = k1_percore[c][:, :, 0:512]
        k1glob[:, :, bb * 512:(bb + 1) * 512] = k1_percore[c][:, :, 512:1024]
        v1glob[a * 512:(a + 1) * 512] = v1_percore[c][0:512]
        v1glob[bb * 512:(bb + 1) * 512] = v1_percore[c][512:1024]
    v1img = np.ascontiguousarray(
        v1glob.reshape(64, 128, 4, 128).transpose(2, 1, 0, 3)).reshape(4, 128, T)
    com["k1all"] = k1glob
    com["v1img"] = v1img
    kb = np.full((128,), KILL, np.float32)
    tri = np.zeros((4, 128, 512), np.float32)
    for j in range(4):
        kk = np.arange(128)[:, None] + j * 128
        qq = np.arange(512)[None, :]
        tri[j] = np.where(kk <= qq, 0.0, KILL / (1.0 / np.sqrt(128.0)))
    tri2 = np.ascontiguousarray(tri.transpose(1, 0, 2)).reshape(128, 4 * 512)
    for c in range(NC):
        a, bb = zz_chunks(c)
        kb = np.full((128,), KILL, np.float32)
        for qt, g in enumerate((a, bb)):
            nfull = 4 * g
            for kc in range(NV_L0[qt]):
                kb[qt * 64 + kc] = SHIFT if kc < nfull else KILL
        im = dict(com)
        im["kbias"] = np.ascontiguousarray(np.broadcast_to(kb[None, :], (128, 128)).copy())
        im["tri"] = tri2
        im["xin"] = x2_percore[c]
        im["xn1"] = xn1_percore[c]
        im["k1own"] = k1_percore[c]
        v1o = v1_percore[c]
        im["v1ownimg"] = np.ascontiguousarray(
            v1o.reshape(8, 128, 4, 128).transpose(2, 1, 0, 3)).reshape(4, 128, TC)
        own = np.r_[a * 512:(a + 1) * 512, bb * 512:(bb + 1) * 512].astype(np.float64)
        freq = (1.0 / 1024.0) ** np.linspace(0.0, 1.0, 32)
        th = own[None, :] * freq[:, None]
        im["cosT"] = np.tile(np.cos(th).astype(np.float32), (4, 1))
        im["sinT"] = np.tile(np.sin(th).astype(np.float32), (4, 1))
        in_maps.append(im)
    return in_maps

HB2 = 8192
HC2 = 64
I3 = 5632
HSH = I3 * 2 // NC // 1
ESH = D // NC


def build_launch3():
    nc = bass.Bass("TRN2", target_bir_lowering=False, debug=False)
    stack = ExitStack()
    nc._stk = stack

    xin = nc.dram_tensor("xin", [NCH, 128, TC], F32, kind="ExternalInput").ap()
    a2p = nc.dram_tensor("a2p", [NHEAD, 128, TC], BF16, kind="ExternalInput").ap()
    qd = nc.dram_tensor("qd", [NHEAD, 128, TC], BF16, kind="ExternalInput").ap()
    sums = nc.dram_tensor("sums", [128, 16 * NHEAD * 128], BF16, kind="ExternalInput").ap()
    wdec = nc.dram_tensor("wdec", [128, 2 * 16 * NHEAD], F32, kind="ExternalInput").ap()
    rmsw = nc.dram_tensor("rmsw", [128, NCH], F32, kind="ExternalInput").ap()
    wo2 = nc.dram_tensor("wo2", [NCH, 128, D], F32, kind="ExternalInput").ap()
    wgu2 = nc.dram_tensor("wgu2", [2 * HC2, 128, D], F32, kind="ExternalInput").ap()
    wdn2 = nc.dram_tensor("wdn2", [NCH, 128, HB2], F32, kind="ExternalInput").ap()
    gup_sh = nc.dram_tensor("gup_sh", [2, 128, HSH], I32, kind="ExternalInput").ap()
    gup_scl = nc.dram_tensor("gup_scl", [2, 128, HSH], F16, kind="ExternalInput").ap()
    dn_sh = nc.dram_tensor("dn_sh", [6, 128, ESH], I32, kind="ExternalInput").ap()
    dn_scl = nc.dram_tensor("dn_scl", [6, 128, ESH], F16, kind="ExternalInput").ap()

    x_out = nc.dram_tensor("x_out", [NCH, 128, TC], F32, kind="ExternalOutput").ap()
    xn4_out = nc.dram_tensor("xn4_out", [NCH, 128, TC], BF16, kind="ExternalOutput").ap()
    wdq_gu = nc.dram_tensor("wdq_gu", [D, HSH], BF16, kind="ExternalOutput").ap()
    wdq_dn = nc.dram_tensor("wdq_dn", [I3, ESH], BF16, kind="ExternalOutput").ap()

    RC = _l2_rc()

    with stack:
        p = Prog(nc, stack)
        ar = Arena(nc, stack, "arena", 206 * 1024)
        bk = Banks(nc, stack)
        b = B()

        b.ones_col_bf = ar.alloc([128, 1], BF16)
        b.epscol = ar.alloc([128, 1], F32)
        b.ones_row_f = ar.alloc([1, 128], F32)
        b.ones_row_bf = ar.alloc([1, 128], BF16)
        b.sq = ar.alloc([128, 512], BF16)
        b.t1 = ar.alloc([128, 512], F32)
        b.xf = [ar.alloc([128, 512], F32) for _ in range(2)]
        ln = LNH(p, b, ar)
        p.do("gpsimd", "memset", b.ones_col_bf[:, :], 1.0, inc="GP")
        p.do("gpsimd", "memset", b.ones_row_f[:, :], 1.0, inc="GP")
        p.do("gpsimd", "memset", b.ones_row_bf[:, :], 1.0, inc="GP")
        p.do("gpsimd", "memset", b.epscol[:, :], 1e-5, inc="GP")
        CGP = p.now("GP")

        def ln1_into(dstbuf, ot, first_wait, xg="XH", src_dram=None,
                     inplace=False):
            sd = src_dram if src_dram is not None else x_out
            xls = []
            for c in range(NCH):
                xl = p.load(dstbuf[:, c * 512:(c + 1) * 512],
                            sd[c, :, ot * 512:(ot + 1) * 512],
                            sem=f"{xg}{c % 2}",
                            wait=(xls[c - 2],) if c >= 2 else first_wait)
                xls.append(xl)
            srcs = [(dstbuf[:, c * 512:(c + 1) * 512], xls[c])
                    for c in range(NCH)]
            ln.stats(p, bk, srcs, 6, 7)
            a1, a2 = ln.bcast(p, bk, 4, 5)
            xtoks = []
            t1prev = None
            for c in range(NCH):
                if inplace:
                    d1 = p.do("vector", "tensor_tensor", b.t1[:, :],
                              dstbuf[:, c * 512:(c + 1) * 512],
                              ln.rs_t[:, :], op=ALU.mult,
                              wait=(a1, a2) + ((t1prev,) if t1prev else ()),
                              inc="DV")
                else:
                    xl = p.load(b.xf[c % 2][:, :],
                                sd[c, :, ot * 512:(ot + 1) * 512],
                                sem=f"{xg}{c % 2}",
                                wait=(xtoks[c - 2],) if c >= 2 else (a1, a2))
                    d1 = p.do("vector", "tensor_tensor", b.t1[:, :],
                              b.xf[c % 2][:, :],
                              ln.rs_t[:, :], op=ALU.mult,
                              wait=(xl, a1, a2) + ((t1prev,) if t1prev else ()),
                              inc="DV")
                d2 = p.do("vector", "tensor_tensor",
                          dstbuf[:, c * 512:(c + 1) * 512], b.t1[:, :],
                          ln.c_t[:, :], op=ALU.add, wait=(d1,), inc="DV")
                t1prev = d2
                xtoks.append(d2)
            return xtoks

        def ln2_residual_store(ybuf, ot, ytoks, src_dram, xg="XI"):
            SXB = (p.now("SX0"), p.now("SX1"))
            srcs = [(ybuf[:, c * 512:(c + 1) * 512], ytoks[c]) for c in range(NCH)]
            ln.stats(p, bk, srcs, 6, 7)
            a1, a2 = ln.bcast(p, bk, 4, 5)
            t1prev = None
            sttoks = []
            for c in range(NCH):
                xl = p.load(b.xf[c % 2][:, :],
                            src_dram[c, :, ot * 512:(ot + 1) * 512],
                            sem=f"{xg}{c % 2}",
                            wait=(sttoks[c - 2],) if c >= 2 else (a1, a2) + SXB)
                d1 = p.do("vector", "tensor_tensor", b.t1[:, :],
                          ybuf[:, c * 512:(c + 1) * 512], ln.rs_t[:, :],
                          op=ALU.mult,
                          wait=(a1, a2) + ((t1prev,) if t1prev else ()), inc="DV")
                d2 = p.do("vector", "tensor_tensor", b.t1[:, :], b.t1[:, :],
                          ln.c_t[:, :], op=ALU.add, wait=(d1,), inc="DV")
                xu = p.do("vector", "tensor_tensor", b.xf[c % 2][:, :],
                          b.xf[c % 2][:, :], b.t1[:, :], op=ALU.add,
                          wait=(d2, xl), inc="DV")
                t1prev = xu
                stt = p.load(x_out[c, :, ot * 512:(ot + 1) * 512],
                             b.xf[c % 2][:, :], eng="gpsimd", sem=f"SX{c % 2}",
                             wait=(xu,))
                sttoks.append(stt)
            return sttoks

        ar.push()
        b.sums = ar.alloc([128, 16 * NHEAD * 128], BF16)
        b.attn = ar.alloc([128, NHEAD * TC], BF16)
        b.kvin = ar.alloc([128, NHEAD * 128], F32)
        b.kvtmp = ar.alloc([128, NHEAD * 128], F32)
        b.kvbf = ar.alloc([128, NHEAD * 128], BF16)
        b.wdec = ar.alloc([128, 2 * 16 * NHEAD], F32)
        b.rmsw = ar.alloc([128, NCH], F32)
        ar.push()
        b.qdb = ar.alloc([128, NHEAD * TC], BF16)
        p.load(b.sums[:, :], sums[:, :], sem="LC")
        p.load(b.wdec[:, :], wdec[:, :], sem="LC")
        p.load(b.rmsw[:, :], rmsw[:, :], sem="LC")
        for h in range(NHEAD):
            p.load(b.attn[:, h * TC:(h + 1) * TC], a2p[h, :, :], sem="LC")
            p.load(b.qdb[:, h * TC:(h + 1) * TC], qd[h, :, :], sem="LC")
        SLD = p.now("LC")

        attn_toks = []
        kvc_prev = None
        for si in range(2):
            mz = p.do("gpsimd", "memset", b.kvin[:, :], 0.0,
                      wait=(kvc_prev,) if kvc_prev else (CGP,), inc="GP")
            tok = mz
            for gp in range(16):
                colb = (si * 16 + gp) * NHEAD
                wbc = b.wdec[:, colb:colb + NHEAD].unsqueeze(2) \
                    .broadcast_to([128, NHEAD, 128])
                mt = p.do("vector", "tensor_tensor",
                          b.kvtmp[:, :].rearrange("p (a b) -> p a b",
                                                  a=NHEAD, b=128),
                          b.sums[:, gp * NHEAD * 128:(gp + 1) * NHEAD * 128]
                          .rearrange("p (a b) -> p a b", a=NHEAD, b=128),
                          wbc, op=ALU.mult, wait=(SLD, tok), inc="DV")
                tok = p.do("vector", "tensor_tensor", b.kvin[:, :],
                           b.kvin[:, :], b.kvtmp[:, :], op=ALU.add,
                           wait=(mt,), inc="DV")
            pe_b = p.now("PE")
            kvc = p.do("vector", "tensor_copy", b.kvbf[:, :], b.kvin[:, :],
                       wait=(tok, pe_b), inc="DV")
            kvc_prev = kvc
            for h in range(NHEAD):
                for j in range(4):
                    bi = (h * 4 + j) % 2
                    tm = p.do("tensor", "matmul", bk.ap(bi)[:, :128],
                              b.kvbf[:, h * 128:(h + 1) * 128],
                              b.qdb[:, h * TC + si * 512 + j * 128:
                                    h * TC + si * 512 + (j + 1) * 128],
                              start=True, stop=True, wait=(kvc,) + bk.w(bi),
                              inc="PE")
                    at = p.do("vector", "scalar_tensor_tensor",
                              b.attn[:, h * TC + si * 512 + j * 128:
                                     h * TC + si * 512 + (j + 1) * 128],
                              bk.ap(bi)[:, :128], float(RC[h] ** j),
                              b.attn[:, h * TC + si * 512 + j * 128:
                                     h * TC + si * 512 + (j + 1) * 128],
                              op0=ALU.mult, op1=ALU.add, wait=(tm,), inc="DV")
                    bk.r(bi, at)
                    attn_toks.append(at)
        ATT = p.now("DV")

        for ot in range(2):
            w6 = bk.w(6)
            sqprev = None
            for c in range(NCH):
                at = p.do("scalar", "activation", b.sq[:, :],
                          b.attn[:, c * TC + ot * 512: c * TC + (ot + 1) * 512],
                          AF.Square, wait=(ATT,) + ((sqprev,) if sqprev else ()),
                          inc="AC")
                sqprev = p.do("tensor", "matmul", bk.ap(6)[0:1, :],
                              b.ones_col_bf[:, :], b.sq[:, :],
                              wait=(at,) + (w6 if c == 0 else ()),
                              start=(c == 0), stop=(c == NCH - 1), inc="PE")
            d2 = p.do("vector", "tensor_scalar", ln.vrs[:, :], bk.ap(6)[0:1, :],
                      1.0 / D, None, op0=ALU.mult, wait=(sqprev,), inc="DV")
            bk.r(6, d2)
            sa = p.do("scalar", "activation", ln.vsd[:, :], ln.vrs[:, :], AF.Sqrt,
                      bias=b.epscol[0:1, :], scale=1.0, wait=(d2,), inc="AC")
            d5 = p.do("vector", "reciprocal", ln.vrs[:, :], ln.vsd[:, :],
                      wait=(sa,), inc="DV")
            t1 = p.do("tensor", "matmul", bk.ap(4)[:, :], b.ones_row_f[:, :],
                      ln.vrs[:, :], start=True, stop=True, wait=(d5,) + bk.w(4),
                      inc="PE")
            a1 = p.do("scalar", "copy", ln.rs_t[:, :], bk.ap(4)[:, :], wait=(t1,),
                      inc="AC")
            bk.r(4, a1)
            prev = a1
            for c in range(NCH):
                sl = b.attn[:, c * TC + ot * 512: c * TC + (ot + 1) * 512]
                d1 = p.do("vector", "tensor_tensor", sl, sl, ln.rs_t[:, :],
                          op=ALU.mult, wait=(prev,), inc="DV")
                d2 = p.do("vector", "tensor_scalar", sl, sl,
                          b.rmsw[:, c:c + 1], None, op0=ALU.mult, wait=(d1,),
                          inc="DV")
                prev = d2
        ATN2 = p.now("DV")

        ar.pop()
        QPE = p.now("PE")
        ar.push()
        b.wpb = [ar.alloc([128, D], BF16) for _ in range(2)]
        b.yb = ar.alloc([128, NCH * TC], BF16)
        wfree = [None, None]
        for ec in range(NCH):
            pb = ec % 2
            wl = p.load(b.wpb[pb][:, :], wo2[ec, :, :], sem=f"WB{pb}",
                        wait=(wfree[pb],) if wfree[pb] else (QPE,))
            for ot in range(2):
                ww = bk.w(pb)
                for hc in range(NCH):
                    p.do("tensor", "matmul", bk.ap(pb)[:, :],
                         b.wpb[pb][:, hc * 128:(hc + 1) * 128],
                         b.attn[:, hc * TC + ot * 512: hc * TC + (ot + 1) * 512],
                         start=(hc == 0), stop=(hc == NCH - 1),
                         wait=((wl, ATN2) + ww) if hc == 0 else (),
                         inc="PE" if hc == NCH - 1 else None)
                tpe = p.now("PE")
                a = p.do("scalar", "copy",
                         b.yb[:, ec * TC + ot * 512: ec * TC + (ot + 1) * 512],
                         bk.ap(pb)[:, :], wait=(tpe,), inc="AC")
                bk.r(pb, a)
                wfree[pb] = a
        YT = p.now("AC")
        for ot in range(2):
            SXB = (p.now("SX0"), p.now("SX1"))
            ytoks = [YT if c == 0 else None for c in range(NCH)]
            srcs = [(b.yb[:, c * TC + ot * 512: c * TC + (ot + 1) * 512],
                     ytoks[c]) for c in range(NCH)]
            ln.stats(p, bk, srcs, 6, 7)
            a1, a2 = ln.bcast(p, bk, 4, 5)
            t1prev = None
            sttoks = []
            for c in range(NCH):
                xl = p.load(b.xf[c % 2][:, :], xin[c, :, ot * 512:(ot + 1) * 512],
                            sem=f"XI{c % 2}",
                            wait=(sttoks[c - 2],) if c >= 2 else (a1, a2) + SXB)
                d1 = p.do("vector", "tensor_tensor", b.t1[:, :],
                          b.yb[:, c * TC + ot * 512: c * TC + (ot + 1) * 512],
                          ln.rs_t[:, :], op=ALU.mult,
                          wait=(a1, a2) + ((t1prev,) if t1prev else ()), inc="DV")
                d2 = p.do("vector", "tensor_tensor", b.t1[:, :], b.t1[:, :],
                          ln.c_t[:, :], op=ALU.add, wait=(d1,), inc="DV")
                xu = p.do("vector", "tensor_tensor", b.xf[c % 2][:, :],
                          b.xf[c % 2][:, :], b.t1[:, :], op=ALU.add,
                          wait=(d2, xl), inc="DV")
                t1prev = xu
                stt = p.load(x_out[c, :, ot * 512:(ot + 1) * 512],
                             b.xf[c % 2][:, :], eng="gpsimd", sem=f"SX{c % 2}",
                             wait=(xu,))
                sttoks.append(stt)
        ar.pop()
        ar.pop()

        ar.push()
        b.xn = ar.alloc([128, NCH * 512], BF16)
        b.h = ar.alloc([128, HC2 * 512], BF16)
        b.yb = b.xn
        b.gsil = ar.alloc([128, 512], BF16)
        b.usc = ar.alloc([128, 512], BF16)
        b.wG = [ar.alloc([128, D], BF16) for _ in range(2)]
        b.wU = [ar.alloc([128, D], BF16) for _ in range(2)]
        b.wD = [ar.alloc([128, HB2], BF16) for _ in range(2)]
        b.pk = ar.alloc([128, HSH], I32)
        b.scl = ar.alloc([128, HSH], F32)
        b.nib = ar.alloc([128, HSH], I32)
        b.ib = ar.alloc([128, HSH], I32)
        b.pf = ar.alloc([128, HSH], F32)
        b.r1 = ar.alloc([128, HSH], F32)
        b.r2 = ar.alloc([128, HSH], F32)
        b.v2 = ar.alloc([128, HSH], F32)
        b.sf = ar.alloc([128, HSH], F32)
        b.ob = [ar.alloc([128, HSH], BF16) for _ in range(2)]
        b.n4col = ar.alloc([128, 1], F32)
        b.n6col = ar.alloc([128, 1], F32)
        wfree = {"G": [None, None], "U": [None, None], "D": [None, None]}
        PH2 = (p.now("PE"), p.now("DV"), p.now("AC"),
               p.now("SX0"), p.now("SX1"), p.now("XI0"), p.now("XI1"),
               p.now("WB0"), p.now("WB1"))
        DQB = PH2 + (p.now("GP"), p.now("ST"))
        m1 = p.do("gpsimd", "memset", b.n4col[:, :], -4.0, wait=DQB, inc="GP")
        m2 = p.do("gpsimd", "memset", b.n6col[:, :], -6.0, inc="GP")
        MGP = p.now("GP")

        def dq_gen():
            obst = [None, None]
            prevs = None
            groups = [(gup_sh, gup_scl, wdq_gu, HSH, [(0, 128)]),
                      (gup_sh, gup_scl, wdq_gu, HSH, [(1, 128)]),
                      (dn_sh, dn_scl, wdq_dn, ESH, [(0, 128), (1, 128)]),
                      (dn_sh, dn_scl, wdq_dn, ESH, [(2, 128), (3, 128)]),
                      (dn_sh, dn_scl, wdq_dn, ESH, [(4, 128), (5, 64)])]
            for src_pk, src_scl, dst, Wc, tiles in groups:
                Wt = Wc * len(tiles)
                for i, (wt, nw) in enumerate(tiles):
                    p.load(b.pk[:, i * Wc:(i + 1) * Wc], src_pk[wt, :, 0:Wc],
                           eng="sync", sem="PK",
                           wait=((prevs if prevs else DQB) if i == 0 else ()))
                    p.load(b.scl[:, i * Wc:(i + 1) * Wc], src_scl[wt, :, 0:Wc],
                           sem="PK2",
                           wait=((prevs if prevs else DQB) if i == 0 else ()))
                PKL = (p.now("PK"), p.now("PK2"))
                for j in range(8):
                    dv = p.do("vector", "tensor_scalar", b.nib[:, 0:Wt],
                              b.pk[:, 0:Wt], 4 * j, 15,
                              op0=ALU.logical_shift_right,
                              op1=ALU.bitwise_and, wait=PKL + (MGP,), inc="DV")
                    i1 = p.do("vector", "tensor_scalar", b.ib[:, 0:Wt],
                              b.nib[:, 0:Wt], 7, None,
                              op0=ALU.bitwise_and, wait=(dv,), inc="DV")
                    pf = p.do("vector", "tensor_copy", b.pf[:, 0:Wt],
                              b.ib[:, 0:Wt], wait=(i1,), inc="DV")
                    rr1 = p.do("scalar", "activation", b.r1[:, 0:Wt],
                               b.pf[:, 0:Wt], AF.Relu, bias=b.n4col[:, :],
                               scale=1.0, wait=(pf,), inc="AC")
                    rr2 = p.do("scalar", "activation", b.r2[:, 0:Wt],
                               b.pf[:, 0:Wt], AF.Relu, bias=b.n6col[:, :],
                               scale=1.0, wait=(pf,), inc="AC")
                    sg = p.do("vector", "tensor_scalar", b.ib[:, 0:Wt],
                              b.nib[:, 0:Wt], 8, 28, op0=ALU.bitwise_and,
                              op1=ALU.logical_shift_left, wait=(pf,), inc="DV")
                    ss = p.do("vector", "tensor_tensor",
                              b.sf[:, 0:Wt].bitcast(I32),
                              b.scl[:, 0:Wt].bitcast(I32), b.ib[:, 0:Wt],
                              op=ALU.bitwise_xor, wait=(sg,), inc="DV")
                    v1 = p.do("vector", "scalar_tensor_tensor",
                              b.r1[:, 0:Wt], b.r1[:, 0:Wt], 0.5,
                              b.r2[:, 0:Wt], op0=ALU.mult, op1=ALU.add,
                              wait=(rr1, rr2), inc="DV")
                    v2 = p.do("vector", "scalar_tensor_tensor",
                              b.v2[:, 0:Wt], b.pf[:, 0:Wt], 0.5,
                              b.r1[:, 0:Wt], op0=ALU.mult, op1=ALU.add,
                              wait=(v1,), inc="DV")
                    jp = j % 2
                    vs = p.do("vector", "tensor_tensor", b.ob[jp][:, 0:Wt],
                              b.v2[:, 0:Wt], b.sf[:, 0:Wt], op=ALU.mult,
                              wait=(v2, ss) + ((obst[jp],) if obst[jp] else ()),
                              inc="DV")
                    st = None
                    for i, (wt, nw) in enumerate(tiles):
                        seg = dst[wt * 1024: wt * 1024 + nw * 8, :].rearrange(
                            "(w j) n -> w j n", j=8)
                        st = p.do("sync", "dma_start",
                                  out=seg[:, j, 0:Wc],
                                  in_=b.ob[jp][0:nw, i * Wc:(i + 1) * Wc],
                                  inc=f"PD{jp}", inck=16,
                                  wait=(vs,) if i == 0 else ())
                    obst[jp] = st
                    yield
                prevs = (p.now("PD0"), p.now("PD1"))

        dq = dq_gen()

        def dq_step():
            try:
                next(dq)
            except StopIteration:
                pass

        for ot in range(2):
            OTB = PH2 if ot == 0 else (p.now("PE"), p.now("DV"), p.now("AC"),
                                       p.now("SX0"), p.now("SX1"))
            xtoks = ln1_into(b.xn, ot, OTB, xg="XH")
            prev_hd = None
            for hc in range(HC2):
                pb = hc % 2
                wgl = p.load(b.wG[pb][:, :], wgu2[hc, :, :], sem=f"WG{pb}",
                             wait=(wfree["G"][pb],) if wfree["G"][pb] else ())
                wul = p.load(b.wU[pb][:, :], wgu2[HC2 + hc, :, :], sem=f"WU{pb}",
                             wait=(wfree["U"][pb],) if wfree["U"][pb] else ())
                w0 = bk.w(0)
                for c in range(NCH):
                    p.do("tensor", "matmul", bk.ap(0)[:, :],
                         b.wG[pb][:, c * 128:(c + 1) * 128],
                         b.xn[:, c * 512:(c + 1) * 512], start=(c == 0),
                         stop=(c == NCH - 1),
                         wait=((wgl, xtoks[-1]) + w0) if c == 0 else (),
                         inc="PE" if c == NCH - 1 else None)
                g_pe = p.now("PE")
                w1 = bk.w(1)
                for c in range(NCH):
                    p.do("tensor", "matmul", bk.ap(1)[:, :],
                         b.wU[pb][:, c * 128:(c + 1) * 128],
                         b.xn[:, c * 512:(c + 1) * 512], start=(c == 0),
                         stop=(c == NCH - 1),
                         wait=((wul,) + w1) if c == 0 else (),
                         inc="PE" if c == NCH - 1 else None)
                u_pe = p.now("PE")
                ga = p.do("scalar", "activation", b.gsil[:, :], bk.ap(0)[:, :],
                          AF.Silu, bias=0.0, scale=1.0,
                          wait=(g_pe,) + ((prev_hd,) if prev_hd else ()), inc="AC")
                bk.r(0, ga)
                wfree["G"][pb] = ga
                ua = p.do("scalar", "copy", b.usc[:, :], bk.ap(1)[:, :],
                          wait=(u_pe,), inc="AC")
                bk.r(1, ua)
                wfree["U"][pb] = ua
                prev_hd = p.do("vector", "tensor_tensor",
                               b.h[:, hc * 512:(hc + 1) * 512], b.gsil[:, :],
                               b.usc[:, :], op=ALU.mult, wait=(ga, ua), inc="DV")
                if ot == 1 and hc % 2 == 0:
                    dq_step()
            HDONE = p.now("DV")
            ytoks = []
            for ec in range(NCH):
                pb = ec % 2
                wdl = p.load(b.wD[pb][:, :], wdn2[ec, :, :], sem=f"WD{pb}",
                             wait=(wfree["D"][pb],) if wfree["D"][pb] else ())
                w2 = bk.w(2)
                for hc in range(HC2):
                    p.do("tensor", "matmul", bk.ap(2)[:, :],
                         b.wD[pb][:, hc * 128:(hc + 1) * 128],
                         b.h[:, hc * 512:(hc + 1) * 512], start=(hc == 0),
                         stop=(hc == HC2 - 1),
                         wait=((wdl, HDONE) + w2) if hc == 0 else (),
                         inc="PE" if hc == HC2 - 1 else None)
                d_pe = p.now("PE")
                ya = p.do("scalar", "copy", b.yb[:, ec * 512:(ec + 1) * 512],
                          bk.ap(2)[:, :], wait=(d_pe,), inc="AC")
                bk.r(2, ya)
                wfree["D"][pb] = ya
                ytoks.append(ya)
                dq_step()
            ln2_residual_store(b.yb, ot, ytoks, x_out, xg="XI")
        for _ in dq:
            pass
        ar.pop()

        ar.push()
        b.xn4 = ar.alloc([128, NCH * 512], BF16)
        for ot in range(2):
            OTB = (p.now("PE"), p.now("DV"), p.now("AC"), p.now("ST"),
                   p.now("SX0"), p.now("SX1"), p.now("PD0"), p.now("PD1"))
            xtoks = ln1_into(b.xn4, ot, OTB, xg="XH", inplace=True)
            for c in range(NCH):
                p.load(xn4_out[c, :, ot * 512:(ot + 1) * 512],
                       b.xn4[:, c * 512:(c + 1) * 512], eng="gpsimd", sem="ST",
                       wait=(xtoks[c],))
        ar.pop()

        p.do("gpsimd", "nop", wait=(p.now("ST"), p.now("SX0"), p.now("SX1"),
                                    p.now("PD0"), p.now("PD1")))
        p.emit()
    return nc


def prep_launch3(I, x3_pc, a2p_pc, qd_pc, sums_pc):
    com = {}
    wo = I["L2_attn_Wo"]
    img = np.empty((NCH, 128, D), np.float32)
    for ec in range(NCH):
        img[ec] = _img_dmaj(wo[:, ec * 128:(ec + 1) * 128])
    com["wo2"] = np.ascontiguousarray(img)
    com["rmsw"] = np.ascontiguousarray(I["L2_attn_rms_w"].reshape(NCH, 128).T)
    guT = I["L2_ffn_w_gate_up"].T
    com["wgu2"] = np.ascontiguousarray(
        guT.reshape(NCH, 128, 2 * HC2, 128).transpose(2, 1, 0, 3)).reshape(2 * HC2, 128, D)
    dnT = I["L2_ffn_w_down"].T
    com["wdn2"] = np.ascontiguousarray(
        dnT.reshape(HC2, 128, NCH, 128).transpose(2, 1, 0, 3)).reshape(NCH, 128, HB2)
    sums_glob = np.empty((16, NHEAD, 128, 128), BF)
    for c in range(NC):
        a, bb = zz_chunks(c)
        sums_glob[a] = sums_pc[c][0]
        sums_glob[bb] = sums_pc[c][1]
    com["sums"] = np.ascontiguousarray(
        sums_glob.transpose(2, 0, 1, 3)).reshape(128, 16 * NHEAD * 128)
    rc = _l2_rc()
    gupT = I["L3_ffn_gate_up_packed"].T
    dnpT = I["L3_ffn_down_packed"].T
    gsc = I["L3_ffn_gate_up_scales"]
    dsc = I["L3_ffn_down_scales"]
    in_maps = []
    for c in range(NC):
        a, bb = zz_chunks(c)
        im = dict(com)
        im["xin"] = x3_pc[c]
        im["a2p"] = a2p_pc[c]
        im["qd"] = qd_pc[c]
        wd = np.zeros((2, 16, NHEAD), np.float32)
        for si, g in enumerate((a, bb)):
            for gp in range(16):
                if gp < g:
                    wd[si, gp] = (rc ** (4.0 * (g - 1 - gp))).astype(np.float32)
        im["wdec"] = np.ascontiguousarray(
            np.broadcast_to(wd.reshape(1, -1), (128, 2 * 16 * NHEAD)).copy())
        gs = gupT[:, c * HSH:(c + 1) * HSH]
        im["gup_sh"] = np.ascontiguousarray(gs.reshape(2, 128, HSH))
        scl = gsc[:, c * HSH:(c + 1) * HSH]
        scl_rows = np.repeat(scl, 16, axis=0)
        im["gup_scl"] = np.ascontiguousarray(scl_rows.reshape(2, 128, HSH))
        ds = dnpT[:, c * ESH:(c + 1) * ESH]
        dsp = np.zeros((768, ESH), np.int32)
        dsp[:704] = ds
        im["dn_sh"] = np.ascontiguousarray(dsp.reshape(6, 128, ESH))
        dscl = dsc[:, c * ESH:(c + 1) * ESH]
        dscl_rows = np.zeros((768, ESH), np.float16)
        dscl_rows[:704] = np.repeat(dscl, 16, axis=0)
        im["dn_scl"] = np.ascontiguousarray(dscl_rows.reshape(6, 128, ESH))
        in_maps.append(im)
    return in_maps

HC3 = 44


def build_launch4():
    nc = bass.Bass("TRN2", target_bir_lowering=False, debug=False)
    stack = ExitStack()
    nc._stk = stack

    xin = nc.dram_tensor("xin", [NCH, 128, TC], F32, kind="ExternalInput").ap()
    xn4 = nc.dram_tensor("xn4", [NCH, 128, TC], BF16, kind="ExternalInput").ap()
    Xtm = nc.dram_tensor("Xtm", [2, 128, D], BF16, kind="ExternalInput").ap()
    Xfm = nc.dram_tensor("Xfm", [128, NCH * 256], BF16, kind="ExternalInput").ap()
    blT = nc.dram_tensor("blT", [2, 128, 256], BF16, kind="ExternalInput").ap()
    clT = nc.dram_tensor("clT", [2, 128, 256], BF16, kind="ExternalInput").ap()
    alT = nc.dram_tensor("alT", [2, 128, 256], BF16, kind="ExternalInput").ap()
    rl = nc.dram_tensor("rl", [128, 1], F32, kind="ExternalInput").ap()
    blendc = nc.dram_tensor("blendc", [128, TC], F32, kind="ExternalInput").ap()
    omblendc = nc.dram_tensor("omblendc", [128, TC], F32, kind="ExternalInput").ap()
    wgu3 = nc.dram_tensor("wgu3", [2 * HC3, 128, D], BF16, kind="ExternalInput").ap()
    wdn3 = nc.dram_tensor("wdn3", [NCH, 128, I3], BF16, kind="ExternalInput").ap()

    x_out = nc.dram_tensor("x_out", [NCH, 128, TC], F32, kind="ExternalOutput").ap()

    with stack:
        p = Prog(nc, stack)
        ar = Arena(nc, stack, "arena", 206 * 1024)
        bk = Banks(nc, stack)
        b = B()

        b.ones_col_bf = ar.alloc([128, 1], BF16)
        b.epscol = ar.alloc([128, 1], F32)
        b.ones_row_f = ar.alloc([1, 128], F32)
        b.ones_row_bf = ar.alloc([1, 128], BF16)
        b.sq = ar.alloc([128, 512], BF16)
        b.t1 = ar.alloc([128, 512], F32)
        b.xf = [ar.alloc([128, 512], F32) for _ in range(2)]
        b.idb = ar.alloc([128, 128], BF16)
        ln = LNH(p, b, ar)
        identb_dram = nc.inline_tensor(np.eye(128, dtype=BF), "idb_c").ap()
        zero_dram = nc.inline_tensor(np.zeros((1, 1), dtype=BF), "z_c").ap()
        p.load(b.idb[:, :], identb_dram[:, :], sem="LC")
        p.do("gpsimd", "memset", b.ones_col_bf[:, :], 1.0, inc="GP")
        p.do("gpsimd", "memset", b.ones_row_f[:, :], 1.0, inc="GP")
        p.do("gpsimd", "memset", b.ones_row_bf[:, :], 1.0, inc="GP")
        p.do("gpsimd", "memset", b.epscol[:, :], 1e-5, inc="GP")
        CGP = p.now("GP")
        CLD = p.now("LC")

        def ln1_into(dstbuf, ot, first_wait, xg="XH", src_dram=None):
            sd = src_dram if src_dram is not None else x_out
            xls = []
            for c in range(NCH):
                xl = p.load(dstbuf[:, c * 512:(c + 1) * 512],
                            sd[c, :, ot * 512:(ot + 1) * 512],
                            sem=f"{xg}{c % 2}",
                            wait=(xls[c - 2],) if c >= 2 else first_wait)
                xls.append(xl)
            srcs = [(dstbuf[:, c * 512:(c + 1) * 512], xls[c])
                    for c in range(NCH)]
            ln.stats(p, bk, srcs, 6, 7)
            a1, a2 = ln.bcast(p, bk, 4, 5)
            xtoks = []
            t1prev = None
            for c in range(NCH):
                xl = p.load(b.xf[c % 2][:, :], sd[c, :, ot * 512:(ot + 1) * 512],
                            sem=f"{xg}{c % 2}",
                            wait=(xtoks[c - 2],) if c >= 2 else (a1, a2))
                d1 = p.do("vector", "tensor_tensor", b.t1[:, :], b.xf[c % 2][:, :],
                          ln.rs_t[:, :], op=ALU.mult,
                          wait=(xl, a1, a2) + ((t1prev,) if t1prev else ()), inc="DV")
                d2 = p.do("vector", "tensor_tensor",
                          dstbuf[:, c * 512:(c + 1) * 512], b.t1[:, :],
                          ln.c_t[:, :], op=ALU.add, wait=(d1,), inc="DV")
                t1prev = d2
                xtoks.append(d2)
            return xtoks

        ar.push()
        b.att = ar.alloc([128, NCH * D], BF16)
        b.xn4s = ar.alloc([128, NCH * TC], BF16)
        b.Xout = ar.alloc([128, NCH * TC], BF16)
        b.Xt = ar.alloc([128, 2 * D], BF16)
        b.Xf = ar.alloc([128, NCH * 256], BF16)
        b.rl = ar.alloc([128, 1], F32)
        b.bl = ar.alloc([128, TC], F32)
        b.omb = ar.alloc([128, TC], F32)
        b.tb = ar.alloc([128, 512], BF16)
        b.tb2 = ar.alloc([128, 512], BF16)
        ar.push()
        b.Ub = ar.alloc([128, 2 * D], BF16)
        b.Wb = ar.alloc([128, 2 * D], BF16)
        b.bcl = [ar.alloc([128, 2 * 256], BF16) for _ in range(3)]

        p.load(b.Xt[:, 0:D], Xtm[0, :, :], sem="LA")
        p.load(b.Xt[:, D:2 * D], Xtm[1, :, :], sem="LA")
        p.load(b.Xf[:, :], Xfm[:, :], sem="LA")
        p.load(b.bcl[0][:, :].rearrange("p (c n) -> p c n", c=2), blT.rearrange("c p n -> p c n"), sem="LA")
        p.load(b.bcl[1][:, :].rearrange("p (c n) -> p c n", c=2), clT.rearrange("c p n -> p c n"), sem="LA")
        p.load(b.bcl[2][:, :].rearrange("p (c n) -> p c n", c=2), alT.rearrange("c p n -> p c n"), sem="LA")
        p.load(b.rl[:, :], rl[:, :], sem="LA")
        p.load(b.bl[:, :], blendc[:, :], sem="LA")
        p.load(b.omb[:, :], omblendc[:, :], sem="LA")
        for ch in range(NCH):
            p.load(b.xn4s[:, ch * TC:(ch + 1) * TC], xn4[ch, :, :], sem="XN")
        XNL = p.now("XN")
        ALD = p.now("LA")
        zpt = p.do("gpsimd", "memset", b.Xout[:, :], 0.0, inc="GP")

        for dst, w_ in ((b.Ub, b.bcl[0]), (b.Wb, b.bcl[1])):
            for ic in range(2):
                for mt in range(4):
                    bi = mt % 2
                    ww = bk.w(bi)
                    for jc in range(2):
                        p.do("tensor", "matmul", bk.ap(bi)[:, :],
                             w_[:, jc * 128 + 0:jc * 128 + 128][:, 0:128] if False else w_.rearrange("p (c n) -> p c n", c=2)[:, jc, ic * 128:(ic + 1) * 128],
                             b.Xt[:, jc * D + mt * 512: jc * D + (mt + 1) * 512],
                             start=(jc == 0), stop=(jc == 1),
                             wait=((ALD, CGP) + ww) if jc == 0 else (),
                             inc="PE" if jc == 1 else None)
                    tpe = p.now("PE")
                    a = p.do("scalar", "copy",
                             dst[:, ic * D + mt * 512: ic * D + (mt + 1) * 512],
                             bk.ap(bi)[:, :], wait=(tpe,), inc="AC")
                    bk.r(bi, a)
        UWT = p.now("AC")
        for ncn in range(NCH):
            for mt in range(4):
                bi = mt % 2
                ww = bk.w(bi)
                for ic in range(2):
                    p.do("tensor", "matmul", bk.ap(bi)[:, :],
                         b.Ub[:, ic * D + ncn * 128: ic * D + (ncn + 1) * 128],
                         b.Wb[:, ic * D + mt * 512: ic * D + (mt + 1) * 512],
                         start=(ic == 0), stop=(ic == 1),
                         wait=((UWT,) + ww) if ic == 0 else (),
                         inc="PE" if ic == 1 else None)
                tpe = p.now("PE")
                a = p.do("scalar", "activation",
                         b.att[:, ncn * D + mt * 512: ncn * D + (mt + 1) * 512],
                         bk.ap(bi)[:, :], AF.Relu, bias=0.0, scale=1.0,
                         wait=(tpe,), inc="AC")
                bk.r(bi, a)
        ATT = p.now("AC")
        zfix = p.do("gpsimd", "dma_start", out=b.att[127:128, 15 * D + 2047: 15 * D + 2048],
                    in_=zero_dram[0:1, 0:1], inc="ZF", inck=16, wait=(ATT,))
        b.Xat = b.Ub
        for dc in range(2):
            for mt in range(4):
                bi = mt % 2
                ww = bk.w(bi)
                for ncn in range(NCH):
                    p.do("tensor", "matmul", bk.ap(bi)[:, :],
                         b.Xf[:, ncn * 256 + dc * 128: ncn * 256 + (dc + 1) * 128],
                         b.att[:, ncn * D + mt * 512: ncn * D + (mt + 1) * 512],
                         start=(ncn == 0), stop=(ncn == NCH - 1),
                         wait=((zfix,) + ww) if ncn == 0 else (),
                         inc="PE" if ncn == NCH - 1 else None)
                tpe = p.now("PE")
                a = p.do("scalar", "copy",
                         b.Xat[:, dc * D + mt * 512: dc * D + (mt + 1) * 512],
                         bk.ap(bi)[:, :], wait=(tpe,), inc="AC")
                bk.r(bi, a)
        XAT = p.now("AC")
        b.Xo_rm = b.Wb
        prev_d = None
        for ic in range(2):
            for mt in range(4):
                bi = mt % 2
                ww = bk.w(bi)
                for jc in range(2):
                    p.do("tensor", "matmul", bk.ap(bi)[:, :],
                         b.bcl[2].rearrange("p (c n) -> p c n", c=2)[:, jc, ic * 128:(ic + 1) * 128],
                         b.Xat[:, jc * D + mt * 512: jc * D + (mt + 1) * 512],
                         start=(jc == 0), stop=(jc == 1),
                         wait=((XAT,) + ww) if jc == 0 else (),
                         inc="PE" if jc == 1 else None)
                tpe = p.now("PE")
                a = p.do("scalar", "copy", b.tb[:, :], bk.ap(bi)[:, :],
                         wait=(tpe,) + ((prev_d,) if prev_d else ()), inc="AC")
                bk.r(bi, a)
                d = p.do("vector", "tensor_tensor",
                         b.Xo_rm[:, ic * D + mt * 512: ic * D + (mt + 1) * 512],
                         b.Xt[:, ic * D + mt * 512: ic * D + (mt + 1) * 512],
                         b.tb[:, :], op=ALU.add, wait=(a,), inc="DV")
                prev_d = d
        XORM = p.now("DV")
        ps_t = bk.ap(7).bitcast(BF16)
        for jc in range(2):
            for fc in range(NCH):
                tt = p.do("tensor", "transpose", ps_t[:, 0:128],
                          b.Xo_rm[:, jc * D + fc * 128: jc * D + (fc + 1) * 128],
                          b.idb[:, :], wait=(XORM, CLD) + bk.w(7), inc="PE")
                a = p.do("scalar", "copy",
                         b.Xout[:, fc * TC + jc * 128: fc * TC + (jc + 1) * 128],
                         ps_t[:, 0:128], wait=(tt, zpt), inc="AC")
                bk.r(7, a)
        XOT = p.now("AC")
        ar.pop()
        b.ybl = ar.alloc([128, NCH * 512], BF16)

        for ot in range(2):
            ytoks = []
            for mc in range(NCH):
                bi = mc % 2
                ww = bk.w(bi)
                for ncn in range(NCH):
                    p.do("tensor", "matmul", bk.ap(bi)[:, :],
                         b.att[:, ncn * D + mc * 128: ncn * D + (mc + 1) * 128],
                         b.xn4s[:, ncn * TC + ot * 512: ncn * TC + (ot + 1) * 512],
                         start=(ncn == 0), stop=(ncn == NCH - 1),
                         wait=((XOT, XNL) + ww) if ncn == 0 else (),
                         inc="PE" if ncn == NCH - 1 else None)
                tpe = p.now("PE")
                a = p.do("scalar", "activation", b.tb[:, :], bk.ap(bi)[:, :],
                         AF.Identity, bias=0.0, scale=b.rl[:, :],
                         wait=(tpe,) + ((tbf,) if mc + ot > 0 else ()), inc="AC")
                bk.r(bi, a)
                d1 = p.do("vector", "tensor_tensor", b.tb2[:, :],
                          b.xn4s[:, mc * TC + ot * 512: mc * TC + (ot + 1) * 512],
                          b.tb[:, :], op=ALU.add, wait=(a,), inc="DV")
                d2 = p.do("vector", "tensor_tensor", b.tb2[:, :], b.tb2[:, :],
                          b.omb[:, ot * 512:(ot + 1) * 512], op=ALU.mult,
                          wait=(d1,), inc="DV")
                d3 = p.do("vector", "tensor_tensor", b.tb[:, :],
                          b.Xout[:, mc * TC + ot * 512: mc * TC + (ot + 1) * 512],
                          b.bl[:, ot * 512:(ot + 1) * 512], op=ALU.mult,
                          wait=(d2,), inc="DV")
                tbf = p.do("vector", "tensor_tensor",
                           b.ybl[:, mc * 512:(mc + 1) * 512], b.tb2[:, :],
                           b.tb[:, :], op=ALU.add, wait=(d3,), inc="DV")
                ytoks.append(tbf)
            SXB = (p.now("SX0"), p.now("SX1"))
            srcs = [(b.ybl[:, c * 512:(c + 1) * 512], ytoks[c]) for c in range(NCH)]
            ln.stats(p, bk, srcs, 6, 7)
            a1, a2 = ln.bcast(p, bk, 4, 5)
            t1prev = None
            sttoks = []
            for c in range(NCH):
                xl = p.load(b.xf[c % 2][:, :], xin[c, :, ot * 512:(ot + 1) * 512],
                            sem=f"XI{c % 2}",
                            wait=(sttoks[c - 2],) if c >= 2 else (a1, a2) + SXB)
                d1 = p.do("vector", "tensor_tensor", b.t1[:, :],
                          b.ybl[:, c * 512:(c + 1) * 512], ln.rs_t[:, :],
                          op=ALU.mult, wait=(a1, a2) + ((t1prev,) if t1prev else ()),
                          inc="DV")
                d2 = p.do("vector", "tensor_tensor", b.t1[:, :], b.t1[:, :],
                          ln.c_t[:, :], op=ALU.add, wait=(d1,), inc="DV")
                xu = p.do("vector", "tensor_tensor", b.xf[c % 2][:, :],
                          b.xf[c % 2][:, :], b.t1[:, :], op=ALU.add,
                          wait=(d2, xl), inc="DV")
                t1prev = xu
                stt = p.load(x_out[c, :, ot * 512:(ot + 1) * 512],
                             b.xf[c % 2][:, :], eng="gpsimd", sem=f"SX{c % 2}",
                             wait=(xu,))
                sttoks.append(stt)
        ar.pop()

        ar.push()
        b.xn = ar.alloc([128, NCH * 512], BF16)
        b.h = ar.alloc([128, HC3 * 512], BF16)
        b.yb = b.xn
        b.gsil = ar.alloc([128, 512], BF16)
        b.usc = ar.alloc([128, 512], BF16)
        b.wG = [ar.alloc([128, D], BF16) for _ in range(2)]
        b.wU = [ar.alloc([128, D], BF16) for _ in range(2)]
        b.wD = [ar.alloc([128, I3], BF16) for _ in range(2)]
        wfree = {"G": [None, None], "U": [None, None], "D": [None, None]}
        for ot in range(2):
            OTB = (p.now("PE"), p.now("DV"), p.now("AC"),
                   p.now("SX0"), p.now("SX1"), p.now("XI0"), p.now("XI1"))
            xtoks = ln1_into(b.xn, ot, OTB, xg="XH")
            WBar = OTB + (p.now("XN"), p.now("LC"))
            prev_hd = None
            for hc in range(HC3):
                pb = hc % 2
                wgl = p.load(b.wG[pb][:, :], wgu3[hc, :, :], eng="sync",
                             sem=f"WG{pb}",
                             wait=((wfree["G"][pb],) if wfree["G"][pb] else ()) + (WBar if hc < 2 else ()))
                wul = p.load(b.wU[pb][:, :], wgu3[HC3 + hc, :, :], eng="sync",
                             sem=f"WU{pb}",
                             wait=((wfree["U"][pb],) if wfree["U"][pb] else ()) + (WBar if hc < 2 else ()))
                w0 = bk.w(0)
                for c in range(NCH):
                    p.do("tensor", "matmul", bk.ap(0)[:, :],
                         b.wG[pb][:, c * 128:(c + 1) * 128],
                         b.xn[:, c * 512:(c + 1) * 512], start=(c == 0),
                         stop=(c == NCH - 1),
                         wait=((wgl, xtoks[-1]) + w0) if c == 0 else (),
                         inc="PE" if c == NCH - 1 else None)
                g_pe = p.now("PE")
                w1 = bk.w(1)
                for c in range(NCH):
                    p.do("tensor", "matmul", bk.ap(1)[:, :],
                         b.wU[pb][:, c * 128:(c + 1) * 128],
                         b.xn[:, c * 512:(c + 1) * 512], start=(c == 0),
                         stop=(c == NCH - 1),
                         wait=((wul,) + w1) if c == 0 else (),
                         inc="PE" if c == NCH - 1 else None)
                u_pe = p.now("PE")
                ga = p.do("scalar", "activation", b.gsil[:, :], bk.ap(0)[:, :],
                          AF.Silu, bias=0.0, scale=1.0,
                          wait=(g_pe,) + ((prev_hd,) if prev_hd else ()), inc="AC")
                bk.r(0, ga)
                wfree["G"][pb] = ga
                ua = p.do("scalar", "copy", b.usc[:, :], bk.ap(1)[:, :],
                          wait=(u_pe,), inc="AC")
                bk.r(1, ua)
                wfree["U"][pb] = ua
                prev_hd = p.do("vector", "tensor_tensor",
                               b.h[:, hc * 512:(hc + 1) * 512], b.gsil[:, :],
                               b.usc[:, :], op=ALU.mult, wait=(ga, ua), inc="DV")
            HDONE = p.now("DV")
            ytoks = []
            for ec in range(NCH):
                pb = ec % 2
                wdl = p.load(b.wD[pb][:, :], wdn3[ec, :, :], eng="sync",
                             sem=f"WD{pb}",
                             wait=((wfree["D"][pb],) if wfree["D"][pb] else ()) + (WBar if ec < 2 else ()))
                w2 = bk.w(2)
                for hc in range(HC3):
                    p.do("tensor", "matmul", bk.ap(2)[:, :],
                         b.wD[pb][:, hc * 128:(hc + 1) * 128],
                         b.h[:, hc * 512:(hc + 1) * 512], start=(hc == 0),
                         stop=(hc == HC3 - 1),
                         wait=((wdl, HDONE) + w2) if hc == 0 else (),
                         inc="PE" if hc == HC3 - 1 else None)
                d_pe = p.now("PE")
                ya = p.do("scalar", "copy", b.yb[:, ec * 512:(ec + 1) * 512],
                          bk.ap(2)[:, :], wait=(d_pe,), inc="AC")
                bk.r(2, ya)
                wfree["D"][pb] = ya
                ytoks.append(ya)
            SXB = (p.now("SX0"), p.now("SX1"))
            srcs = [(b.yb[:, c * 512:(c + 1) * 512], ytoks[c]) for c in range(NCH)]
            bkw6 = bk.w(6); bkw7 = bk.w(7)
            sq_prev = None
            for c, (src, wt) in enumerate(srcs):
                st = dict(start=(c == 0), stop=(c == NCH - 1))
                p.do("tensor", "matmul", bk.ap(6)[0:1, :], b.ones_col_bf[:, :],
                     src, wait=((wt,) if wt else ()) + (bkw6 if c == 0 else ()),
                     **st, inc="PE" if c == NCH - 1 else None)
                at = p.do("scalar", "activation", b.sq[:, :], src, AF.Square,
                          wait=((wt,) if wt else ()) + ((sq_prev,) if sq_prev else ()),
                          inc="AC")
                sq_prev = p.do("tensor", "matmul", bk.ap(7)[0:1, :],
                               b.ones_col_bf[:, :], b.sq[:, :],
                               wait=(at,) + (bkw7 if c == 0 else ()), **st,
                               inc="PE")
            pe = p.now("PE")
            d1 = p.do("vector", "tensor_scalar", ln.vm[:, :], bk.ap(6)[0:1, :],
                      1.0 / D, None, op0=ALU.mult, wait=(pe,), inc="DV")
            d2 = p.do("vector", "tensor_scalar", ln.vrs[:, :], bk.ap(7)[0:1, :],
                      1.0 / D, None, op0=ALU.mult, wait=(pe,), inc="DV")
            bk.r(6, d1); bk.r(7, d2)
            d3 = p.do("vector", "tensor_scalar", ln.vneg[:, :], ln.vrs[:, :],
                      1e-6, None, op0=ALU.add, wait=(d2,), inc="DV")
            d4 = p.do("vector", "tensor_tensor", ln.vsd[:, :], ln.vm[:, :],
                      ln.vm[:, :], op=ALU.mult, wait=(d1,), inc="DV")
            d5 = p.do("vector", "tensor_tensor", ln.vrs[:, :], ln.vrs[:, :],
                      ln.vsd[:, :], op=ALU.subtract, wait=(d2, d4), inc="DV")
            d6 = p.do("vector", "scalar_tensor_tensor", ln.vrs[:, :],
                      ln.vneg[:, :], 1e-5, ln.vrs[:, :], op0=ALU.mult,
                      op1=ALU.add, wait=(d3, d5), inc="DV")
            sa = p.do("scalar", "activation", ln.vsd[:, :], ln.vrs[:, :],
                      AF.Sqrt, bias=0.0, scale=1.0, wait=(d6,), inc="AC")
            d7 = p.do("vector", "reciprocal", ln.vrs[:, :], ln.vsd[:, :],
                      wait=(sa,), inc="DV")
            d8 = p.do("vector", "tensor_scalar", ln.vneg[:, :], ln.vm[:, :],
                      -1.0, None, op0=ALU.mult, wait=(d4,), inc="DV")
            ln.rows = (d7, d8)
            a1, a2 = ln.bcast(p, bk, 4, 5)
            t1prev = None
            sttoks = []
            for c in range(NCH):
                xl = p.load(b.xf[c % 2][:, :], x_out[c, :, ot * 512:(ot + 1) * 512],
                            sem=f"XI{c % 2}",
                            wait=(sttoks[c - 2],) if c >= 2 else (a1, a2) + SXB)
                d1 = p.do("vector", "tensor_tensor", b.t1[:, :],
                          b.yb[:, c * 512:(c + 1) * 512], ln.rs_t[:, :],
                          op=ALU.mult, wait=(a1, a2) + ((t1prev,) if t1prev else ()),
                          inc="DV")
                d2 = p.do("vector", "tensor_tensor", b.t1[:, :], b.t1[:, :],
                          ln.c_t[:, :], op=ALU.add, wait=(d1,), inc="DV")
                xu = p.do("vector", "tensor_tensor", b.xf[c % 2][:, :],
                          b.xf[c % 2][:, :], b.t1[:, :], op=ALU.add,
                          wait=(d2, xl), inc="DV")
                t1prev = xu
                stt = p.load(x_out[c, :, ot * 512:(ot + 1) * 512],
                             b.xf[c % 2][:, :], eng="gpsimd", sem=f"SX{c % 2}",
                             wait=(xu,))
                sttoks.append(stt)
        ar.pop()

        p.do("gpsimd", "nop", wait=(p.now("SX0"), p.now("SX1"), p.now("ZF")))
        p.emit()
    return nc


def prep_launch4(I, x5_pc, xn4_pc, Xn_fm):
    com = {}
    Xf = np.ascontiguousarray(
        Xn_fm.transpose(1, 0, 2)).reshape(128, NCH * 256)
    com["Xfm"] = Xf
    Xtm_full = Xn_fm.reshape(D, 256).T
    com["Xtm"] = np.ascontiguousarray(Xtm_full.reshape(2, 128, D))
    com["blT"] = np.ascontiguousarray(
        I["L3_attn_B_l"].T.astype(BF).reshape(2, 128, 256))
    com["clT"] = np.ascontiguousarray(
        I["L3_attn_C_l"].T.astype(BF).reshape(2, 128, 256))
    com["alT"] = np.ascontiguousarray(
        I["L3_attn_A_l"].T.astype(BF).reshape(2, 128, 256))
    com["rl"] = np.ascontiguousarray(
        np.broadcast_to(I["L3_attn_r_l"].reshape(1, 1), (128, 1)).copy())
    wgu = np.concatenate(prep_launch4.wgu_parts, axis=1)
    com["wgu3"] = np.ascontiguousarray(
        wgu.reshape(NCH, 128, 2 * HC3, 128).transpose(2, 1, 0, 3)).reshape(2 * HC3, 128, D)
    wdn = np.concatenate(prep_launch4.wdn_parts, axis=1)
    com["wdn3"] = np.ascontiguousarray(
        wdn.reshape(HC3, 128, NCH, 128).transpose(2, 1, 0, 3)).reshape(NCH, 128, I3)
    in_maps = []
    for c in range(NC):
        a, bb = zz_chunks(c)
        own = np.r_[a * 512:(a + 1) * 512, bb * 512:(bb + 1) * 512]
        bl = (own < 256).astype(np.float32)
        im = dict(com)
        im["xin"] = x5_pc[c]
        im["xn4"] = xn4_pc[c]
        im["blendc"] = np.ascontiguousarray(
            np.broadcast_to(bl[None, :], (128, TC)).copy())
        im["omblendc"] = np.ascontiguousarray(
            np.broadcast_to((1.0 - bl)[None, :], (128, TC)).copy())
        in_maps.append(im)
    return in_maps

def _probe_devmap():
    if DEVMAP[0] is not None:
        return
    nc = bass.Bass("TRN2", target_bir_lowering=False, debug=False)
    out = nc.dram_tensor("o", [128, 4], F32, kind="ExternalOutput").ap()
    stack = ExitStack()
    with stack:
        ar = stack.enter_context(nc.sbuf_tensor("parena", [128, 1024], U8))
        sem = stack.enter_context(nc.semaphore("S"))
        sem2 = stack.enter_context(nc.semaphore("S2"))
        blk = stack.enter_context(nc.Block())
        fo = ar[:, 0:16].bitcast(F32)

        @blk.gpsimd
        def _(h):
            k = 0
            for arm in nc.gpsimd.switch_core_id(128):
                nc.gpsimd.memset(fo[:, :], float(k)).then_inc(sem, 1)
                k += 1
            nc.gpsimd.wait_ge(sem, 1)
            nc.gpsimd.dma_start(out=out[:, :], in_=fo[:, :]).then_inc(sem2, 16)
            nc.gpsimd.wait_ge(sem2, 16)
    r = run_bass_kernel_spmd(nc, [dict() for _ in range(NC)],
                             core_ids=list(range(NC)))
    DEVMAP[0] = [int(r.results[c]["o"][0, 0]) for c in range(NC)]


def kernel(**inputs):
    I = inputs
    _probe_devmap()
    in1 = prep_launch1(I)
    nc1 = build_launch1()
    r1 = run_bass_kernel_spmd(nc1, in1, core_ids=list(range(NC)))
    x2_pc = [r1.results[c]["x_out"] for c in range(NC)]
    xn1_pc = [r1.results[c]["xn1_out"] for c in range(NC)]
    k1_pc = [r1.results[c]["k1_out"] for c in range(NC)]
    v1_pc = [r1.results[c]["v1_out"] for c in range(NC)]

    in2 = prep_launch2(I, x2_pc, xn1_pc, k1_pc, v1_pc)
    nc2 = build_launch2()
    r2 = run_bass_kernel_spmd(nc2, in2, core_ids=list(range(NC)))
    x3_pc = [r2.results[c]["x_out"] for c in range(NC)]
    a2p_pc = [r2.results[c]["a2p_out"] for c in range(NC)]
    qd_pc = [r2.results[c]["qd_out"] for c in range(NC)]
    sums_pc = [r2.results[c]["sum_out"] for c in range(NC)]

    in3 = prep_launch3(I, x3_pc, a2p_pc, qd_pc, sums_pc)
    nc3 = build_launch3()
    r3 = run_bass_kernel_spmd(nc3, in3, core_ids=list(range(NC)))
    x5_pc = [r3.results[c]["x_out"] for c in range(NC)]
    xn4_pc = [r3.results[c]["xn4_out"] for c in range(NC)]
    prep_launch4.wgu_parts = [r3.results[c]["wdq_gu"] for c in range(NC)]
    prep_launch4.wdn_parts = [r3.results[c]["wdq_dn"] for c in range(NC)]
    Xn_fm = xn4_pc[0][:, :, 0:256].copy()

    in4 = prep_launch4(I, x5_pc, xn4_pc, Xn_fm)
    nc4 = build_launch4()
    r4 = run_bass_kernel_spmd(nc4, in4, core_ids=list(range(NC)))

    out = np.empty((T, D), np.float32)
    for c in range(NC):
        a, bb = zz_chunks(c)
        xfm = r4.results[c]["x_out"].reshape(D, TC)
        out[a * 512:(a + 1) * 512] = xfm[:, 0:512].T
        out[bb * 512:(bb + 1) * 512] = xfm[:, 512:1024].T
    return out.reshape(1, T, D)


# version 18
# speedup vs baseline: 1.0042x; 1.0042x over previous
import numpy as np
import ml_dtypes
from contextlib import ExitStack

import concourse.bass as bass
import concourse.mybir as mybir
from concourse.bass_utils import run_bass_kernel_spmd

F32 = mybir.dt.float32
BF16 = mybir.dt.bfloat16
I32 = mybir.dt.int32
U8 = mybir.dt.uint8
F16 = mybir.dt.float16
AF = mybir.ActivationFunctionType
ALU = mybir.AluOpType
BF = ml_dtypes.bfloat16

NC = 8
T = 8192
D = 2048
TC = 1024
NCH = 16
KILL = -30000.0
SHIFT = -12.0
NV_L0 = (28, 60)
DEVMAP = [None]
SIM_CORE_OVERRIDE = [None]


def zz_chunks(c):
    return (c, 15 - c)


class Prog:
    ENGS = ("tensor", "vector", "scalar", "gpsimd", "sync")

    def __init__(self, nc, stack):
        self.nc = nc
        self.stack = stack
        self.q = {e: [] for e in self.ENGS}
        self.sems = {}
        self.cnt = {}

    def _sem(self, name):
        if name not in self.sems:
            self.sems[name] = self.stack.enter_context(self.nc.semaphore(name))
        if name not in self.cnt:
            self.cnt[name] = 0
        return self.sems[name]

    def now(self, name):
        self._sem(name)
        return (name, self.cnt[name])

    def do(self, eng, meth, *args, wait=(), inc=None, inck=1, **kw):
        tok = None
        incs = ()
        if inc is not None:
            self._sem(inc)
            self.cnt[inc] += inck
            tok = (inc, self.cnt[inc])
            incs = ((inc, inck),)
        wl = tuple((w[0], w[1]) for w in wait if w is not None)
        self.q[eng].append((meth, args, kw, wl, incs))
        return tok

    def load(self, dst, src, eng="gpsimd", sem=None, wait=()):
        sem = sem or ("LD" if eng == "gpsimd" else "SD")
        return self.do(eng, "dma_start", wait=wait, inc=sem, inck=16,
                       out=dst, in_=src)

    def percore(self, bk, build_fn, boxes=()):
        cnt0 = dict(self.cnt)
        g0 = list(bk.guard)
        b0 = [list(b_) for b_ in boxes]
        marks = {e: len(self.q[e]) for e in self.ENGS}
        streams = []
        fin_cnt = None
        fin_g = None
        fin_b = None
        for core in range(NC):
            self.cnt = dict(cnt0)
            bk.guard = list(g0)
            for b_, init in zip(boxes, b0):
                b_[:] = list(init)
            build_fn(core)
            streams.append({e: self.q[e][marks[e]:] for e in self.ENGS})
            for e in self.ENGS:
                del self.q[e][marks[e]:]
            if fin_cnt is None:
                fin_cnt = dict(self.cnt)
                fin_g = list(bk.guard)
                fin_b = [list(b_) for b_ in boxes]
            else:
                assert fin_cnt == self.cnt, "per-core sem count mismatch"
                assert fin_g == bk.guard, "per-core bank guard mismatch"
                assert fin_b == [list(b_) for b_ in boxes], "box mismatch"
        self.cnt = fin_cnt
        bk.guard = fin_g
        for b_, fin in zip(boxes, fin_b):
            b_[:] = list(fin)
        for e in self.ENGS:
            self.q[e].append(("__percore__", (streams, e), {}, (), ()))

    def barrier_tokens(self):
        return tuple(self.now(s) for s in list(self.cnt))

    def emit(self):
        blk = self.stack.enter_context(self.nc.Block())
        sems = self.sems
        devmap = DEVMAP[0]
        body_to_core = {}
        if devmap is not None:
            for lg, k2 in enumerate(devmap):
                body_to_core[k2] = lg

        def runner(ops):
            def replay(h, seq):
                for meth, args, kw, waits, incs in seq:
                    for (sn, v) in waits:
                        h.wait_ge(sems[sn], v)
                    inst = getattr(h, meth)(*args, **kw)
                    for (sn, k) in incs:
                        inst.then_inc(sems[sn], k)

            def body(h):
                for entry in ops:
                    if entry[0] == "__percore__":
                        streams, eng = entry[1]
                        k = 0
                        for arm in h.switch_core_id(128):
                            core = body_to_core.get(k)
                            if core is None:
                                lg = getattr(arm, "logical", None)
                                if lg is not None and 0 <= lg < NC:
                                    core = lg
                            if core is not None:
                                if SIM_CORE_OVERRIDE[0] is not None:
                                    core = SIM_CORE_OVERRIDE[0]
                                replay(h, streams[core][eng])
                            k += 1
                    else:
                        replay(h, [entry])
            return body

        blk.tensor(runner(self.q["tensor"]))
        blk.vector(runner(self.q["vector"]))
        blk.scalar(runner(self.q["scalar"]))
        blk.gpsimd(runner(self.q["gpsimd"]))
        blk.sync(runner(self.q["sync"]))

    def n_instructions(self):
        return {e: len(q) for e, q in self.q.items()}


class Arena:
    def __init__(self, nc, stack, name, nbytes):
        self.t = stack.enter_context(nc.sbuf_tensor(name, [128, nbytes], U8))
        self.off = 0
        self.marks = []
        self.hi = 0

    def alloc(self, shape, dtype, align=32):
        assert shape[0] <= 128
        esz = mybir.dt.size(dtype)
        self.off = (self.off + align - 1) // align * align
        nfree = int(np.prod(shape[1:])) if len(shape) > 1 else 1
        nb = nfree * esz
        ap = self.t[0:shape[0], self.off:self.off + nb].bitcast(dtype)
        self.off += nb
        self.hi = max(self.hi, self.off)
        if len(shape) == 3:
            ap = ap.rearrange("p (a b) -> p a b", a=shape[1], b=shape[2])
        return ap

    def push(self):
        self.marks.append(self.off)

    def pop(self):
        self.off = self.marks.pop()


class Banks:

    def __init__(self, nc, stack):
        self.b = [stack.enter_context(nc.psum_tensor(f"bk{i}", [128, 512], F32))
                  for i in range(8)]
        self.guard = [None] * 8

    def ap(self, i):
        return self.b[i]

    def w(self, i):
        g = self.guard[i]
        self.guard[i] = None
        return (g,) if g else ()

    def r(self, i, tok):
        self.guard[i] = tok


class B:
    pass


class LNH:
    def __init__(self, p, b, ar):
        self.p, self.b = p, b
        b.sq2 = ar.alloc([128, 512], BF16)
        self.vm = ar.alloc([1, 512], F32)
        self.vrs = ar.alloc([1, 512], F32)
        self.vsd = ar.alloc([1, 512], F32)
        self.vneg = ar.alloc([1, 512], F32)
        self.rs_t = ar.alloc([128, 512], F32)
        self.c_t = ar.alloc([128, 512], F32)
        self.prev_rows = None
        self.prev_bc = None

    def stats(self, p, bk, srcs, s1, s2, need_sd=False):
        b = self.b
        w1 = bk.w(s1)
        w2 = bk.w(s2)
        sqb = (b.sq, b.sq2)
        n = len(srcs)
        ats = [None] * n
        mm2 = [None] * n
        for c, (src, wt) in enumerate(srcs):
            ats[c] = p.do("scalar", "activation", sqb[c % 2][:, :], src,
                          AF.Square,
                          wait=((wt,) if wt else ())
                          + ((mm2[c - 2],) if c >= 2 else ()),
                          inc="AC")
            p.do("tensor", "matmul", bk.ap(s1)[0:1, :], b.ones_col_bf[:, :], src,
                 wait=((wt,) if wt else ()) + (w1 if c == 0 else ()),
                 start=(c == 0), stop=(c == n - 1),
                 inc="PE" if c == n - 1 else None)
            if c >= 1:
                mm2[c - 1] = p.do("tensor", "matmul", bk.ap(s2)[0:1, :],
                                  b.ones_col_bf[:, :], sqb[(c - 1) % 2][:, :],
                                  wait=(ats[c - 1],) + (w2 if c == 1 else ()),
                                  start=(c == 1), stop=False, inc="PE")
        mm2[n - 1] = p.do("tensor", "matmul", bk.ap(s2)[0:1, :],
                          b.ones_col_bf[:, :], sqb[(n - 1) % 2][:, :],
                          wait=(ats[n - 1],), start=False, stop=True, inc="PE")
        pe = p.now("PE")
        d1 = p.do("vector", "tensor_scalar", self.vm[:, :], bk.ap(s1)[0:1, :],
                  1.0 / D, None, op0=ALU.mult, wait=(pe,), inc="DV")
        d2 = p.do("vector", "tensor_scalar", self.vrs[:, :], bk.ap(s2)[0:1, :],
                  1.0 / D, None, op0=ALU.mult, wait=(pe,), inc="DV")
        bk.r(s1, d1)
        bk.r(s2, d2)
        d3 = p.do("vector", "tensor_tensor", self.vsd[:, :], self.vm[:, :],
                  self.vm[:, :], op=ALU.mult, wait=(d1,), inc="DV")
        d4 = p.do("vector", "tensor_tensor", self.vrs[:, :], self.vrs[:, :],
                  self.vsd[:, :], op=ALU.subtract, wait=(d2, d3), inc="DV")
        if need_sd:
            a1 = p.do("scalar", "activation", self.vsd[:, :], self.vrs[:, :],
                      AF.Sqrt, bias=b.epscol[0:1, :], scale=1.0, wait=(d4,),
                      inc="AC")
            d5 = p.do("vector", "reciprocal", self.vrs[:, :], self.vsd[:, :],
                      wait=(a1,), inc="DV")
        else:
            a1 = p.do("scalar", "activation", self.vsd[:, :], self.vrs[:, :],
                      AF.Sqrt, bias=b.epscol[0:1, :], scale=1.0, wait=(d4,),
                      inc="AC")
            d5 = p.do("vector", "reciprocal", self.vrs[:, :], self.vsd[:, :],
                      wait=(a1,), inc="DV")
        d6 = p.do("vector", "tensor_scalar", self.vneg[:, :], self.vm[:, :],
                  -1.0, None, op0=ALU.mult, wait=(d1,), inc="DV")
        self.rows = (d5, d6)
        return d5, d6

    def bcast(self, p, bk, b1, b2):
        b = self.b
        d5, d6 = self.rows
        d7 = p.do("vector", "tensor_tensor", self.vm[:, :], self.vneg[:, :],
                  self.vrs[:, :], op=ALU.mult, wait=(d5, d6), inc="DV")
        t1 = p.do("tensor", "matmul", bk.ap(b1)[:, :], b.ones_row_f[:, :],
                  self.vrs[:, :], start=True, stop=True,
                  wait=(d5,) + bk.w(b1), inc="PE")
        t2 = p.do("tensor", "matmul", bk.ap(b2)[:, :], b.ones_row_f[:, :],
                  self.vm[:, :], start=True, stop=True,
                  wait=(d7,) + bk.w(b2), inc="PE")
        a1 = p.do("scalar", "copy", self.rs_t[:, :], bk.ap(b1)[:, :],
                  wait=(t1,), inc="AC")
        a2 = p.do("scalar", "copy", self.c_t[:, :], bk.ap(b2)[:, :],
                  wait=(t2,), inc="AC")
        bk.r(b1, a1)
        bk.r(b2, a2)
        return a1, a2


HB0 = 5632
HC0 = 44


def build_launch1():
    nc = bass.Bass("TRN2", target_bir_lowering=False, debug=False)
    stack = ExitStack()
    nc._stk = stack

    xT = nc.dram_tensor("xT", [NCH, 128, T], F32, kind="ExternalInput").ap()
    xown = nc.dram_tensor("xown", [NCH, 128, TC], F32, kind="ExternalInput").ap()
    wq = nc.dram_tensor("wq", [128, NCH * 128], F32, kind="ExternalInput").ap()
    wk = nc.dram_tensor("wk", [128, NCH * 128], F32, kind="ExternalInput").ap()
    wv = nc.dram_tensor("wv", [128, NCH * 128], F32, kind="ExternalInput").ap()
    wo = nc.dram_tensor("wo", [128, D], F32, kind="ExternalInput").ap()
    bq = nc.dram_tensor("bq", [128, 1], F32, kind="ExternalInput").ap()
    bk_in = nc.dram_tensor("bk", [128, 1], F32, kind="ExternalInput").ap()
    bv = nc.dram_tensor("bv", [1, 128], F32, kind="ExternalInput").ap()
    bo = nc.dram_tensor("bo", [128, NCH], F32, kind="ExternalInput").ap()
    tri = nc.dram_tensor("tri", [128, 4 * 512], F32, kind="ExternalInput").ap()
    kbias = nc.dram_tensor("kbias", [128, 128], F32, kind="ExternalInput").ap()
    wg_img = nc.dram_tensor("wg_img", [HC0, 128, D], U8, kind="ExternalInput").ap()
    wu_img = nc.dram_tensor("wu_img", [HC0, 128, D], U8, kind="ExternalInput").ap()
    wd_img = nc.dram_tensor("wd_img", [NCH, 128, HB0], U8, kind="ExternalInput").ap()
    sg = nc.dram_tensor("sg", [128, HC0], F32, kind="ExternalInput").ap()
    su = nc.dram_tensor("su", [128, HC0], F32, kind="ExternalInput").ap()
    sdn = nc.dram_tensor("sdn", [128, NCH], F32, kind="ExternalInput").ap()
    wk1 = nc.dram_tensor("wk1", [128, NCH * 512], F32, kind="ExternalInput").ap()
    wv1 = nc.dram_tensor("wv1", [128, NCH * 512], F32, kind="ExternalInput").ap()

    x_out = nc.dram_tensor("x_out", [NCH, 128, TC], F32, kind="ExternalOutput").ap()
    xn1_out = nc.dram_tensor("xn1_out", [NCH, 128, TC], BF16, kind="ExternalOutput").ap()
    k1_out = nc.dram_tensor("k1_out", [4, 128, TC], BF16, kind="ExternalOutput").ap()
    v1_out = nc.dram_tensor("v1_out", [TC, 512], BF16, kind="ExternalOutput").ap()

    with stack:
        p = Prog(nc, stack)
        ar = Arena(nc, stack, "arena", 206 * 1024)
        bk = Banks(nc, stack)
        b = B()

        b.x1 = ar.alloc([128, NCH, TC], F32)
        b.ones_col_bf = ar.alloc([128, 1], BF16)
        b.epscol = ar.alloc([128, 1], F32)
        b.shiftcol = ar.alloc([128, 1], F32)
        b.ones_row_f = ar.alloc([1, 128], F32)
        b.ones_row_bf = ar.alloc([1, 128], BF16)
        b.sq = ar.alloc([128, 512], BF16)
        b.idf = ar.alloc([128, 128], F32)
        b.bqc = ar.alloc([128, 1], F32)
        b.bkc = ar.alloc([128, 1], F32)
        b.bvr = ar.alloc([1, 128], F32)
        b.bv_bf = ar.alloc([1, 128], BF16)
        b.bos = ar.alloc([128, NCH], F32)
        b.csq = ar.alloc([1, 128], BF16)
        b.csk = ar.alloc([1, 128], BF16)
        b.csv = ar.alloc([1, 128], BF16)
        b.bq_row = ar.alloc([1, 128], BF16)
        b.bk_row = ar.alloc([1, 128], BF16)
        b.rcol = ar.alloc([128, 8], F32)
        b.negm_bf = ar.alloc([1, 512], BF16)
        b.sd_bf = ar.alloc([1, 512], BF16)
        b.t1 = ar.alloc([128, 512], F32)

        ln = LNH(p, b, ar)

        p.load(b.bqc[:, :], bq[:, :], sem="LC")
        p.load(b.bkc[:, :], bk_in[:, :], sem="LC")
        p.load(b.bvr[:, :], bv[:, :], sem="LC")
        p.load(b.bos[:, :], bo[:, :], sem="LC")
        ident_dram = nc.inline_tensor(np.eye(128, dtype=np.float32), "idf_c").ap()
        p.load(b.idf[:, :], ident_dram[:, :], sem="LC")
        p.do("gpsimd", "memset", b.ones_col_bf[:, :], 1.0, inc="GP")
        p.do("gpsimd", "memset", b.ones_row_f[:, :], 1.0, inc="GP")
        p.do("gpsimd", "memset", b.ones_row_bf[:, :], 1.0, inc="GP")
        p.do("gpsimd", "memset", b.epscol[:, :], 1e-5, inc="GP")
        p.do("gpsimd", "memset", b.shiftcol[:, :], SHIFT, inc="GP")
        CGP = p.now("GP")

        def colsum_biasrow(wfull, dst_row, bias_col, dst_bias_row, first):
            for c in range(NCH):
                p.do("tensor", "matmul", bk.ap(4)[0:1, 0:128], b.ones_col_bf[:, :],
                     wfull[:, c * 128:(c + 1) * 128], start=(c == 0),
                     stop=(c == NCH - 1),
                     wait=((CLD, CGP) if (c == 0 and first) else ()) + (bk.w(4) if c == 0 else ()),
                     inc="PE" if c == NCH - 1 else None)
            tpe = p.now("PE")
            a = p.do("scalar", "copy", dst_row[:, :], bk.ap(4)[0:1, 0:128],
                     wait=(tpe,), inc="AC")
            bk.r(4, a)
            if bias_col is not None:
                tt = p.do("tensor", "transpose", bk.ap(5)[0:1, 0:128],
                          bias_col[:, :], b.idf[:, :],
                          wait=bk.w(5) + ((CLD,) if first else ()), inc="PE")
                a2 = p.do("scalar", "copy", dst_bias_row[:, :],
                          bk.ap(5)[0:1, 0:128], wait=(tt,), inc="AC")
                bk.r(5, a2)

        ar.push()
        b.wqs = ar.alloc([128, NCH * 128], BF16)
        b.wks = ar.alloc([128, NCH * 128], BF16)
        b.wvs = ar.alloc([128, NCH * 128], BF16)
        b.wos = ar.alloc([128, D], BF16)
        b.kbias = ar.alloc([128, 128], F32)
        b.tri = ar.alloc([128, 4 * 512], F32)
        b.recip = ar.alloc([1, 512], F32)
        b.rcp_t = ar.alloc([128, 512], F32)
        p.load(b.wqs[:, :], wq[:, :], sem="LC")
        p.load(b.wks[:, :], wk[:, :], sem="LC")
        p.load(b.wvs[:, :], wv[:, :], sem="LC")
        p.load(b.wos[:, :], wo[:, :], sem="LC")
        p.load(b.kbias[:, :], kbias[:, :], sem="LC")
        p.load(b.tri[:, :], tri[:, :], sem="LC")
        CLD = p.now("LC")
        colsum_biasrow(b.wqs, b.csq, b.bqc, b.bq_row, True)
        colsum_biasrow(b.wks, b.csk, b.bkc, b.bk_row, False)
        colsum_biasrow(b.wvs, b.csv, None, None, False)
        p.do("vector", "tensor_copy", b.bv_bf[:, :], b.bvr[:, :], wait=(CLD,), inc="DV")
        b.Kf = ar.alloc([128, T], BF16)
        b.Vt = ar.alloc([128, T], BF16)
        b.Qf = ar.alloc([128, TC], BF16)
        b.Kown = ar.alloc([128, TC], BF16)
        b.Vown = ar.alloc([128, TC], BF16)
        b.xb = [ar.alloc([128, NCH * 512], BF16) for _ in range(2)]
        b.ybf = ar.alloc([128, NCH * 512], BF16)
        b.attn = ar.alloc([128, 2 * 512], BF16)
        b.pt = ar.alloc([128, 1024], BF16)
        xbfree = [None, None]
        vfree = [None, None]

        def rs_cols_and_rows(nsub):
            d5, d6 = ln.rows
            lasta = None
            for j in range(nsub):
                bt = 4 + (j % 2)
                tt = p.do("tensor", "transpose", bk.ap(bt)[:, 0:1],
                          ln.vrs[:, j * 128:(j + 1) * 128], b.idf[0:1, 0:1],
                          wait=(d5,) + bk.w(bt), inc="PE")
                lasta = p.do("scalar", "copy", b.rcol[:, j:j + 1],
                             bk.ap(bt)[:, 0:1], wait=(tt,), inc="AC")
                bk.r(bt, lasta)
            nb = p.do("vector", "tensor_copy", b.negm_bf[:, :], ln.vneg[:, :],
                      wait=(d6,), inc="DV")
            sb = p.do("vector", "tensor_copy", b.sd_bf[:, :], ln.vsd[:, :],
                      wait=(d5,), inc="DV")
            return nb, sb, lasta

        def proj_fm(dst, wfull, xb_t, bi, csum_row, bias_row, nbsb):
            nb, sb = nbsb[0], nbsb[1]
            for c in range(NCH):
                p.do("tensor", "matmul", bk.ap(bi)[:, :],
                     wfull[:, c * 128:(c + 1) * 128],
                     xb_t[:, c * 512:(c + 1) * 512],
                     start=(c == 0), stop=False,
                     wait=bk.w(bi) if c == 0 else ())
            p.do("tensor", "matmul", bk.ap(bi)[:, :], csum_row[:, :],
                 b.negm_bf[:, :], start=False, stop=False, wait=(nb,))
            tpe = p.do("tensor", "matmul", bk.ap(bi)[:, :], bias_row[:, :],
                       b.sd_bf[:, :], start=False, stop=True, wait=(sb,),
                       inc="PE")
            dv = p.do("vector", "tensor_tensor", dst, bk.ap(bi)[:, :],
                      ln.rs_t[:, :], op=ALU.mult, wait=(tpe,), inc="DV")
            bk.r(bi, dv)
            return dv

        def proj_tm(dst, xb_t, wfull, csum_row, bias_row, nbsb):
            nb, sb, rct = nbsb
            last = None
            for j in range(4):
                bi = 2 + (j % 2)
                for c in range(NCH):
                    p.do("tensor", "matmul", bk.ap(bi)[:, 0:128],
                         xb_t[:, c * 512 + j * 128: c * 512 + (j + 1) * 128],
                         wfull[:, c * 128:(c + 1) * 128],
                         start=(c == 0), stop=False,
                         wait=bk.w(bi) if c == 0 else ())
                p.do("tensor", "matmul", bk.ap(bi)[:, 0:128],
                     b.negm_bf[:, j * 128:(j + 1) * 128], csum_row[:, :],
                     start=False, stop=False, wait=(nb,))
                tpe = p.do("tensor", "matmul", bk.ap(bi)[:, 0:128],
                           b.ones_row_bf[:, :], bias_row[:, :],
                           start=False, stop=True, wait=(sb,), inc="PE")
                a = p.do("scalar", "activation", dst[:, j * 128:(j + 1) * 128],
                         bk.ap(bi)[:, 0:128], AF.Identity, bias=0.0,
                         scale=b.rcol[:, j:j + 1], wait=(tpe, rct), inc="AC")
                bk.r(bi, a)
                last = a
            return last

        for ch in range(NCH):
            p.load(b.x1[:, ch, :], xown[ch, :, :], eng="sync", sem="X1")
        X1LD = p.now("X1")

        for src_dram, dsts in ((xown, "own"), (xT, "full")):
            nt = 2 if dsts == "own" else NCH
            for tt in range(nt):
                par = tt % 2
                xb_t = b.xb[par]
                lds = []
                for ch in range(NCH):
                    w = (xbfree[par],) if (ch == 0 and xbfree[par]) else ()
                    srcap = src_dram[ch, :, tt * 512:(tt + 1) * 512]
                    lds.append(p.load(xb_t[:, ch * 512:(ch + 1) * 512], srcap,
                                      wait=w, sem=f"XB{par}"))
                alltok = lds[-1]
                srcs = [(xb_t[:, c * 512:(c + 1) * 512], alltok) for c in range(NCH)]
                ln.stats(p, bk, srcs, 6, 7, need_sd=True)
                ln.bcast(p, bk, 4, 5)
                nbsb = rs_cols_and_rows(4)
                if dsts == "own":
                    proj_fm(b.Qf[:, tt * 512:(tt + 1) * 512], b.wqs, xb_t, 0,
                            b.csq, b.bq_row, nbsb)
                    proj_fm(b.Kown[:, tt * 512:(tt + 1) * 512], b.wks, xb_t, 1,
                            b.csk, b.bk_row, nbsb)
                    xbfree[par] = proj_tm(b.Vown[:, tt * 512:(tt + 1) * 512],
                                          xb_t, b.wvs, b.csv, b.bv_bf, nbsb)
                else:
                    proj_fm(b.Kf[:, tt * 512:(tt + 1) * 512], b.wks, xb_t, 1,
                            b.csk, b.bk_row, nbsb)
                    xbfree[par] = proj_tm(b.Vt[:, tt * 512:(tt + 1) * 512],
                                          xb_t, b.wvs, b.csv, b.bv_bf, nbsb)

        A1DV = p.now("DV")
        A1AC = p.now("AC")
        scale0 = 1.0 / np.sqrt(128.0)

        def build_a2(core):
            a_c, b_c = zz_chunks(core)
            vis = ([(0, "main", kc) for kc in range(4 * a_c)]
                   + [(0, "diag", j) for j in range(4)]
                   + [(1, "main", kc) for kc in range(4 * b_c)]
                   + [(1, "diag", j) for j in range(4)])
            assert len(vis) == 68
            nv0 = 4 * a_c + 4
            prev_ae = [None, None]
            wN0 = bk.w(4)
            wA0 = bk.w(2)
            wN1 = bk.w(5)
            wA1 = bk.w(3)
            vi_box = [0]
            pend = None

            def navs1(v, ae):
                qt, kind, kc = v
                vi = vi_box[0]
                half = (vi % 2) * 512
                if kind == "main":
                    vap = b.Vt[:, kc * 128:(kc + 1) * 128]
                else:
                    vap = b.Vown[:, qt * 512 + kc * 128: qt * 512 + (kc + 1) * 128]
                first = (vi == 0) if qt == 0 else (vi == nv0)
                last = (vi == nv0 - 1) if qt == 0 else (vi == 67)
                bN = 4 if qt == 0 else 5
                bA = 2 if qt == 0 else 3
                wN_ = (wN0 if qt == 0 else wN1) if first else ()
                wA_ = (wA0 if qt == 0 else wA1) if first else ()
                p.do("tensor", "matmul", bk.ap(bN)[0:1, :], b.ones_col_bf[:, :],
                     b.pt[:, half:half + 512],
                     wait=(ae,) + wN_, start=first, stop=last)
                p.do("tensor", "matmul", bk.ap(bA)[:, :], vap,
                     b.pt[:, half:half + 512],
                     wait=wA_, start=first, stop=last,
                     inc="PE" if vi == 67 else None)
                vi_box[0] = vi + 1

            for si, v in enumerate(vis):
                qt, kind, kc = v
                sb_i = si % 2
                half = (si % 2) * 512
                if kind == "main":
                    kap = b.Kf[:, kc * 128:(kc + 1) * 128]
                else:
                    kap = b.Kown[:, qt * 512 + kc * 128: qt * 512 + (kc + 1) * 128]
                ew = (A1DV, A1AC) if si == 0 else ()
                ts = p.do("tensor", "matmul", bk.ap(sb_i)[:, :], kap,
                          b.Qf[:, qt * 512:(qt + 1) * 512], start=True,
                          stop=True,
                          wait=((prev_ae[sb_i],) if prev_ae[sb_i] else ())
                          + bk.w(sb_i) + ew,
                          inc="PE")
                if kind == "diag":
                    ts = p.do("vector", "tensor_tensor", bk.ap(sb_i)[:, :],
                              bk.ap(sb_i)[:, :],
                              b.tri[:, kc * 512:(kc + 1) * 512],
                              op=ALU.add, wait=(ts,), inc="DV")
                ae = p.do("scalar", "activation", b.pt[:, half:half + 512],
                          bk.ap(sb_i)[:, :], AF.Exp, bias=b.shiftcol[:, :],
                          scale=scale0, wait=(ts,), inc="AC")
                prev_ae[sb_i] = ae
                bk.r(sb_i, ae)
                if pend is not None:
                    navs1(*pend)
                pend = (v, ae)
            navs1(*pend)
            for qt in range(2):
                bN = 4 if qt == 0 else 5
                bA = 2 if qt == 0 else 3
                pe = p.now("PE")
                r1 = p.do("vector", "reciprocal", b.recip[:, :],
                          bk.ap(bN)[0:1, :], wait=(pe,), inc="DV")
                bk.r(bN, r1)
                t1 = p.do("tensor", "matmul", bk.ap(6)[:, :],
                          b.ones_row_f[:, :], b.recip[:, :], start=True,
                          stop=True, wait=(r1,) + bk.w(6), inc="PE")
                a1 = p.do("scalar", "copy", b.rcp_t[:, :], bk.ap(6)[:, :],
                          wait=(t1,) + ((prev_rcp[0],) if prev_rcp[0] else ()),
                          inc="AC")
                bk.r(6, a1)
                dv = p.do("vector", "tensor_tensor",
                          b.attn[:, qt * 512:(qt + 1) * 512],
                          bk.ap(bA)[:, :], b.rcp_t[:, :], op=ALU.mult,
                          wait=(a1,), inc="DV")
                prev_rcp[0] = dv
                bk.r(bA, dv)

        prev_rcp = [None]
        p.percore(bk, build_a2, boxes=(prev_rcp,))
        ATN = p.now("DV")

        prev_ybf_use = None
        for ot in range(2):
            ytoks = []
            for ec in range(NCH):
                bi = ec % 2
                tpe = p.do("tensor", "matmul", bk.ap(bi)[:, :],
                           b.wos[:, ec * 128:(ec + 1) * 128],
                           b.attn[:, ot * 512:(ot + 1) * 512],
                           start=True, stop=True,
                           wait=(ATN,) + bk.w(bi), inc="PE")
                a = p.do("scalar", "activation",
                         b.ybf[:, ec * 512:(ec + 1) * 512],
                         bk.ap(bi)[:, :], AF.Identity, bias=b.bos[:, ec:ec + 1],
                         scale=1.0,
                         wait=(tpe,) + ((prev_ybf_use,) if (ec == 0 and prev_ybf_use) else ()),
                         inc="AC")
                bk.r(bi, a)
                ytoks.append(a)
            srcs = [(b.ybf[:, c * 512:(c + 1) * 512], ytoks[c])
                    for c in range(NCH)]
            ln.stats(p, bk, srcs, 6, 7)
            a1, a2 = ln.bcast(p, bk, 4, 5)
            t1prev = None
            for c in range(NCH):
                d1 = p.do("vector", "tensor_tensor", b.t1[:, :],
                          b.ybf[:, c * 512:(c + 1) * 512],
                          ln.rs_t[:, :], op=ALU.mult,
                          wait=(a1, a2) + ((t1prev,) if t1prev else ()), inc="DV")
                d2 = p.do("vector", "tensor_tensor", b.t1[:, :], b.t1[:, :],
                          ln.c_t[:, :], op=ALU.add, wait=(d1,), inc="DV")
                xu = p.do("vector", "tensor_tensor",
                          b.x1[:, c, ot * 512:(ot + 1) * 512],
                          b.x1[:, c, ot * 512:(ot + 1) * 512], b.t1[:, :],
                          op=ALU.add, wait=(d2, X1LD), inc="DV")
                t1prev = xu
            prev_ybf_use = p.now("DV")
        ar.pop()

        ar.push()
        b.xn = ar.alloc([128, NCH * 512], BF16)
        b.h = ar.alloc([128, HC0 * 512], BF16)
        b.yb = ar.alloc([128, NCH * 512], BF16)
        b.gsil = ar.alloc([128, 512], BF16)
        b.usc = ar.alloc([128, 512], BF16)
        b.wG = [ar.alloc([128, D], BF16) for _ in range(2)]
        b.wU = [ar.alloc([128, D], BF16) for _ in range(2)]
        b.wD = [ar.alloc([128, HB0], BF16) for _ in range(2)]
        b.sg = ar.alloc([128, HC0], F32)
        b.su = ar.alloc([128, HC0], F32)
        b.sdn = ar.alloc([128, NCH], F32)
        b.corr = ar.alloc([1, 512], BF16)
        b.corr2 = ar.alloc([1, 512], BF16)

        PHB = (p.now("PE"), p.now("DV"), p.now("AC"),
               p.now("XB0"), p.now("XB1"))
        p.load(b.sg[:, :], sg[:, :], sem="SC", wait=PHB)
        p.load(b.su[:, :], su[:, :], sem="SC")
        p.load(b.sdn[:, :], sdn[:, :], sem="SC")
        sctok = p.now("SC")
        sld = [sctok, sctok, sctok]
        wfree = {"G": [None, None], "U": [None, None], "D": [None, None]}

        for ot in range(2):
            PEB = p.now("PE")
            ACB = p.now("AC")
            casts = []
            cprev = None
            for c in range(NCH):
                ct = p.do("vector", "tensor_copy", b.xn[:, c * 512:(c + 1) * 512],
                          b.x1[:, c, ot * 512:(ot + 1) * 512],
                          wait=((PEB, ACB) if c == 0 else ()), inc="DV")
                casts.append(ct)
            srcs = [(b.xn[:, c * 512:(c + 1) * 512], casts[c]) for c in range(NCH)]
            ln.stats(p, bk, srcs, 6, 7)
            a1, a2 = ln.bcast(p, bk, 4, 5)
            xtoks = []
            t1prev = None
            for c in range(NCH):
                d1 = p.do("vector", "tensor_tensor", b.t1[:, :],
                          b.x1[:, c, ot * 512:(ot + 1) * 512], ln.rs_t[:, :],
                          op=ALU.mult,
                          wait=(a1, a2) + ((t1prev,) if t1prev else ()), inc="DV")
                d2 = p.do("vector", "tensor_tensor",
                          b.xn[:, c * 512:(c + 1) * 512], b.t1[:, :],
                          ln.c_t[:, :], op=ALU.add, wait=(d1,), inc="DV")
                t1prev = d2
                xtoks.append(d2)
            w6 = bk.w(6)
            for c in range(NCH):
                p.do("tensor", "matmul", bk.ap(6)[0:1, :], b.ones_col_bf[:, :],
                     b.xn[:, c * 512:(c + 1) * 512], start=(c == 0),
                     stop=(c == NCH - 1), wait=(xtoks[c],) + (w6 if c == 0 else ()),
                     inc="PE" if c == NCH - 1 else None)
            tpe = p.now("PE")
            ccor = p.do("scalar", "activation", b.corr[:, :], bk.ap(6)[0:1, :],
                        AF.Identity, bias=0.0, scale=-128.0, wait=(tpe,), inc="AC")
            bk.r(6, ccor)
            w3 = bk.w(3)
            for hc in range(HC0):
                pb = hc % 2
                wgl = p.load(b.wG[pb][:, :], wg_img[hc, :, :], sem=f"WG{pb}",
                             wait=(wfree["G"][pb],) if wfree["G"][pb] else ())
                wul = p.load(b.wU[pb][:, :], wu_img[hc, :, :], sem=f"WU{pb}",
                             wait=(wfree["U"][pb],) if wfree["U"][pb] else ())
                w0 = bk.w(0)
                for c in range(NCH):
                    p.do("tensor", "matmul", bk.ap(0)[:, :],
                         b.wG[pb][:, c * 128:(c + 1) * 128],
                         b.xn[:, c * 512:(c + 1) * 512], start=(c == 0),
                         stop=False, wait=((wgl,) + w0) if c == 0 else ())
                g_pe = p.do("tensor", "matmul", bk.ap(0)[:, :], b.ones_row_bf[:, :],
                            b.corr[:, :], start=False, stop=True, wait=(ccor,),
                            inc="PE")
                w1 = bk.w(1)
                for c in range(NCH):
                    p.do("tensor", "matmul", bk.ap(1)[:, :],
                         b.wU[pb][:, c * 128:(c + 1) * 128],
                         b.xn[:, c * 512:(c + 1) * 512], start=(c == 0),
                         stop=False, wait=((wul,) + w1) if c == 0 else ())
                u_pe = p.do("tensor", "matmul", bk.ap(1)[:, :], b.ones_row_bf[:, :],
                            b.corr[:, :], start=False, stop=True, inc="PE")
                ga = p.do("scalar", "activation", b.gsil[:, :], bk.ap(0)[:, :],
                          AF.Silu, bias=0.0, scale=b.sg[:, hc:hc + 1],
                          wait=(g_pe, sld[0]) if hc == 0 else (g_pe,), inc="AC")
                bk.r(0, ga)
                wfree["G"][pb] = ga
                ua = p.do("scalar", "activation", b.usc[:, :], bk.ap(1)[:, :],
                          AF.Identity, bias=0.0, scale=b.su[:, hc:hc + 1],
                          wait=(u_pe, sld[1]) if hc == 0 else (u_pe,), inc="AC")
                bk.r(1, ua)
                wfree["U"][pb] = ua
                hd = p.do("vector", "tensor_tensor",
                          b.h[:, hc * 512:(hc + 1) * 512], b.gsil[:, :],
                          b.usc[:, :], op=ALU.mult, wait=(ga, ua), inc="DV")
                p.do("tensor", "matmul", bk.ap(3)[0:1, :], b.ones_col_bf[:, :],
                     b.h[:, hc * 512:(hc + 1) * 512], start=(hc == 0),
                     stop=(hc == HC0 - 1), wait=(hd,) + (w3 if hc == 0 else ()),
                     inc="PE" if hc == HC0 - 1 else None)
            tpe = p.now("PE")
            ccor2 = p.do("scalar", "activation", b.corr2[:, :], bk.ap(3)[0:1, :],
                         AF.Identity, bias=0.0, scale=-128.0, wait=(tpe,), inc="AC")
            bk.r(3, ccor2)
            ytoks = []
            for ec in range(NCH):
                pb = ec % 2
                wdl = p.load(b.wD[pb][:, :], wd_img[ec, :, :], sem=f"WD{pb}",
                             wait=(wfree["D"][pb],) if wfree["D"][pb] else ())
                w2 = bk.w(2)
                for hc in range(HC0):
                    p.do("tensor", "matmul", bk.ap(2)[:, :],
                         b.wD[pb][:, hc * 128:(hc + 1) * 128],
                         b.h[:, hc * 512:(hc + 1) * 512], start=(hc == 0),
                         stop=False, wait=((wdl,) + w2) if hc == 0 else ())
                d_pe = p.do("tensor", "matmul", bk.ap(2)[:, :], b.ones_row_bf[:, :],
                            b.corr2[:, :], start=False, stop=True, wait=(ccor2,),
                            inc="PE")
                ya = p.do("scalar", "activation",
                          b.yb[:, ec * 512:(ec + 1) * 512], bk.ap(2)[:, :],
                          AF.Identity, bias=0.0, scale=b.sdn[:, ec:ec + 1],
                          wait=(d_pe, sld[2]) if ec == 0 else (d_pe,), inc="AC")
                bk.r(2, ya)
                wfree["D"][pb] = ya
                ytoks.append(ya)
            srcs = [(b.yb[:, c * 512:(c + 1) * 512], ytoks[c]) for c in range(NCH)]
            ln.stats(p, bk, srcs, 6, 7)
            a1, a2 = ln.bcast(p, bk, 4, 5)
            t1prev = None
            for c in range(NCH):
                d1 = p.do("vector", "tensor_tensor", b.t1[:, :],
                          b.yb[:, c * 512:(c + 1) * 512], ln.rs_t[:, :],
                          op=ALU.mult, wait=(a1, a2) + ((t1prev,) if t1prev else ()),
                          inc="DV")
                d2 = p.do("vector", "tensor_tensor", b.t1[:, :], b.t1[:, :],
                          ln.c_t[:, :], op=ALU.add, wait=(d1,), inc="DV")
                xu = p.do("vector", "tensor_tensor",
                          b.x1[:, c, ot * 512:(ot + 1) * 512],
                          b.x1[:, c, ot * 512:(ot + 1) * 512], b.t1[:, :],
                          op=ALU.add, wait=(d2,), inc="DV")
                t1prev = xu
                p.load(x_out[c, :, ot * 512:(ot + 1) * 512],
                       b.x1[:, c, ot * 512:(ot + 1) * 512], eng="gpsimd",
                       sem="ST", wait=(xu,))
        ar.pop()

        ar.push()
        b.xn1 = ar.alloc([128, NCH * 512], BF16)
        b.wk1s = ar.alloc([128, NCH * 512], BF16)
        b.wv1s = ar.alloc([128, NCH * 512], BF16)
        b.kv1 = [ar.alloc([128, 512], BF16) for _ in range(2)]
        PHC = (p.now("PE"), p.now("DV"), p.now("AC"),
               p.now("WG0"), p.now("WG1"), p.now("WU0"), p.now("WU1"),
               p.now("WD0"), p.now("WD1"), p.now("SC"))
        wl1 = p.load(b.wk1s[:, :], wk1[:, :], sem="WK1", wait=PHC)
        wl2 = p.load(b.wv1s[:, :], wv1[:, :], sem="WK1")
        wl1 = wl2
        kvst = [None, None]
        for ot in range(2):
            PEB = p.now("PE")
            ACB = p.now("AC")
            STB = p.now("ST")
            casts = []
            for c in range(NCH):
                ct = p.do("vector", "tensor_copy",
                          b.xn1[:, c * 512:(c + 1) * 512],
                          b.x1[:, c, ot * 512:(ot + 1) * 512],
                          wait=((PEB, ACB, STB) if c == 0 else ()), inc="DV")
                casts.append(ct)
            srcs = [(b.xn1[:, c * 512:(c + 1) * 512], casts[c]) for c in range(NCH)]
            ln.stats(p, bk, srcs, 6, 7)
            a1, a2 = ln.bcast(p, bk, 4, 5)
            xtoks = []
            t1prev = None
            for c in range(NCH):
                d1 = p.do("vector", "tensor_tensor", b.t1[:, :],
                          b.x1[:, c, ot * 512:(ot + 1) * 512], ln.rs_t[:, :],
                          op=ALU.mult,
                          wait=(a1, a2) + ((t1prev,) if t1prev else ()), inc="DV")
                d2 = p.do("vector", "tensor_tensor",
                          b.xn1[:, c * 512:(c + 1) * 512], b.t1[:, :],
                          ln.c_t[:, :], op=ALU.add, wait=(d1,), inc="DV")
                t1prev = d2
                xtoks.append(d2)
                p.load(xn1_out[c, :, ot * 512:(ot + 1) * 512],
                       b.xn1[:, c * 512:(c + 1) * 512], eng="gpsimd", sem="ST",
                       wait=(d2,))
            for g in range(4):
                pb = g % 2
                w0 = bk.w(0 + pb)
                for c in range(NCH):
                    p.do("tensor", "matmul", bk.ap(0 + pb)[:, :],
                         b.wk1s[:, c * 512 + g * 128: c * 512 + (g + 1) * 128],
                         b.xn1[:, c * 512:(c + 1) * 512], start=(c == 0),
                         stop=(c == NCH - 1),
                         wait=((wl1, xtoks[c]) + w0) if c == 0 else (xtoks[c],),
                         inc="PE" if c == NCH - 1 else None)
                tpe = p.now("PE")
                ka = p.do("vector", "tensor_copy", b.kv1[pb][:, :],
                          bk.ap(0 + pb)[:, :],
                          wait=(tpe,) + ((kvst[pb],) if kvst[pb] else ()), inc="DV")
                bk.r(0 + pb, ka)
                kvst[pb] = p.load(k1_out[g, :, ot * 512:(ot + 1) * 512],
                                  b.kv1[pb][:, :], eng="gpsimd", sem=f"KV{pb}",
                                  wait=(ka,))
            for j in range(4):
                pb = j % 2
                w2 = bk.w(2 + pb)
                for c in range(NCH):
                    p.do("tensor", "matmul", bk.ap(2 + pb)[:, :],
                         b.xn1[:, c * 512 + j * 128: c * 512 + (j + 1) * 128],
                         b.wv1s[:, c * 512:(c + 1) * 512], start=(c == 0),
                         stop=(c == NCH - 1),
                         wait=((wl2,) + w2) if c == 0 else (),
                         inc="PE" if c == NCH - 1 else None)
                tpe = p.now("PE")
                va = p.do("vector", "tensor_copy", b.kv1[pb][:, :],
                          bk.ap(2 + pb)[:, :],
                          wait=(tpe,) + ((kvst[pb],) if kvst[pb] else ()), inc="DV")
                bk.r(2 + pb, va)
                kvst[pb] = p.load(
                    v1_out[ot * 512 + j * 128: ot * 512 + (j + 1) * 128, :],
                    b.kv1[pb][:, :], eng="gpsimd", sem=f"KV{pb}", wait=(va,))
        ar.pop()

        p.do("gpsimd", "nop", wait=(p.now("ST"), p.now("KV0"), p.now("KV1")))
        p.emit()
    return nc


def _img_dmaj(wT):
    wT = np.asarray(wT)
    out = wT.shape[1]
    return np.ascontiguousarray(
        wT.reshape(NCH, 128, out).transpose(1, 0, 2)).reshape(128, NCH * out)


def prep_launch1(I):
    x0 = np.ascontiguousarray(I["L0_attn_x"].reshape(T, D)).astype(np.float32, copy=False)
    xT = np.ascontiguousarray(x0.T).reshape(NCH, 128, T)
    com = dict(
        xT=xT,
        wq=_img_dmaj(I["L0_attn_Wq"]),
        wk=_img_dmaj(I["L0_attn_Wk"]),
        wv=_img_dmaj(I["L0_attn_Wv"]),
        wo=np.ascontiguousarray(I["L0_attn_Wo"]),
        bq=np.ascontiguousarray(I["L0_attn_bq"]).reshape(128, 1),
        bk=np.ascontiguousarray(I["L0_attn_bk"]).reshape(128, 1),
        bv=np.ascontiguousarray(I["L0_attn_bv"]).reshape(1, 128),
        bo=np.ascontiguousarray(I["L0_attn_bo"].reshape(NCH, 128).T),
        sg=np.ascontiguousarray(I["L0_ffn_s_gate"].reshape(HC0, 128).T),
        su=np.ascontiguousarray(I["L0_ffn_s_up"].reshape(HC0, 128).T),
        sdn=np.ascontiguousarray(I["L0_ffn_s_down"].reshape(NCH, 128).T),
    )

    def img_gu(wq_):
        u8 = wq_.astype(np.uint8)
        wTl = u8.T
        return np.ascontiguousarray(
            wTl.reshape(NCH, 128, HC0, 128).transpose(2, 1, 0, 3)).reshape(HC0, 128, D)

    com["wg_img"] = img_gu(I["L0_ffn_wq_gate"])
    com["wu_img"] = img_gu(I["L0_ffn_wq_up"])
    d8 = I["L0_ffn_wq_down"].astype(np.uint8)
    com["wd_img"] = np.ascontiguousarray(
        d8.T.reshape(HC0, 128, NCH, 128).transpose(2, 1, 0, 3)).reshape(NCH, 128, HB0)

    sel = np.concatenate([np.arange(512 * g, 512 * g + 128) for g in range(4)])
    com["wk1"] = _img_dmaj(I["L1_attn_Wk"][sel].T)
    com["wv1"] = _img_dmaj(I["L1_attn_Wv"][sel].T)

    tri = np.zeros((4, 128, 512), np.float32)
    for j in range(4):
        kk = np.arange(128)[:, None] + j * 128
        qq = np.arange(512)[None, :]
        tri[j] = np.where(kk <= qq, 0.0, KILL / (1.0 / np.sqrt(128.0)))
    tri2 = np.ascontiguousarray(tri.transpose(1, 0, 2)).reshape(128, 4 * 512)

    in_maps = []
    for c in range(NC):
        a, bb = zz_chunks(c)
        own = np.concatenate([x0[a * 512:(a + 1) * 512],
                              x0[bb * 512:(bb + 1) * 512]], 0)
        xo = np.ascontiguousarray(own.T).reshape(NCH, 128, TC)
        kb = np.full((128,), KILL, np.float32)
        for qt, g in enumerate((a, bb)):
            nfull = 4 * g
            for kc in range(NV_L0[qt]):
                kb[qt * 64 + kc] = SHIFT if kc < nfull else KILL
        kbias_arr = np.ascontiguousarray(
            np.broadcast_to(kb[None, :], (128, 128)).copy())
        im = dict(com)
        im["xown"] = xo
        im["kbias"] = kbias_arr
        im["tri"] = tri2
        in_maps.append(im)
    return in_maps


def kernel(**inputs):
    raise NotImplementedError("assembly in progress")



HB1 = 8192
HC1 = 64
NHEAD = 16


def build_launch2(upto=9):
    nc = bass.Bass("TRN2", target_bir_lowering=False, debug=False)
    stack = ExitStack()
    nc._stk = stack

    xin = nc.dram_tensor("xin", [NCH, 128, TC], F32, kind="ExternalInput").ap()
    xn1 = nc.dram_tensor("xn1", [NCH, 128, TC], BF16, kind="ExternalInput").ap()
    k1all = nc.dram_tensor("k1all", [4, 128, T], BF16, kind="ExternalInput").ap()
    v1img = nc.dram_tensor("v1img", [4, 128, T], BF16, kind="ExternalInput").ap()
    k1own = nc.dram_tensor("k1own", [4, 128, TC], BF16, kind="ExternalInput").ap()
    v1ownimg = nc.dram_tensor("v1ownimg", [4, 128, TC], BF16, kind="ExternalInput").ap()
    wq1 = nc.dram_tensor("wq1", [NHEAD, 128, D], F32, kind="ExternalInput").ap()
    wp1 = nc.dram_tensor("wp1", [NCH, 128, D], F32, kind="ExternalInput").ap()
    bp1 = nc.dram_tensor("bp1", [128, NCH], F32, kind="ExternalInput").ap()
    kbias = nc.dram_tensor("kbias", [128, 128], F32, kind="ExternalInput").ap()
    tri = nc.dram_tensor("tri", [128, 4 * 512], F32, kind="ExternalInput").ap()
    wup = nc.dram_tensor("wup", [HC1, 128, D], F32, kind="ExternalInput").ap()
    wgt = nc.dram_tensor("wgt", [HC1, 128, D], F32, kind="ExternalInput").ap()
    wdn1 = nc.dram_tensor("wdn1", [NCH, 128, HB1], F32, kind="ExternalInput").ap()
    su_ug = nc.dram_tensor("su_ug", [128, NCH], F32, kind="ExternalInput").ap()
    su_dp = nc.dram_tensor("su_dp", [128, HC1], F32, kind="ExternalInput").ap()
    wsc_up = nc.dram_tensor("wsc_up", [128, HC1], F32, kind="ExternalInput").ap()
    wsc_gt = nc.dram_tensor("wsc_gt", [128, HC1], F32, kind="ExternalInput").ap()
    wsc_dp = nc.dram_tensor("wsc_dp", [128, NCH], F32, kind="ExternalInput").ap()
    h128 = nc.dram_tensor("h128", [128, 128], BF16, kind="ExternalInput").ap()
    wq2 = nc.dram_tensor("wq2", [NHEAD, 128, D], F32, kind="ExternalInput").ap()
    wk2 = nc.dram_tensor("wk2", [NHEAD, 128, D], F32, kind="ExternalInput").ap()
    wv2 = nc.dram_tensor("wv2", [NHEAD, 128, D], F32, kind="ExternalInput").ap()
    cosT = nc.dram_tensor("cosT", [128, TC], F32, kind="ExternalInput").ap()
    sinT = nc.dram_tensor("sinT", [128, TC], F32, kind="ExternalInput").ap()
    qdec_t = nc.dram_tensor("qdec_t", [128, NHEAD * 128], F32, kind="ExternalInput").ap()
    kdec_c = nc.dram_tensor("kdec_c", [128, NHEAD], F32, kind="ExternalInput").ap()
    dmask = nc.dram_tensor("dmask", [128, NHEAD * 128], F32, kind="ExternalInput").ap()

    x_out = nc.dram_tensor("x_out", [NCH, 128, TC], F32, kind="ExternalOutput").ap()
    a2p_out = nc.dram_tensor("a2p_out", [NHEAD, 128, TC], BF16, kind="ExternalOutput").ap()
    qd_out = nc.dram_tensor("qd_out", [NHEAD, 128, TC], BF16, kind="ExternalOutput").ap()
    sum_out = nc.dram_tensor("sum_out", [2, NHEAD, 128, 128], BF16, kind="ExternalOutput").ap()

    RC = _l2_rc()

    with stack:
        p = Prog(nc, stack)
        ar = Arena(nc, stack, "arena", 206 * 1024)
        bk = Banks(nc, stack)
        b = B()

        b.ones_col_bf = ar.alloc([128, 1], BF16)
        b.epscol = ar.alloc([128, 1], F32)
        b.shiftcol = ar.alloc([128, 1], F32)
        b.ones_row_f = ar.alloc([1, 128], F32)
        b.ones_row_bf = ar.alloc([1, 128], BF16)
        b.sq = ar.alloc([128, 512], BF16)
        b.idf = ar.alloc([128, 128], F32)
        b.t1 = ar.alloc([128, 512], F32)
        ln = LNH(p, b, ar)

        b.idb = ar.alloc([128, 128], BF16)
        ident_dram = nc.inline_tensor(np.eye(128, dtype=np.float32), "idf_c").ap()
        identb_dram = nc.inline_tensor(np.eye(128, dtype=BF), "idb_c").ap()
        p.load(b.idf[:, :], ident_dram[:, :], sem="LC")
        p.load(b.idb[:, :], identb_dram[:, :], sem="LC")
        p.do("gpsimd", "memset", b.ones_col_bf[:, :], 1.0, inc="GP")
        p.do("gpsimd", "memset", b.ones_row_f[:, :], 1.0, inc="GP")
        p.do("gpsimd", "memset", b.ones_row_bf[:, :], 1.0, inc="GP")
        p.do("gpsimd", "memset", b.epscol[:, :], 1e-5, inc="GP")
        p.do("gpsimd", "memset", b.shiftcol[:, :], SHIFT, inc="GP")
        CLD = p.now("LC")
        CGP = p.now("GP")

        scale0 = 1.0 / np.sqrt(128.0)

        ar.push()
        b.x1 = ar.alloc([128, NCH, TC], F32)
        b.Qf = ar.alloc([128, NHEAD * TC], BF16)
        b.attn = ar.alloc([128, NHEAD * TC], BF16)
        b.recip = ar.alloc([1, 512], F32)
        b.rcp_t = ar.alloc([128, 512], F32)
        b.kbias = ar.alloc([128, 128], F32)
        b.tri = ar.alloc([128, 4 * 512], F32)
        p.load(b.kbias[:, :], kbias[:, :], sem="LC2")
        p.load(b.tri[:, :], tri[:, :], sem="LC2")
        KB2 = p.now("LC2")
        for ch in range(NCH):
            p.load(b.x1[:, ch, :], xin[ch, :, :], eng="sync", sem="X1")
        X1LD = p.now("X1")

        ar.push()
        b.xn1s = ar.alloc([128, NCH * TC], BF16)
        b.wb = [ar.alloc([128, D], BF16) for _ in range(2)]
        for ch in range(NCH):
            p.load(b.xn1s[:, ch * TC:(ch + 1) * TC], xn1[ch, :, :], sem="XN")
        XNL = p.now("XN")
        wfree = [None, None]
        for e in range(NHEAD):
            pb = e % 2
            wl = p.load(b.wb[pb][:, :], wq1[e, :, :], sem=f"WA{pb}",
                        wait=(wfree[pb],) if wfree[pb] else ())
            for ot in range(2):
                ww = bk.w(pb)
                for c in range(NCH):
                    p.do("tensor", "matmul", bk.ap(pb)[:, :],
                         b.wb[pb][:, c * 128:(c + 1) * 128],
                         b.xn1s[:, c * TC + ot * 512: c * TC + (ot + 1) * 512],
                         start=(c == 0), stop=(c == NCH - 1),
                         wait=((wl, XNL, CGP) + ww) if c == 0 else (),
                         inc="PE" if c == NCH - 1 else None)
                tpe = p.now("PE")
                a = p.do("scalar", "copy",
                         b.Qf[:, e * TC + ot * 512: e * TC + (ot + 1) * 512],
                         bk.ap(pb)[:, :], wait=(tpe,), inc="AC")
                bk.r(pb, a)
                wfree[pb] = a
        ar.pop()

        if upto < 1:
            ar.pop()
            p.do("gpsimd", "nop", wait=(p.now("ST"),))
            p.emit()
            return nc
        ar.push()
        b.Kg = ar.alloc([128, T], BF16)
        b.Vg = ar.alloc([128, T], BF16)
        b.Kow = ar.alloc([128, TC], BF16)
        b.Vow = ar.alloc([128, TC], BF16)
        b.pt2 = ar.alloc([128, 2048], BF16)
        kvfree = None
        prev_rcp = [None]
        PH1 = (p.now("PE"), p.now("AC"))
        for g in range(4):
            lk = p.load(b.Kg[:, :], k1all[g, :, :], sem="KG",
                        wait=((kvfree,) if kvfree else ()) + (PH1 if g == 0 else ()))
            lv = p.load(b.Vg[:, :], v1img[g, :, :], sem="KG")
            lko = p.load(b.Kow[:, :], k1own[g, :, :], sem="KG")
            lvo = p.load(b.Vow[:, :], v1ownimg[g, :, :], sem="KG")
            KGL = p.now("KG")
            def build_g(core):
                a_c, b_c = zz_chunks(core)
                nonlocal_box = None
                for sct in range(4):
                    e = 4 * g + sct
                    vis = ([(0, "main", kc) for kc in range(4 * a_c)]
                           + [(0, "diag", j) for j in range(4)]
                           + [(1, "main", kc) for kc in range(4 * b_c)]
                           + [(1, "diag", j) for j in range(4)])
                    assert len(vis) == 68
                    nv0 = 4 * a_c + 4
                    pairs = [(vis[i], vis[i + 1]) for i in range(0, 68, 2)]
                    prev_ae = [None, None]
                    wN0 = bk.w(5)
                    wA0 = bk.w(4)
                    wN1 = bk.w(7)
                    wA1 = bk.w(6)
                    vi_box = [0]

                    def navs(pr, ae, ppi):
                        half = (ppi % 2) * 1024
                        for sl, v in enumerate(pr):
                            qt, kind, kc = v
                            vi = vi_box[0]
                            if kind == "main":
                                vap = b.Vg[:, kc * 128:(kc + 1) * 128]
                            else:
                                vap = b.Vow[:, qt * 512 + kc * 128:
                                            qt * 512 + (kc + 1) * 128]
                            first = (vi == 0) if qt == 0 else (vi == nv0)
                            last = (vi == nv0 - 1) if qt == 0 else (vi == 67)
                            bN = 5 if qt == 0 else 7
                            bA = 4 if qt == 0 else 6
                            wN_ = (wN0 if qt == 0 else wN1) if first else ()
                            wA_ = (wA0 if qt == 0 else wA1) if first else ()
                            p.do("tensor", "matmul", bk.ap(bN)[0:1, :],
                                 b.ones_col_bf[:, :],
                                 b.pt2[:, half + sl * 512:half + (sl + 1) * 512],
                                 wait=(ae,) + wN_, start=first, stop=last)
                            p.do("tensor", "matmul", bk.ap(bA)[:, :], vap,
                                 b.pt2[:, half + sl * 512:half + (sl + 1) * 512],
                                 wait=wA_, start=first, stop=last,
                                 inc="PE" if vi == 67 else None)
                            vi_box[0] = vi + 1

                    pend = None
                    for pi, pr in enumerate(pairs):
                        sb0 = (pi % 2) * 2
                        half = (pi % 2) * 1024
                        ts_toks = []
                        for sl, v in enumerate(pr):
                            qt, kind, kc = v
                            if kind == "main":
                                kap = b.Kg[:, kc * 128:(kc + 1) * 128]
                            else:
                                kap = b.Kow[:, qt * 512 + kc * 128:
                                            qt * 512 + (kc + 1) * 128]
                            ts = p.do("tensor", "matmul", bk.ap(sb0 + sl)[:, :],
                                      kap,
                                      b.Qf[:, e * TC + qt * 512:
                                           e * TC + (qt + 1) * 512],
                                      start=True, stop=True,
                                      wait=((prev_ae[pi % 2],)
                                            if prev_ae[pi % 2] else ())
                                      + bk.w(sb0 + sl)
                                      + ((KGL, CGP, CLD, KB2)
                                         if (pi == 0 and sct == 0) else ()),
                                      inc="PE")
                            if kind == "diag":
                                ts = p.do("vector", "tensor_tensor",
                                          bk.ap(sb0 + sl)[:, :],
                                          bk.ap(sb0 + sl)[:, :],
                                          b.tri[:, kc * 512:(kc + 1) * 512],
                                          op=ALU.add, wait=(ts,), inc="DV")
                            ts_toks.append(ts)
                        ae = None
                        for sl in range(2):
                            ae = p.do("scalar", "activation",
                                      b.pt2[:, half + sl * 512:
                                            half + (sl + 1) * 512],
                                      bk.ap(sb0 + sl)[:, :], AF.Exp,
                                      bias=b.shiftcol[:, :], scale=scale0,
                                      wait=(ts_toks[sl],) + ((ae,) if ae else ()),
                                      inc="AC")
                        prev_ae[pi % 2] = ae
                        bk.r(sb0, ae)
                        bk.r(sb0 + 1, ae)
                        if pend is not None:
                            navs(*pend)
                        pend = (pr, ae, pi)
                    navs(*pend)
                    for qt in (0, 1):
                        bN = 5 if qt == 0 else 7
                        bA = 4 if qt == 0 else 6
                        bT = 0 if qt == 0 else 1
                        pe = p.now("PE")
                        r1 = p.do("vector", "reciprocal", b.recip[:, :],
                                  bk.ap(bN)[0:1, :], wait=(pe,), inc="DV")
                        bk.r(bN, r1)
                        t1 = p.do("tensor", "matmul", bk.ap(bT)[:, :],
                                  b.ones_row_f[:, :], b.recip[:, :],
                                  start=True, stop=True,
                                  wait=(r1,) + bk.w(bT), inc="PE")
                        a1 = p.do("scalar", "copy", b.rcp_t[:, :],
                                  bk.ap(bT)[:, :],
                                  wait=(t1,) + ((prev_rcp[0],)
                                                if prev_rcp[0] else ()),
                                  inc="AC")
                        bk.r(bT, a1)
                        dv = p.do("vector", "tensor_tensor",
                                  b.attn[:, e * TC + qt * 512:
                                         e * TC + (qt + 1) * 512],
                                  bk.ap(bA)[:, :], b.rcp_t[:, :], op=ALU.mult,
                                  wait=(a1,), inc="DV")
                        prev_rcp[0] = dv
                        bk.r(bA, dv)

            p.percore(bk, build_g, boxes=(prev_rcp,))
            kvfree = p.now("DV")
        ar.pop()

        if upto < 2:
            ar.pop()
            p.do("gpsimd", "nop", wait=(p.now("ST"),))
            p.emit()
            return nc
        ar.push()
        b.wpb = [ar.alloc([128, D], BF16) for _ in range(2)]
        b.yb = ar.alloc([128, NCH * TC], BF16)
        b.bp1 = ar.alloc([128, NCH], F32)
        p.load(b.bp1[:, :], bp1[:, :], sem="BP")
        BPL = p.now("BP")
        wfree = [None, None]
        ATN = p.now("DV")
        PH2 = (p.now("PE"), p.now("AC"), ATN)
        for ec in range(NCH):
            pb = ec % 2
            wl = p.load(b.wpb[pb][:, :], wp1[ec, :, :], sem=f"WB{pb}",
                        wait=((wfree[pb],) if wfree[pb] else ()) + (PH2 if ec < 2 else ()))
            for ot in range(2):
                ww = bk.w(pb)
                for hc in range(NCH):
                    p.do("tensor", "matmul", bk.ap(pb)[:, :],
                         b.wpb[pb][:, hc * 128:(hc + 1) * 128],
                         b.attn[:, hc * TC + ot * 512: hc * TC + (ot + 1) * 512],
                         start=(hc == 0), stop=(hc == NCH - 1),
                         wait=((wl, ATN, BPL) + ww) if hc == 0 else (),
                         inc="PE" if hc == NCH - 1 else None)
                tpe = p.now("PE")
                a = p.do("scalar", "activation",
                         b.yb[:, ec * TC + ot * 512: ec * TC + (ot + 1) * 512],
                         bk.ap(pb)[:, :], AF.Identity, bias=b.bp1[:, ec:ec + 1],
                         scale=1.0, wait=(tpe,), inc="AC")
                bk.r(pb, a)
                wfree[pb] = a
        YT2 = p.now("AC")
        for ot in range(2):
            srcs = [(b.yb[:, c * TC + ot * 512: c * TC + (ot + 1) * 512],
                     YT2 if c == 0 else None) for c in range(NCH)]
            ln.stats(p, bk, srcs, 6, 7)
            a1, a2 = ln.bcast(p, bk, 4, 5)
            t1prev = None
            for c in range(NCH):
                d1 = p.do("vector", "tensor_tensor", b.t1[:, :],
                          b.yb[:, c * TC + ot * 512: c * TC + (ot + 1) * 512],
                          ln.rs_t[:, :], op=ALU.mult,
                          wait=(a1, a2) + ((t1prev,) if t1prev else ()), inc="DV")
                d2 = p.do("vector", "tensor_tensor", b.t1[:, :], b.t1[:, :],
                          ln.c_t[:, :], op=ALU.add, wait=(d1,), inc="DV")
                xu = p.do("vector", "tensor_tensor",
                          b.x1[:, c, ot * 512:(ot + 1) * 512],
                          b.x1[:, c, ot * 512:(ot + 1) * 512], b.t1[:, :],
                          op=ALU.add, wait=(d2, X1LD), inc="DV")
                t1prev = xu
                p.load(x_out[c, :, ot * 512:(ot + 1) * 512],
                       b.x1[:, c, ot * 512:(ot + 1) * 512], eng="gpsimd",
                       sem="ST", wait=(xu,))
        ar.pop()
        ar.pop()

        if upto < 3:
            p.do("gpsimd", "nop", wait=(p.now("ST"), p.now("SX0"), p.now("SX1")))
            p.emit()
            return nc
        ar.push()
        b.xn = ar.alloc([128, NCH * 512], BF16)
        b.xh2 = ar.alloc([128, NCH * 512], BF16)
        b.h = ar.alloc([128, HC1 * 512], BF16)
        b.yb = b.xh2
        b.gsil = ar.alloc([128, 512], BF16)
        b.usc = ar.alloc([128, 512], BF16)
        b.wG = [ar.alloc([128, D], BF16) for _ in range(2)]
        b.wU = [ar.alloc([128, D], BF16) for _ in range(2)]
        b.wD = [ar.alloc([128, HB1], BF16) for _ in range(2)]
        b.Hs1 = ar.alloc([128, NCH * 128], BF16)
        b.Hs2 = ar.alloc([128, HC1 * 128], BF16)
        b.Hs2n = ar.alloc([128, HC1 * 128], BF16)
        b.h128 = ar.alloc([128, 128], BF16)
        b.suc = ar.alloc([128, NCH], F32)
        b.sdc = ar.alloc([128, HC1], F32)
        b.scup = ar.alloc([128, HC1], F32)
        b.scgt = ar.alloc([128, HC1], F32)
        b.scdp = ar.alloc([128, NCH], F32)
        b.xf = [ar.alloc([128, 512], F32) for _ in range(2)]

        PH3 = (p.now("PE"), p.now("DV"), p.now("AC"), p.now("ST"),
               p.now("KG"), p.now("WB0"), p.now("WB1"))
        p.load(b.h128[:, :], h128[:, :], sem="SC", wait=PH3)
        p.load(b.suc[:, :], su_ug[:, :], sem="SC")
        p.load(b.sdc[:, :], su_dp[:, :], sem="SC")
        p.load(b.scup[:, :], wsc_up[:, :], sem="SC")
        p.load(b.scgt[:, :], wsc_gt[:, :], sem="SC")
        p.load(b.scdp[:, :], wsc_dp[:, :], sem="SC")
        SCL = p.now("SC")
        s1 = p.do("vector", "tensor_scalar", b.scup[:, :], b.scup[:, :],
                  1.0 / np.sqrt(2048.0), None, op0=ALU.mult, wait=(SCL,), inc="DV")
        s2 = p.do("vector", "tensor_scalar", b.scgt[:, :], b.scgt[:, :],
                  1.0 / np.sqrt(2048.0), None, op0=ALU.mult, wait=(SCL,), inc="DV")
        s3 = p.do("vector", "tensor_scalar", b.scdp[:, :], b.scdp[:, :],
                  1.0 / np.sqrt(8192.0), None, op0=ALU.mult, wait=(SCL,), inc="DV")
        hs_toks = []
        for c in range(NCH):
            tkk = p.do("vector", "tensor_scalar", b.Hs1[:, c * 128:(c + 1) * 128],
                       b.h128[:, :], b.suc[:, c:c + 1], None, op0=ALU.mult,
                       wait=(SCL,), inc="DV")
            hs_toks.append(tkk)
        for c in range(HC1):
            tkk = p.do("vector", "tensor_scalar", b.Hs2[:, c * 128:(c + 1) * 128],
                       b.h128[:, :], b.sdc[:, c:c + 1], None, op0=ALU.mult,
                       wait=(SCL,), inc="DV")
            hs_toks.append(tkk)
            tkk = p.do("vector", "tensor_scalar",
                       b.Hs2n[:, c * 128:(c + 1) * 128],
                       b.h128[:, :], b.sdc[:, c:c + 1], -1.0, op0=ALU.mult,
                       op1=ALU.mult, wait=(SCL,), inc="DV")
            hs_toks.append(tkk)
        HST = p.now("DV")

        wfree = {"G": [None, None], "U": [None, None], "D": [None, None]}
        bfD = [None]
        bfG = [None]
        for ot in range(2):
            PEB = p.now("PE"); ACB = p.now("AC"); STB = p.now("ST")
            SXB = (p.now("SX0"), p.now("SX1"))
            xls = []
            for c in range(NCH):
                xl = p.load(b.xn[:, c * 512:(c + 1) * 512],
                            x_out[c, :, ot * 512:(ot + 1) * 512],
                            sem=f"XF{c % 2}",
                            wait=(xls[c - 2],) if c >= 2 else (PEB, ACB, STB) + SXB)
                xls.append(xl)
            srcs = [(b.xn[:, c * 512:(c + 1) * 512], xls[c]) for c in range(NCH)]
            ln.stats(p, bk, srcs, 6, 7)
            a1, a2 = ln.bcast(p, bk, 4, 5)
            xtoks = []
            t1prev = None
            for c in range(NCH):
                xl = p.load(b.xf[c % 2][:, :], x_out[c, :, ot * 512:(ot + 1) * 512],
                            sem=f"XF{c % 2}",
                            wait=(xtoks[c - 2],) if c >= 2 else (a1, a2))
                d1 = p.do("vector", "tensor_tensor", b.t1[:, :], b.xf[c % 2][:, :],
                          ln.rs_t[:, :], op=ALU.mult,
                          wait=(xl, a1, a2) + ((t1prev,) if t1prev else ()), inc="DV")
                d2 = p.do("vector", "tensor_tensor",
                          b.xn[:, c * 512:(c + 1) * 512], b.t1[:, :],
                          ln.c_t[:, :], op=ALU.add, wait=(d1,), inc="DV")
                t1prev = d2
                xtoks.append(d2)
            sat1 = [None] * NCH
            for c in range(NCH):
                bi = c % 4
                tpe = p.do("tensor", "matmul", bk.ap(bi)[:, :],
                           b.Hs1[:, c * 128:(c + 1) * 128],
                           b.xn[:, c * 512:(c + 1) * 512], start=True, stop=True,
                           wait=(xtoks[c], HST) + bk.w(bi), inc="PE")
                a = p.do("scalar", "copy", b.xh2[:, c * 512:(c + 1) * 512],
                         bk.ap(bi)[:, :], wait=(tpe,), inc="AC")
                bk.r(bi, a)
                sat1[c] = a
            dprev = None
            for hh in (1, 2, 4, 8):
                W = hh * 512
                for q in range(NCH // (2 * hh)):
                    base = q * 2 * W
                    a0, b0 = base, base + W
                    wv = ((sat1[2 * q * hh + 2 * hh - 1],) if hh == 1 else ()) + \
                        ((dprev,) if dprev else ())
                    if hh == 1:
                        wv = (sat1[2 * q + 1],) + ((dprev,) if dprev else ())
                    e1 = p.do("vector", "tensor_tensor",
                              b.xh2[:, a0:a0 + W], b.xh2[:, a0:a0 + W],
                              b.xh2[:, b0:b0 + W], op=ALU.add,
                              wait=wv, inc="DV")
                    dprev = p.do("vector", "scalar_tensor_tensor",
                                 b.xh2[:, b0:b0 + W],
                                 b.xh2[:, b0:b0 + W], -2.0,
                                 b.xh2[:, a0:a0 + W],
                                 op0=ALU.mult, op1=ALU.add, wait=(e1,), inc="DV")
            XH = b.xh2
            XHT = p.now("DV")
            XHTG = p.now("GP")
            satok = [None] * HC1
            bft = {}

            def emit_bf(hh, q):
                W = hh * 512
                wDc = W
                wGc = 0
                base = q * 2 * W
                a0, b0 = base, base + W
                if hh == 4:
                    wD_ = (satok[8 * q + 7],)
                    wG_ = ()
                else:
                    s0 = bft[(hh // 2, 2 * q)]
                    s1 = bft[(hh // 2, 2 * q + 1)]
                    wD_ = tuple(t for t in (s0[1], s1[1]) if t is not None)
                    wG_ = (s0[0], s1[0])
                if bfD[0] is not None:
                    wD_ = wD_ + (bfD[0],)
                e1 = p.do("vector", "tensor_tensor", b.h[:, a0:a0 + wDc],
                          b.h[:, a0:a0 + wDc], b.h[:, b0:b0 + wDc],
                          op=ALU.add, wait=wD_, inc="DV")
                dT = p.do("vector", "scalar_tensor_tensor", b.h[:, b0:b0 + wDc],
                          b.h[:, b0:b0 + wDc], -2.0, b.h[:, a0:a0 + wDc],
                          op0=ALU.mult, op1=ALU.add, wait=(e1,), inc="DV")
                gT = None
                if wGc:
                    if bfG[0] is not None:
                        wG_ = wG_ + (bfG[0],)
                    f0 = p.do("gpsimd", "tensor_copy", b.tmpG[:, 0:wGc],
                              b.h[:, a0 + wDc:a0 + W], wait=wG_, inc="GP")
                    f1 = p.do("gpsimd", "tensor_tensor", b.h[:, a0 + wDc:a0 + W],
                              b.tmpG[:, 0:wGc], b.h[:, b0 + wDc:b0 + W],
                              op=ALU.add, wait=(f0,), inc="GP")
                    gT = p.do("gpsimd", "tensor_tensor",
                              b.h[:, b0 + wDc:b0 + W], b.tmpG[:, 0:wGc],
                              b.h[:, b0 + wDc:b0 + W], op=ALU.subtract,
                              wait=(f1,), inc="GP")
                    bfG[0] = gT
                bft[(hh, q)] = (dT, gT)
                bfD[0] = dT

            prev_hd = None
            hmt = []
            for hc in range(HC1):
                pb = hc % 2
                wgl = p.load(b.wG[pb][:, :], wgt[hc, :, :], sem=f"WG{pb}",
                             wait=(wfree["G"][pb],) if wfree["G"][pb] else ())
                wul = p.load(b.wU[pb][:, :], wup[hc, :, :], sem=f"WU{pb}",
                             wait=(wfree["U"][pb],) if wfree["U"][pb] else ())
                w0 = bk.w(0)
                for c in range(NCH):
                    p.do("tensor", "matmul", bk.ap(0)[:, :],
                         b.wG[pb][:, c * 128:(c + 1) * 128],
                         XH[:, c * 512:(c + 1) * 512], start=(c == 0),
                         stop=(c == NCH - 1),
                         wait=((wgl, XHT, XHTG) + w0) if c == 0 else (),
                         inc="PE" if c == NCH - 1 else None)
                g_pe = p.now("PE")
                w1 = bk.w(1)
                for c in range(NCH):
                    p.do("tensor", "matmul", bk.ap(1)[:, :],
                         b.wU[pb][:, c * 128:(c + 1) * 128],
                         XH[:, c * 512:(c + 1) * 512], start=(c == 0),
                         stop=(c == NCH - 1),
                         wait=((wul,) + w1) if c == 0 else (),
                         inc="PE" if c == NCH - 1 else None)
                u_pe = p.now("PE")
                ga = p.do("scalar", "activation", b.gsil[:, :], bk.ap(0)[:, :],
                          AF.Silu, bias=0.0, scale=b.scgt[:, hc:hc + 1],
                          wait=(g_pe, s2) + ((prev_hd,) if prev_hd else ()),
                          inc="AC")
                bk.r(0, ga)
                wfree["G"][pb] = ga
                ua = p.do("scalar", "activation", b.usc[:, :], bk.ap(1)[:, :],
                          AF.Identity, bias=0.0, scale=b.scup[:, hc:hc + 1],
                          wait=(u_pe, s1), inc="AC")
                bk.r(1, ua)
                wfree["U"][pb] = ua
                prev_hd = p.do("vector", "tensor_tensor",
                               b.h[:, hc * 512:(hc + 1) * 512],
                               b.gsil[:, :], b.usc[:, :], op=ALU.mult,
                               wait=(ga, ua), inc="DV")
                hmt.append(prev_hd)
            HDONE = p.now("DV")
            H4 = ((1, 1, 1, 1), (1, -1, 1, -1), (1, 1, -1, -1), (1, -1, -1, 1))
            gbanks = (0, 1, 3, 4)
            for g4 in range(HC1 // 4):
                toks = []
                for r in range(4):
                    bi = gbanks[r]
                    ww = bk.w(bi)
                    for i in range(4):
                        ic = 4 * g4 + i
                        stat = b.Hs2 if H4[r][i] > 0 else b.Hs2n
                        p.do("tensor", "matmul", bk.ap(bi)[:, :],
                             stat[:, ic * 128:(ic + 1) * 128],
                             b.h[:, ic * 512:(ic + 1) * 512],
                             start=(i == 0), stop=(i == 3),
                             wait=((hmt[4 * g4 + 3], HST) + ww) if i == 0 else (),
                             inc="PE" if i == 3 else None)
                pe4 = p.now("PE")
                for r in range(4):
                    oc = 4 * g4 + r
                    av = p.do("scalar", "copy", b.h[:, oc * 512:(oc + 1) * 512],
                              bk.ap(gbanks[r])[:, :], wait=(pe4,), inc="AC")
                    bk.r(gbanks[r], av)
                    satok[oc] = av
            for hh in (4, 8, 16, 32):
                for q in range(HC1 // (2 * hh)):
                    emit_bf(hh, q)
            ytoks = []
            for ec in range(NCH):
                pb = ec % 2
                wdl = p.load(b.wD[pb][:, :], wdn1[ec, :, :], sem=f"WD{pb}",
                             wait=(wfree["D"][pb],) if wfree["D"][pb] else ())
                w2 = bk.w(2)
                for hc in range(HC1):
                    p.do("tensor", "matmul", bk.ap(2)[:, :],
                         b.wD[pb][:, hc * 128:(hc + 1) * 128],
                         b.h[:, hc * 512:(hc + 1) * 512], start=(hc == 0),
                         stop=(hc == HC1 - 1),
                         wait=((wdl,) + bft[(32, 0)] + w2) if hc == 0 else (),
                         inc="PE" if hc == HC1 - 1 else None)
                d_pe = p.now("PE")
                ya = p.do("scalar", "activation",
                          b.yb[:, ec * 512:(ec + 1) * 512], bk.ap(2)[:, :],
                          AF.Identity, bias=0.0, scale=b.scdp[:, ec:ec + 1],
                          wait=(d_pe, s3), inc="AC")
                bk.r(2, ya)
                wfree["D"][pb] = ya
                ytoks.append(ya)
            srcs = [(b.yb[:, c * 512:(c + 1) * 512], ytoks[c]) for c in range(NCH)]
            ln.stats(p, bk, srcs, 6, 7)
            a1, a2 = ln.bcast(p, bk, 4, 5)
            t1prev = None
            sttoks = []
            for c in range(NCH):
                xl = p.load(b.xf[c % 2][:, :], x_out[c, :, ot * 512:(ot + 1) * 512],
                            sem=f"XF{c % 2}",
                            wait=(sttoks[c - 2],) if c >= 2 else (a1, a2))
                d1 = p.do("vector", "tensor_tensor", b.t1[:, :],
                          b.yb[:, c * 512:(c + 1) * 512], ln.rs_t[:, :],
                          op=ALU.mult, wait=(a1, a2) + ((t1prev,) if t1prev else ()),
                          inc="DV")
                d2 = p.do("vector", "tensor_tensor", b.t1[:, :], b.t1[:, :],
                          ln.c_t[:, :], op=ALU.add, wait=(d1,), inc="DV")
                xu = p.do("vector", "tensor_tensor", b.xf[c % 2][:, :],
                          b.xf[c % 2][:, :], b.t1[:, :], op=ALU.add,
                          wait=(d2, xl), inc="DV")
                t1prev = xu
                stt = p.load(x_out[c, :, ot * 512:(ot + 1) * 512],
                             b.xf[c % 2][:, :], eng="gpsimd", sem=f"SX{c % 2}",
                             wait=(xu,))
                sttoks.append(stt)
        ar.pop()

        if upto < 4:
            p.do("gpsimd", "nop", wait=(p.now("ST"), p.now("SX0"), p.now("SX1")))
            p.emit()
            return nc
        ar.push()
        b.xn2 = ar.alloc([128, NCH * TC], BF16)
        b.Vall = ar.alloc([128, 8 * D], BF16)
        b.kv2 = [ar.alloc([128, 128], F32) for _ in range(2)]
        b.kvbf2 = [ar.alloc([128, 128], BF16) for _ in range(2)]
        b.sm4 = [[ar.alloc([128, 128], BF16) for _ in range(2)]
                 for _ in range(2)]
        b.ra = ar.alloc([128, TC], F32)
        b.rb = ar.alloc([128, TC], F32)
        b.rsw = ar.alloc([128, TC], BF16)
        b.rtp = ar.alloc([128, TC], BF16)
        b.rt2 = ar.alloc([128, TC], F32)
        b.cosT = ar.alloc([128, TC], F32)
        b.sinT = ar.alloc([128, TC], F32)
        b.qdt = ar.alloc([128, NHEAD * 128], F32)
        b.kdc = ar.alloc([128, NHEAD], F32)
        b.dmk = ar.alloc([128, NHEAD * 128], F32)
        b.xf2 = [ar.alloc([128, 512], F32) for _ in range(2)]

        PH4 = (p.now("PE"), p.now("DV"), p.now("AC"), p.now("ST"),
               p.now("WG0"), p.now("WG1"), p.now("WU0"), p.now("WU1"),
               p.now("WD0"), p.now("WD1"), p.now("SC"), p.now("SX0"), p.now("SX1"),
               p.now("XF0"), p.now("XF1"))
        p.load(b.cosT[:, :], cosT[:, :], sem="C2", wait=PH4)
        p.load(b.sinT[:, :], sinT[:, :], sem="C2")
        p.load(b.qdt[:, :], qdec_t[:, :], sem="C2")
        p.load(b.kdc[:, :], kdec_c[:, :], sem="C2")
        p.load(b.dmk[:, :], dmask[:, :], sem="C2")
        C2L = p.now("C2")

        ar.push()
        b.wvA = ar.alloc([128, NCH * D], BF16)
        VB = (p.now("PE"), p.now("AC"), p.now("GP"), p.now("ST"), p.now("DV"))
        vls = []
        for c in range(NCH):
            vl = p.load(b.wvA[:, c * D:(c + 1) * D], wv2[c, :, :], sem="W2V",
                        wait=VB if c == 0 else ())
            vls.append(vl)

        for ot in range(2):
            OTB = PH4 if ot == 0 else (p.now("DV"),)
            xls = []
            for c in range(NCH):
                xl = p.load(b.xn2[:, c * TC + ot * 512: c * TC + (ot + 1) * 512],
                            x_out[c, :, ot * 512:(ot + 1) * 512],
                            sem=f"XG{c % 2}",
                            wait=(xls[c - 2],) if c >= 2 else OTB)
                xls.append(xl)
            srcs = [(b.xn2[:, c * TC + ot * 512: c * TC + (ot + 1) * 512],
                     xls[c]) for c in range(NCH)]
            ln.stats(p, bk, srcs, 6, 7)
            a1, a2 = ln.bcast(p, bk, 4, 5)
            xtoks = []
            t1prev = None
            for c in range(NCH):
                xl = p.load(b.xf2[c % 2][:, :], x_out[c, :, ot * 512:(ot + 1) * 512],
                            sem=f"XG{c % 2}",
                            wait=(xtoks[c - 2],) if c >= 2 else (a1, a2))
                d1 = p.do("vector", "tensor_tensor", b.t1[:, :], b.xf2[c % 2][:, :],
                          ln.rs_t[:, :], op=ALU.mult,
                          wait=(xl, a1, a2) + ((t1prev,) if t1prev else ()),
                          inc="DV")
                d2 = p.do("vector", "tensor_tensor",
                          b.xn2[:, c * TC + ot * 512: c * TC + (ot + 1) * 512],
                          b.t1[:, :], ln.c_t[:, :], op=ALU.add, wait=(d1,),
                          inc="DV")
                t1prev = d2
                xtoks.append(d2)
        XN2 = p.now("DV")

        for js in range(32):
            j, sx = js // 4, js % 4
            bi = 1 + (js % 2)
            ww = bk.w(bi)
            for c in range(NCH):
                if js == 0:
                    mmw = ((vls[-1], XN2) + ww) if c == 0 else ()
                else:
                    mmw = ww if c == 0 else ()
                p.do("tensor", "matmul", bk.ap(bi)[:, :],
                     b.xn2[:, c * TC + j * 128:c * TC + (j + 1) * 128],
                     b.wvA[:, c * D + sx * 512:c * D + (sx + 1) * 512],
                     start=(c == 0), stop=(c == NCH - 1),
                     wait=mmw, inc="PE" if c == NCH - 1 else None)
            tpe = p.now("PE")
            a = p.do("scalar", "copy",
                     b.Vall[:, j * D + sx * 512:j * D + (sx + 1) * 512],
                     bk.ap(bi)[:, :], wait=(tpe,), inc="AC")
            bk.r(bi, a)
        VALL = p.now("AC")
        ar.pop()

        ar.push()
        b.wq4 = [ar.alloc([128, D], BF16) for _ in range(4)]
        b.wk4 = [ar.alloc([128, D], BF16) for _ in range(4)]
        b.Qh4 = [ar.alloc([128, TC], BF16) for _ in range(4)]
        b.Kh4 = [ar.alloc([128, TC], BF16) for _ in range(4)]
        b.Ktm4 = [ar.alloc([128, TC], BF16) for _ in range(4)]
        b.qdh4 = [ar.alloc([128, TC], F32) for _ in range(4)]
        b.a2h2 = [ar.alloc([128, TC], BF16) for _ in range(2)]

        wfree2 = [None, None]
        VBAR = p.now("PE")
        qfree = [VBAR] * 4
        qd_st4 = [None] * 4
        a2_st2 = [None, None]
        sum_st2 = [None, None]
        kvlast = [None, None]
        rotch = {"dma": None, "last": ()}

        def prep(i, res):
            pi = i % 2
            for s in (0, 1):
                h = 2 * i + s
                bi4 = 2 * pi + s
                wlq = p.load(b.wq4[bi4][:, :], wq2[h, :, :], sem=f"W2{bi4}",
                             wait=(qfree[bi4],) if qfree[bi4] else ())
                wlk = p.load(b.wk4[bi4][:, :], wk2[h, :, :], sem=f"W2{bi4}")
                res[("WL", s)] = (wlq, wlk)
            for s in (0, 1):
                h = 2 * i + s
                bi4 = 2 * pi + s
                wlq, wlk = res[("WL", s)]
                for ot in range(2):
                    ww = bk.w(6)
                    for c in range(NCH):
                        p.do("tensor", "matmul", bk.ap(6)[:, :],
                             b.wq4[bi4][:, c * 128:(c + 1) * 128],
                             b.xn2[:, c * TC + ot * 512: c * TC + (ot + 1) * 512],
                             start=(c == 0), stop=(c == NCH - 1),
                             wait=((wlk, XN2) + ww) if c == 0 else (),
                             inc="PE" if c == NCH - 1 else None)
                    tpe = p.now("PE")
                    a = p.do("scalar", "copy",
                             b.Qh4[bi4][:, ot * 512:(ot + 1) * 512],
                             bk.ap(6)[:, :],
                             wait=(tpe,) + ((qfree[bi4],)
                                            if (qfree[bi4] and ot == 0) else ()),
                             inc="AC")
                    bk.r(6, a)
                res[("QEV", s)] = p.now("AC")
            for s in (0, 1):
                h = 2 * i + s
                bi4 = 2 * pi + s
                for ot in range(2):
                    ww = bk.w(6)
                    for c in range(NCH):
                        p.do("tensor", "matmul", bk.ap(6)[:, :],
                             b.wk4[bi4][:, c * 128:(c + 1) * 128],
                             b.xn2[:, c * TC + ot * 512: c * TC + (ot + 1) * 512],
                             start=(c == 0), stop=(c == NCH - 1),
                             wait=ww if c == 0 else (),
                             inc="PE" if c == NCH - 1 else None)
                    tpe = p.now("PE")
                    a = p.do("scalar", "copy",
                             b.Kh4[bi4][:, ot * 512:(ot + 1) * 512],
                             bk.ap(6)[:, :],
                             wait=(tpe,) + ((qfree[bi4],)
                                            if (qfree[bi4] and ot == 0) else ()),
                             inc="AC")
                    bk.r(6, a)
                res[("KEV", s)] = p.now("AC")

        def prepB(i, res):
            pi = i % 2
            if upto < 5:
                return
            bufs = [b.Qh4[2 * pi + 0], b.Kh4[2 * pi + 0],
                    b.Qh4[2 * pi + 1], b.Kh4[2 * pi + 1]]
            ing = []
            for k, buf in enumerate(bufs):
                sK = res[("KEV", k // 2)]
                w0 = (sK, C2L) + rotch["last"] + \
                    (((rotch["dma"],)) if (rotch["dma"] and k == 0) else ())
                ing.append(p.do("gpsimd", "dma_start",
                                out=b.rsw[32 * k:32 * k + 32, :],
                                in_=buf[64:96, :], inc="RSW", inck=16, wait=w0))
                ing.append(p.do("gpsimd", "dma_start",
                                out=b.rtp[32 * k:32 * k + 32, :],
                                in_=buf[0:32, :], inc="RSW", inck=16))
            GIN = p.now("RSW")
            u1 = p.do("vector", "tensor_tensor", b.ra[:, :], b.rtp[:, :],
                      b.cosT[:, :], op=ALU.mult,
                      wait=(GIN, C2L) + rotch["last"], inc="DV")
            u2 = p.do("vector", "tensor_tensor", b.rt2[:, :], b.rsw[:, :],
                      b.sinT[:, :], op=ALU.mult, wait=(GIN,), inc="DV")
            y1 = p.do("vector", "tensor_tensor", b.ra[:, :], b.ra[:, :],
                      b.rt2[:, :], op=ALU.add, wait=(u1, u2), inc="DV")
            v1 = p.do("vector", "tensor_tensor", b.rb[:, :], b.rsw[:, :],
                      b.cosT[:, :], op=ALU.mult, wait=(u2,), inc="DV")
            v2 = p.do("vector", "tensor_tensor", b.rt2[:, :], b.rtp[:, :],
                      b.sinT[:, :], op=ALU.mult, wait=(y1,), inc="DV")
            y2 = p.do("vector", "tensor_tensor", b.rb[:, :], b.rb[:, :],
                      b.rt2[:, :], op=ALU.subtract, wait=(v1, v2), inc="DV")
            dlast = None
            for k, buf in enumerate(bufs):
                dlast = p.do("gpsimd", "dma_start", out=buf[0:32, :],
                             in_=b.ra[32 * k:32 * k + 32, :], inc="RSW",
                             inck=16, wait=(y1, v2) if k == 0 else ())
                dlast = p.do("gpsimd", "dma_start", out=buf[64:96, :],
                             in_=b.rb[32 * k:32 * k + 32, :], inc="RSW",
                             inck=16, wait=(y2,) if k == 0 else ())
            rotch["dma"] = dlast
            rotch["last"] = (y2, v2)
            res["ROT"] = p.now("DV")
            res["ROTD"] = p.now("RSW")
            if upto < 6:
                return
            for s in (0, 1):
                h = 2 * i + s
                bi4 = 2 * pi + s
                qv = b.qdt[:, h * 128:(h + 1) * 128]
                qv3 = qv.unsqueeze(1).broadcast_to([128, 8, 128])
                qd1 = p.do("vector", "tensor_tensor",
                           b.qdh4[bi4][:, :].rearrange("p (a b) -> p a b", a=8, b=128),
                           b.Qh4[bi4][:, :].rearrange("p (a b) -> p a b", a=8, b=128),
                           qv3, op=ALU.mult,
                           wait=(res["ROT"], res["ROTD"], C2L)
                           + ((qd_st4[bi4],) if qd_st4[bi4] else ()), inc="DV")
                res[("QD1", s)] = qd1
                qd_st4[bi4] = p.load(qd_out[h, :, :], b.qdh4[bi4][:, :],
                                     eng="gpsimd", sem=f"SQ{bi4}", wait=(qd1,))
                if upto < 7:
                    continue
                for j in range(8):
                    bt = 6 + (j % 2)
                    ps_t = bk.ap(bt).bitcast(BF16)
                    tt = p.do("tensor", "transpose", ps_t[:, 0:128],
                              b.Kh4[bi4][:, j * 128:(j + 1) * 128], b.idb[:, :],
                              wait=(res["ROT"], res["ROTD"]) + bk.w(bt), inc="PE")
                    a = p.do("scalar", "activation",
                             b.Ktm4[bi4][:, j * 128:(j + 1) * 128],
                             ps_t[:, 0:128], AF.Identity, bias=0.0,
                             scale=b.kdc[:, h:h + 1],
                             wait=(tt,) + ((qfree[bi4],)
                                           if (qfree[bi4] and j == 0) else ()),
                             inc="AC")
                    bk.r(bt, a)
                res[("KTM", s)] = p.now("AC")

        def scan(i, res):
            pi = i % 2
            kvtok = [None, None]
            for s in (0, 1):
                kvtok[s] = p.do("gpsimd", "memset", b.kv2[s][:, :], 0.0,
                                wait=(res["ROT"],)
                                + ((kvlast[s],) if kvlast[s] else ()),
                                inc="GP")
            spool = ((0, 1), (3, 4))
            avb = (2, 5)
            smtok = [[None, None], [None, None]]
            smev = [[None, None], [None, None]]

            def emit_ts(j):
                for s in (0, 1):
                    h = 2 * i + s
                    bi4 = 2 * pi + s
                    Sb = spool[s][j % 2]
                    ts = p.do("tensor", "matmul", bk.ap(Sb)[:, :128],
                              b.Kh4[bi4][:, j * 128:(j + 1) * 128],
                              b.Qh4[bi4][:, j * 128:(j + 1) * 128],
                              start=True, stop=True,
                              wait=(res["ROT"], res["ROTD"]) + bk.w(Sb),
                              inc="PE")
                    dm = b.dmk[:, h * 128:(h + 1) * 128]
                    sme = p.do("vector", "tensor_tensor", b.sm4[s][j % 2][:, :],
                               bk.ap(Sb)[:, :128], dm, op=ALU.mult,
                               wait=(ts, C2L)
                               + ((smtok[s][j % 2],) if smtok[s][j % 2] else ()),
                               inc="DV")
                    bk.r(Sb, sme)
                    smev[s][j % 2] = sme

            emit_ts(0)
            for j in range(8):
                if j + 1 < 8:
                    emit_ts(j + 1)
                for s in (0, 1):
                    h = 2 * i + s
                    bi4 = 2 * pi + s
                    Ab = avb[s]
                    sme = smev[s][j % 2]
                    wv_ = bk.w(Ab)
                    t1_ = p.do("tensor", "matmul", bk.ap(Ab)[:, :128],
                               b.Vall[:, j * D + h * 128:j * D + (h + 1) * 128],
                               b.sm4[s][j % 2][:, :], start=True, stop=False,
                               wait=(sme,) + wv_, inc="PE")
                    smtok[s][j % 2] = t1_
                    t2_ = p.do("tensor", "matmul", bk.ap(Ab)[:, :128],
                               b.kv2[s][:, :],
                               b.qdh4[bi4][:, j * 128:(j + 1) * 128],
                               start=False, stop=True,
                               wait=(kvtok[s], res[("QD1", s)]), inc="PE")
                    aev = p.do("scalar", "copy",
                               b.a2h2[s][:, j * 128:(j + 1) * 128],
                               bk.ap(Ab)[:, :128],
                               wait=(t2_,) + ((a2_st2[s],)
                                              if (j == 0 and a2_st2[s]) else ()),
                               inc="AC")
                    bk.r(Ab, aev)
                    Kb = spool[s][j % 2]
                    wu_ = bk.w(Kb)
                    tu = p.do("tensor", "matmul", bk.ap(Kb)[:, :128],
                              b.Ktm4[bi4][:, j * 128:(j + 1) * 128],
                              b.Vall[:, j * D + h * 128:j * D + (h + 1) * 128],
                              start=True, stop=True,
                              wait=(res[("KTM", s)],) + wu_, inc="PE")
                    kvtok[s] = p.do("vector", "scalar_tensor_tensor",
                                    b.kv2[s][:, :], b.kv2[s][:, :],
                                    float(RC[h]), bk.ap(Kb)[:, :128],
                                    op0=ALU.mult, op1=ALU.add,
                                    wait=(tu,)
                                    + ((sum_st2[s],) if (j == 4 and sum_st2[s])
                                       else ()),
                                    inc="DV")
                    bk.r(Kb, kvtok[s])
                    if j == 3 or j == 7:
                        seg = j // 4
                        sum_st2[s] = p.load(sum_out[seg, h, :, :],
                                            b.kv2[s][:, :], eng="gpsimd",
                                            sem=f"SS{s}", wait=(kvtok[s],))
                        if j == 3:
                            kvtok[s] = p.do("gpsimd", "memset", b.kv2[s][:, :],
                                            0.0, wait=(sum_st2[s],), inc="GP")
            for s in (0, 1):
                h = 2 * i + s
                a2_st2[s] = p.load(a2p_out[h, :, :], b.a2h2[s][:, :],
                                   eng="gpsimd", sem=f"SA{s}",
                                   wait=(p.now("AC"),))
                kvlast[s] = sum_st2[s]
            pe_done = p.now("PE")
            qfree[2 * pi] = pe_done
            qfree[2 * pi + 1] = pe_done

        res_all = [dict() for _ in range(NHEAD // 2)]
        prep(0, res_all[0])
        prepB(0, res_all[0])
        for i in range(NHEAD // 2):
            if i + 1 < NHEAD // 2:
                prep(i + 1, res_all[i + 1])
            if upto >= 8:
                scan(i, res_all[i])
            if i + 1 < NHEAD // 2:
                prepB(i + 1, res_all[i + 1])
        ar.pop()
        p.do("gpsimd", "nop", wait=(p.now("ST"), p.now("SX0"), p.now("SX1"),
                                    p.now("SS0"), p.now("SS1"),
                                    p.now("SQ0"), p.now("SQ1"), p.now("SQ2"),
                                    p.now("SQ3"), p.now("SA0"), p.now("SA1")))
        p.emit()
    return nc


def _l2_slopes():
    n = NHEAD
    start = 2.0 ** (-(2.0 ** (-(np.log2(n) - 3))))
    return np.array([start * start ** i for i in range(n)], np.float64)


def _l2_rc():
    r = np.exp(-_l2_slopes())
    return (r ** 128.0).astype(np.float64)


def prep_launch2(I, x2_percore, xn1_percore, k1_percore, v1_percore):
    com = {}
    wq1 = np.empty((NHEAD, 128, D), np.float32)
    for g in range(4):
        for sct in range(4):
            e = 4 * g + sct
            rows = I["L1_attn_Wq"][512 * g + 128 * sct: 512 * g + 128 * (sct + 1)]
            wq1[e] = _img_dmaj(rows.T).reshape(128, NCH, 128).transpose(1, 0, 2).reshape(128, D) if False else _img_dmaj(rows.T)
    com["wq1"] = np.ascontiguousarray(wq1)
    wpT = I["L1_attn_Wp"].T
    wp1 = np.empty((NCH, 128, D), np.float32)
    for ec in range(NCH):
        wp1[ec] = _img_dmaj(wpT[:, ec * 128:(ec + 1) * 128])
    com["wp1"] = np.ascontiguousarray(wp1)
    com["bp1"] = np.ascontiguousarray(I["L1_attn_bp"].reshape(NCH, 128).T)
    wupT = I["L1_ffn_W_up"].T
    wgtT = I["L1_ffn_W_gate"].T
    com["wup"] = np.ascontiguousarray(
        wupT.reshape(NCH, 128, HC1, 128).transpose(2, 1, 0, 3)).reshape(HC1, 128, D)
    com["wgt"] = np.ascontiguousarray(
        wgtT.reshape(NCH, 128, HC1, 128).transpose(2, 1, 0, 3)).reshape(HC1, 128, D)
    wdnT = I["L1_ffn_W_down"].T
    com["wdn1"] = np.ascontiguousarray(
        wdnT.reshape(HC1, 128, NCH, 128).transpose(2, 1, 0, 3)).reshape(NCH, 128, HB1)
    com["su_ug"] = np.ascontiguousarray(I["L1_ffn_SU_ug"].reshape(NCH, 128).T)
    com["su_dp"] = np.ascontiguousarray(I["L1_ffn_SU_dp"].reshape(HC1, 128).T)
    com["wsc_up"] = np.ascontiguousarray(I["L1_ffn_Wscale_ug"][:HB1].reshape(HC1, 128).T)
    com["wsc_gt"] = np.ascontiguousarray(I["L1_ffn_Wscale_ug"][HB1:].reshape(HC1, 128).T)
    com["wsc_dp"] = np.ascontiguousarray(I["L1_ffn_Wscale_dp"].reshape(NCH, 128).T)
    H = np.array([[1.0]])
    while H.shape[0] < 128:
        H = np.block([[H, H], [H, -H]])
    com["h128"] = H.astype(BF)
    for nm, w in (("wq2", I["L2_attn_Wq"]), ("wk2", I["L2_attn_Wk"]),
                  ("wv2", I["L2_attn_Wv"])):
        img = np.empty((NHEAD, 128, D), np.float32)
        for hh in range(NHEAD):
            img[hh] = _img_dmaj(w[:, hh * 128:(hh + 1) * 128])
        if nm == "wv2":
            img = img.reshape(NHEAD, 128, NCH, 128).transpose(2, 1, 0, 3)
            img = img.reshape(NCH, 128, NHEAD * 128)
        com[nm] = np.ascontiguousarray(img)
    slopes = _l2_slopes()
    r = np.exp(-slopes)
    i = np.arange(128, dtype=np.float64)
    qdec = (r[:, None] ** (i[None, :] + 1.0)).astype(np.float32)
    kdec = (r[:, None] ** (127.0 - i[None, :])).astype(np.float32)
    diff = i[:, None] - i[None, :]
    mask = diff >= 0
    dmk = np.where(mask[None], r[:, None, None] ** np.where(mask, diff, 0.0)[None], 0.0)
    dmkT = dmk.transpose(0, 2, 1).astype(np.float32)
    com["qdec_t"] = np.ascontiguousarray(
        np.broadcast_to(qdec[:, None, :], (NHEAD, 128, 128)).transpose(1, 0, 2).reshape(128, NHEAD * 128).copy())
    com["kdec_c"] = np.ascontiguousarray(kdec.T)
    com["dmask"] = np.ascontiguousarray(
        dmkT.transpose(1, 0, 2)).reshape(128, NHEAD * 128)
    in_maps = []
    k1all = to_global([k.transpose(1, 0) if False else k for k in k1_percore]) if False else None
    k1glob = np.empty((4, 128, T), BF)
    v1glob = np.empty((T, 512), BF)
    for c in range(NC):
        a, bb = zz_chunks(c)
        k1glob[:, :, a * 512:(a + 1) * 512] = k1_percore[c][:, :, 0:512]
        k1glob[:, :, bb * 512:(bb + 1) * 512] = k1_percore[c][:, :, 512:1024]
        v1glob[a * 512:(a + 1) * 512] = v1_percore[c][0:512]
        v1glob[bb * 512:(bb + 1) * 512] = v1_percore[c][512:1024]
    v1img = np.ascontiguousarray(
        v1glob.reshape(64, 128, 4, 128).transpose(2, 1, 0, 3)).reshape(4, 128, T)
    com["k1all"] = k1glob
    com["v1img"] = v1img
    kb = np.full((128,), KILL, np.float32)
    tri = np.zeros((4, 128, 512), np.float32)
    for j in range(4):
        kk = np.arange(128)[:, None] + j * 128
        qq = np.arange(512)[None, :]
        tri[j] = np.where(kk <= qq, 0.0, KILL / (1.0 / np.sqrt(128.0)))
    tri2 = np.ascontiguousarray(tri.transpose(1, 0, 2)).reshape(128, 4 * 512)
    for c in range(NC):
        a, bb = zz_chunks(c)
        kb = np.full((128,), KILL, np.float32)
        for qt, g in enumerate((a, bb)):
            nfull = 4 * g
            for kc in range(NV_L0[qt]):
                kb[qt * 64 + kc] = SHIFT if kc < nfull else KILL
        im = dict(com)
        im["kbias"] = np.ascontiguousarray(np.broadcast_to(kb[None, :], (128, 128)).copy())
        im["tri"] = tri2
        im["xin"] = x2_percore[c]
        im["xn1"] = xn1_percore[c]
        im["k1own"] = k1_percore[c]
        v1o = v1_percore[c]
        im["v1ownimg"] = np.ascontiguousarray(
            v1o.reshape(8, 128, 4, 128).transpose(2, 1, 0, 3)).reshape(4, 128, TC)
        own = np.r_[a * 512:(a + 1) * 512, bb * 512:(bb + 1) * 512].astype(np.float64)
        freq = (1.0 / 1024.0) ** np.linspace(0.0, 1.0, 32)
        th = own[None, :] * freq[:, None]
        im["cosT"] = np.tile(np.cos(th).astype(np.float32), (4, 1))
        im["sinT"] = np.tile(np.sin(th).astype(np.float32), (4, 1))
        in_maps.append(im)
    return in_maps

HB2 = 8192
HC2 = 64
I3 = 5632
HSH = I3 * 2 // NC // 1
ESH = D // NC


def build_launch3():
    nc = bass.Bass("TRN2", target_bir_lowering=False, debug=False)
    stack = ExitStack()
    nc._stk = stack

    xin = nc.dram_tensor("xin", [NCH, 128, TC], F32, kind="ExternalInput").ap()
    a2p = nc.dram_tensor("a2p", [NHEAD, 128, TC], BF16, kind="ExternalInput").ap()
    qd = nc.dram_tensor("qd", [NHEAD, 128, TC], BF16, kind="ExternalInput").ap()
    sums = nc.dram_tensor("sums", [128, 16 * NHEAD * 128], BF16, kind="ExternalInput").ap()
    wdec = nc.dram_tensor("wdec", [128, 2 * 16 * NHEAD], F32, kind="ExternalInput").ap()
    rmsw = nc.dram_tensor("rmsw", [128, NCH], F32, kind="ExternalInput").ap()
    wo2 = nc.dram_tensor("wo2", [NCH, 128, D], F32, kind="ExternalInput").ap()
    wgu2 = nc.dram_tensor("wgu2", [2 * HC2, 128, D], F32, kind="ExternalInput").ap()
    wdn2 = nc.dram_tensor("wdn2", [NCH, 128, HB2], F32, kind="ExternalInput").ap()
    gup_sh = nc.dram_tensor("gup_sh", [2, 128, HSH], I32, kind="ExternalInput").ap()
    gup_scl = nc.dram_tensor("gup_scl", [2, 128, HSH], F16, kind="ExternalInput").ap()
    dn_sh = nc.dram_tensor("dn_sh", [6, 128, ESH], I32, kind="ExternalInput").ap()
    dn_scl = nc.dram_tensor("dn_scl", [6, 128, ESH], F16, kind="ExternalInput").ap()

    x_out = nc.dram_tensor("x_out", [NCH, 128, TC], F32, kind="ExternalOutput").ap()
    xn4_out = nc.dram_tensor("xn4_out", [NCH, 128, TC], BF16, kind="ExternalOutput").ap()
    wdq_gu = nc.dram_tensor("wdq_gu", [D, HSH], BF16, kind="ExternalOutput").ap()
    wdq_dn = nc.dram_tensor("wdq_dn", [I3, ESH], BF16, kind="ExternalOutput").ap()

    RC = _l2_rc()

    with stack:
        p = Prog(nc, stack)
        ar = Arena(nc, stack, "arena", 206 * 1024)
        bk = Banks(nc, stack)
        b = B()

        b.ones_col_bf = ar.alloc([128, 1], BF16)
        b.epscol = ar.alloc([128, 1], F32)
        b.ones_row_f = ar.alloc([1, 128], F32)
        b.ones_row_bf = ar.alloc([1, 128], BF16)
        b.sq = ar.alloc([128, 512], BF16)
        b.t1 = ar.alloc([128, 512], F32)
        b.xf = [ar.alloc([128, 512], F32) for _ in range(2)]
        ln = LNH(p, b, ar)
        p.do("gpsimd", "memset", b.ones_col_bf[:, :], 1.0, inc="GP")
        p.do("gpsimd", "memset", b.ones_row_f[:, :], 1.0, inc="GP")
        p.do("gpsimd", "memset", b.ones_row_bf[:, :], 1.0, inc="GP")
        p.do("gpsimd", "memset", b.epscol[:, :], 1e-5, inc="GP")
        CGP = p.now("GP")

        def ln1_into(dstbuf, ot, first_wait, xg="XH", src_dram=None,
                     inplace=False):
            sd = src_dram if src_dram is not None else x_out
            xls = []
            for c in range(NCH):
                xl = p.load(dstbuf[:, c * 512:(c + 1) * 512],
                            sd[c, :, ot * 512:(ot + 1) * 512],
                            sem=f"{xg}{c % 2}",
                            wait=(xls[c - 2],) if c >= 2 else first_wait)
                xls.append(xl)
            srcs = [(dstbuf[:, c * 512:(c + 1) * 512], xls[c])
                    for c in range(NCH)]
            ln.stats(p, bk, srcs, 6, 7)
            a1, a2 = ln.bcast(p, bk, 4, 5)
            xtoks = []
            t1prev = None
            for c in range(NCH):
                if inplace:
                    d1 = p.do("vector", "tensor_tensor", b.t1[:, :],
                              dstbuf[:, c * 512:(c + 1) * 512],
                              ln.rs_t[:, :], op=ALU.mult,
                              wait=(a1, a2) + ((t1prev,) if t1prev else ()),
                              inc="DV")
                else:
                    xl = p.load(b.xf[c % 2][:, :],
                                sd[c, :, ot * 512:(ot + 1) * 512],
                                sem=f"{xg}{c % 2}",
                                wait=(xtoks[c - 2],) if c >= 2 else (a1, a2))
                    d1 = p.do("vector", "tensor_tensor", b.t1[:, :],
                              b.xf[c % 2][:, :],
                              ln.rs_t[:, :], op=ALU.mult,
                              wait=(xl, a1, a2) + ((t1prev,) if t1prev else ()),
                              inc="DV")
                d2 = p.do("vector", "tensor_tensor",
                          dstbuf[:, c * 512:(c + 1) * 512], b.t1[:, :],
                          ln.c_t[:, :], op=ALU.add, wait=(d1,), inc="DV")
                t1prev = d2
                xtoks.append(d2)
            return xtoks

        def ln2_residual_store(ybuf, ot, ytoks, src_dram, xg="XI"):
            SXB = (p.now("SX0"), p.now("SX1"))
            srcs = [(ybuf[:, c * 512:(c + 1) * 512], ytoks[c]) for c in range(NCH)]
            ln.stats(p, bk, srcs, 6, 7)
            a1, a2 = ln.bcast(p, bk, 4, 5)
            t1prev = None
            sttoks = []
            for c in range(NCH):
                xl = p.load(b.xf[c % 2][:, :],
                            src_dram[c, :, ot * 512:(ot + 1) * 512],
                            sem=f"{xg}{c % 2}",
                            wait=(sttoks[c - 2],) if c >= 2 else (a1, a2) + SXB)
                d1 = p.do("vector", "tensor_tensor", b.t1[:, :],
                          ybuf[:, c * 512:(c + 1) * 512], ln.rs_t[:, :],
                          op=ALU.mult,
                          wait=(a1, a2) + ((t1prev,) if t1prev else ()), inc="DV")
                d2 = p.do("vector", "tensor_tensor", b.t1[:, :], b.t1[:, :],
                          ln.c_t[:, :], op=ALU.add, wait=(d1,), inc="DV")
                xu = p.do("vector", "tensor_tensor", b.xf[c % 2][:, :],
                          b.xf[c % 2][:, :], b.t1[:, :], op=ALU.add,
                          wait=(d2, xl), inc="DV")
                t1prev = xu
                stt = p.load(x_out[c, :, ot * 512:(ot + 1) * 512],
                             b.xf[c % 2][:, :], eng="gpsimd", sem=f"SX{c % 2}",
                             wait=(xu,))
                sttoks.append(stt)
            return sttoks

        ar.push()
        b.sums = ar.alloc([128, 16 * NHEAD * 128], BF16)
        b.attn = ar.alloc([128, NHEAD * TC], BF16)
        b.kvin = ar.alloc([128, NHEAD * 128], F32)
        b.kvtmp = ar.alloc([128, NHEAD * 128], F32)
        b.kvbf = ar.alloc([128, NHEAD * 128], BF16)
        b.wdec = ar.alloc([128, 2 * 16 * NHEAD], F32)
        b.rmsw = ar.alloc([128, NCH], F32)
        ar.push()
        b.qdb = ar.alloc([128, NHEAD * TC], BF16)
        p.load(b.sums[:, :], sums[:, :], sem="LC")
        p.load(b.wdec[:, :], wdec[:, :], sem="LC")
        p.load(b.rmsw[:, :], rmsw[:, :], sem="LC")
        for h in range(NHEAD):
            p.load(b.attn[:, h * TC:(h + 1) * TC], a2p[h, :, :], sem="LC")
            p.load(b.qdb[:, h * TC:(h + 1) * TC], qd[h, :, :], sem="LC")
        SLD = p.now("LC")

        attn_toks = []
        kvc_prev = None
        for si in range(2):
            mz = p.do("gpsimd", "memset", b.kvin[:, :], 0.0,
                      wait=(kvc_prev,) if kvc_prev else (CGP,), inc="GP")
            tok = mz
            for gp in range(16):
                colb = (si * 16 + gp) * NHEAD
                wbc = b.wdec[:, colb:colb + NHEAD].unsqueeze(2) \
                    .broadcast_to([128, NHEAD, 128])
                mt = p.do("vector", "tensor_tensor",
                          b.kvtmp[:, :].rearrange("p (a b) -> p a b",
                                                  a=NHEAD, b=128),
                          b.sums[:, gp * NHEAD * 128:(gp + 1) * NHEAD * 128]
                          .rearrange("p (a b) -> p a b", a=NHEAD, b=128),
                          wbc, op=ALU.mult, wait=(SLD, tok), inc="DV")
                tok = p.do("vector", "tensor_tensor", b.kvin[:, :],
                           b.kvin[:, :], b.kvtmp[:, :], op=ALU.add,
                           wait=(mt,), inc="DV")
            pe_b = p.now("PE")
            kvc = p.do("vector", "tensor_copy", b.kvbf[:, :], b.kvin[:, :],
                       wait=(tok, pe_b), inc="DV")
            kvc_prev = kvc
            for h in range(NHEAD):
                for j in range(4):
                    bi = (h * 4 + j) % 2
                    tm = p.do("tensor", "matmul", bk.ap(bi)[:, :128],
                              b.kvbf[:, h * 128:(h + 1) * 128],
                              b.qdb[:, h * TC + si * 512 + j * 128:
                                    h * TC + si * 512 + (j + 1) * 128],
                              start=True, stop=True, wait=(kvc,) + bk.w(bi),
                              inc="PE")
                    at = p.do("vector", "scalar_tensor_tensor",
                              b.attn[:, h * TC + si * 512 + j * 128:
                                     h * TC + si * 512 + (j + 1) * 128],
                              bk.ap(bi)[:, :128], float(RC[h] ** j),
                              b.attn[:, h * TC + si * 512 + j * 128:
                                     h * TC + si * 512 + (j + 1) * 128],
                              op0=ALU.mult, op1=ALU.add, wait=(tm,), inc="DV")
                    bk.r(bi, at)
                    attn_toks.append(at)
        ATT = p.now("DV")

        for ot in range(2):
            w6 = bk.w(6)
            sqprev = None
            for c in range(NCH):
                at = p.do("scalar", "activation", b.sq[:, :],
                          b.attn[:, c * TC + ot * 512: c * TC + (ot + 1) * 512],
                          AF.Square, wait=(ATT,) + ((sqprev,) if sqprev else ()),
                          inc="AC")
                sqprev = p.do("tensor", "matmul", bk.ap(6)[0:1, :],
                              b.ones_col_bf[:, :], b.sq[:, :],
                              wait=(at,) + (w6 if c == 0 else ()),
                              start=(c == 0), stop=(c == NCH - 1), inc="PE")
            d2 = p.do("vector", "tensor_scalar", ln.vrs[:, :], bk.ap(6)[0:1, :],
                      1.0 / D, None, op0=ALU.mult, wait=(sqprev,), inc="DV")
            bk.r(6, d2)
            sa = p.do("scalar", "activation", ln.vsd[:, :], ln.vrs[:, :], AF.Sqrt,
                      bias=b.epscol[0:1, :], scale=1.0, wait=(d2,), inc="AC")
            d5 = p.do("vector", "reciprocal", ln.vrs[:, :], ln.vsd[:, :],
                      wait=(sa,), inc="DV")
            t1 = p.do("tensor", "matmul", bk.ap(4)[:, :], b.ones_row_f[:, :],
                      ln.vrs[:, :], start=True, stop=True, wait=(d5,) + bk.w(4),
                      inc="PE")
            a1 = p.do("scalar", "copy", ln.rs_t[:, :], bk.ap(4)[:, :], wait=(t1,),
                      inc="AC")
            bk.r(4, a1)
            prev = a1
            for c in range(NCH):
                sl = b.attn[:, c * TC + ot * 512: c * TC + (ot + 1) * 512]
                d1 = p.do("vector", "tensor_tensor", sl, sl, ln.rs_t[:, :],
                          op=ALU.mult, wait=(prev,), inc="DV")
                d2 = p.do("vector", "tensor_scalar", sl, sl,
                          b.rmsw[:, c:c + 1], None, op0=ALU.mult, wait=(d1,),
                          inc="DV")
                prev = d2
        ATN2 = p.now("DV")

        ar.pop()
        QPE = p.now("PE")
        ar.push()
        b.wpb = [ar.alloc([128, D], BF16) for _ in range(2)]
        b.yb = ar.alloc([128, NCH * TC], BF16)
        wfree = [None, None]
        for ec in range(NCH):
            pb = ec % 2
            wl = p.load(b.wpb[pb][:, :], wo2[ec, :, :], sem=f"WB{pb}",
                        wait=(wfree[pb],) if wfree[pb] else (QPE,))
            for ot in range(2):
                ww = bk.w(pb)
                for hc in range(NCH):
                    p.do("tensor", "matmul", bk.ap(pb)[:, :],
                         b.wpb[pb][:, hc * 128:(hc + 1) * 128],
                         b.attn[:, hc * TC + ot * 512: hc * TC + (ot + 1) * 512],
                         start=(hc == 0), stop=(hc == NCH - 1),
                         wait=((wl, ATN2) + ww) if hc == 0 else (),
                         inc="PE" if hc == NCH - 1 else None)
                tpe = p.now("PE")
                a = p.do("scalar", "copy",
                         b.yb[:, ec * TC + ot * 512: ec * TC + (ot + 1) * 512],
                         bk.ap(pb)[:, :], wait=(tpe,), inc="AC")
                bk.r(pb, a)
                wfree[pb] = a
        YT = p.now("AC")
        for ot in range(2):
            SXB = (p.now("SX0"), p.now("SX1"))
            ytoks = [YT if c == 0 else None for c in range(NCH)]
            srcs = [(b.yb[:, c * TC + ot * 512: c * TC + (ot + 1) * 512],
                     ytoks[c]) for c in range(NCH)]
            ln.stats(p, bk, srcs, 6, 7)
            a1, a2 = ln.bcast(p, bk, 4, 5)
            t1prev = None
            sttoks = []
            for c in range(NCH):
                xl = p.load(b.xf[c % 2][:, :], xin[c, :, ot * 512:(ot + 1) * 512],
                            sem=f"XI{c % 2}",
                            wait=(sttoks[c - 2],) if c >= 2 else (a1, a2) + SXB)
                d1 = p.do("vector", "tensor_tensor", b.t1[:, :],
                          b.yb[:, c * TC + ot * 512: c * TC + (ot + 1) * 512],
                          ln.rs_t[:, :], op=ALU.mult,
                          wait=(a1, a2) + ((t1prev,) if t1prev else ()), inc="DV")
                d2 = p.do("vector", "tensor_tensor", b.t1[:, :], b.t1[:, :],
                          ln.c_t[:, :], op=ALU.add, wait=(d1,), inc="DV")
                xu = p.do("vector", "tensor_tensor", b.xf[c % 2][:, :],
                          b.xf[c % 2][:, :], b.t1[:, :], op=ALU.add,
                          wait=(d2, xl), inc="DV")
                t1prev = xu
                stt = p.load(x_out[c, :, ot * 512:(ot + 1) * 512],
                             b.xf[c % 2][:, :], eng="gpsimd", sem=f"SX{c % 2}",
                             wait=(xu,))
                sttoks.append(stt)
        ar.pop()
        ar.pop()

        ar.push()
        b.xn = ar.alloc([128, NCH * 512], BF16)
        b.h = ar.alloc([128, HC2 * 512], BF16)
        b.yb = b.xn
        b.gsil = ar.alloc([128, 512], BF16)
        b.usc = ar.alloc([128, 512], BF16)
        b.wG = [ar.alloc([128, D], BF16) for _ in range(2)]
        b.wU = [ar.alloc([128, D], BF16) for _ in range(2)]
        b.wD = [ar.alloc([128, HB2], BF16) for _ in range(2)]
        b.pk = ar.alloc([128, HSH], I32)
        b.scl = ar.alloc([128, HSH], F32)
        b.nib = ar.alloc([128, HSH], I32)
        b.ib = ar.alloc([128, HSH], I32)
        b.pf = ar.alloc([128, HSH], F32)
        b.r1 = ar.alloc([128, HSH], F32)
        b.r2 = ar.alloc([128, HSH], F32)
        b.v2 = ar.alloc([128, HSH], F32)
        b.sf = ar.alloc([128, HSH], F32)
        b.ob = [ar.alloc([128, HSH], BF16) for _ in range(2)]
        b.n4col = ar.alloc([128, 1], F32)
        b.n6col = ar.alloc([128, 1], F32)
        wfree = {"G": [None, None], "U": [None, None], "D": [None, None]}
        PH2 = (p.now("PE"), p.now("DV"), p.now("AC"),
               p.now("SX0"), p.now("SX1"), p.now("XI0"), p.now("XI1"),
               p.now("WB0"), p.now("WB1"))
        DQB = PH2 + (p.now("GP"), p.now("ST"))
        m1 = p.do("gpsimd", "memset", b.n4col[:, :], -4.0, wait=DQB, inc="GP")
        m2 = p.do("gpsimd", "memset", b.n6col[:, :], -6.0, inc="GP")
        MGP = p.now("GP")

        def dq_gen():
            obst = [None, None]
            prevs = None
            groups = [(gup_sh, gup_scl, wdq_gu, HSH, [(0, 128)]),
                      (gup_sh, gup_scl, wdq_gu, HSH, [(1, 128)]),
                      (dn_sh, dn_scl, wdq_dn, ESH, [(0, 128), (1, 128)]),
                      (dn_sh, dn_scl, wdq_dn, ESH, [(2, 128), (3, 128)]),
                      (dn_sh, dn_scl, wdq_dn, ESH, [(4, 128), (5, 64)])]
            for src_pk, src_scl, dst, Wc, tiles in groups:
                Wt = Wc * len(tiles)
                for i, (wt, nw) in enumerate(tiles):
                    p.load(b.pk[:, i * Wc:(i + 1) * Wc], src_pk[wt, :, 0:Wc],
                           eng="sync", sem="PK",
                           wait=((prevs if prevs else DQB) if i == 0 else ()))
                    p.load(b.scl[:, i * Wc:(i + 1) * Wc], src_scl[wt, :, 0:Wc],
                           sem="PK2",
                           wait=((prevs if prevs else DQB) if i == 0 else ()))
                PKL = (p.now("PK"), p.now("PK2"))
                for j in range(8):
                    dv = p.do("vector", "tensor_scalar", b.nib[:, 0:Wt],
                              b.pk[:, 0:Wt], 4 * j, 15,
                              op0=ALU.logical_shift_right,
                              op1=ALU.bitwise_and, wait=PKL + (MGP,), inc="DV")
                    i1 = p.do("vector", "tensor_scalar", b.ib[:, 0:Wt],
                              b.nib[:, 0:Wt], 7, None,
                              op0=ALU.bitwise_and, wait=(dv,), inc="DV")
                    pf = p.do("vector", "tensor_copy", b.pf[:, 0:Wt],
                              b.ib[:, 0:Wt], wait=(i1,), inc="DV")
                    rr1 = p.do("scalar", "activation", b.r1[:, 0:Wt],
                               b.pf[:, 0:Wt], AF.Relu, bias=b.n4col[:, :],
                               scale=1.0, wait=(pf,), inc="AC")
                    rr2 = p.do("scalar", "activation", b.r2[:, 0:Wt],
                               b.pf[:, 0:Wt], AF.Relu, bias=b.n6col[:, :],
                               scale=1.0, wait=(pf,), inc="AC")
                    sg = p.do("vector", "tensor_scalar", b.ib[:, 0:Wt],
                              b.nib[:, 0:Wt], 8, 28, op0=ALU.bitwise_and,
                              op1=ALU.logical_shift_left, wait=(pf,), inc="DV")
                    ss = p.do("vector", "tensor_tensor",
                              b.sf[:, 0:Wt].bitcast(I32),
                              b.scl[:, 0:Wt].bitcast(I32), b.ib[:, 0:Wt],
                              op=ALU.bitwise_xor, wait=(sg,), inc="DV")
                    v1 = p.do("vector", "scalar_tensor_tensor",
                              b.r1[:, 0:Wt], b.r1[:, 0:Wt], 0.5,
                              b.r2[:, 0:Wt], op0=ALU.mult, op1=ALU.add,
                              wait=(rr1, rr2), inc="DV")
                    v2 = p.do("vector", "scalar_tensor_tensor",
                              b.v2[:, 0:Wt], b.pf[:, 0:Wt], 0.5,
                              b.r1[:, 0:Wt], op0=ALU.mult, op1=ALU.add,
                              wait=(v1,), inc="DV")
                    jp = j % 2
                    vs = p.do("vector", "tensor_tensor", b.ob[jp][:, 0:Wt],
                              b.v2[:, 0:Wt], b.sf[:, 0:Wt], op=ALU.mult,
                              wait=(v2, ss) + ((obst[jp],) if obst[jp] else ()),
                              inc="DV")
                    st = None
                    for i, (wt, nw) in enumerate(tiles):
                        seg = dst[wt * 1024: wt * 1024 + nw * 8, :].rearrange(
                            "(w j) n -> w j n", j=8)
                        st = p.do("sync", "dma_start",
                                  out=seg[:, j, 0:Wc],
                                  in_=b.ob[jp][0:nw, i * Wc:(i + 1) * Wc],
                                  inc=f"PD{jp}", inck=16,
                                  wait=(vs,) if i == 0 else ())
                    obst[jp] = st
                    yield
                prevs = (p.now("PD0"), p.now("PD1"))

        dq = dq_gen()

        def dq_step():
            try:
                next(dq)
            except StopIteration:
                pass

        for ot in range(2):
            OTB = PH2 if ot == 0 else (p.now("PE"), p.now("DV"), p.now("AC"),
                                       p.now("SX0"), p.now("SX1"))
            xtoks = ln1_into(b.xn, ot, OTB, xg="XH")
            prev_hd = None
            for hc in range(HC2):
                pb = hc % 2
                wgl = p.load(b.wG[pb][:, :], wgu2[hc, :, :], sem=f"WG{pb}",
                             wait=(wfree["G"][pb],) if wfree["G"][pb] else ())
                wul = p.load(b.wU[pb][:, :], wgu2[HC2 + hc, :, :], sem=f"WU{pb}",
                             wait=(wfree["U"][pb],) if wfree["U"][pb] else ())
                w0 = bk.w(0)
                for c in range(NCH):
                    p.do("tensor", "matmul", bk.ap(0)[:, :],
                         b.wG[pb][:, c * 128:(c + 1) * 128],
                         b.xn[:, c * 512:(c + 1) * 512], start=(c == 0),
                         stop=(c == NCH - 1),
                         wait=((wgl, xtoks[-1]) + w0) if c == 0 else (),
                         inc="PE" if c == NCH - 1 else None)
                g_pe = p.now("PE")
                w1 = bk.w(1)
                for c in range(NCH):
                    p.do("tensor", "matmul", bk.ap(1)[:, :],
                         b.wU[pb][:, c * 128:(c + 1) * 128],
                         b.xn[:, c * 512:(c + 1) * 512], start=(c == 0),
                         stop=(c == NCH - 1),
                         wait=((wul,) + w1) if c == 0 else (),
                         inc="PE" if c == NCH - 1 else None)
                u_pe = p.now("PE")
                ga = p.do("scalar", "activation", b.gsil[:, :], bk.ap(0)[:, :],
                          AF.Silu, bias=0.0, scale=1.0,
                          wait=(g_pe,) + ((prev_hd,) if prev_hd else ()), inc="AC")
                bk.r(0, ga)
                wfree["G"][pb] = ga
                ua = p.do("scalar", "copy", b.usc[:, :], bk.ap(1)[:, :],
                          wait=(u_pe,), inc="AC")
                bk.r(1, ua)
                wfree["U"][pb] = ua
                prev_hd = p.do("vector", "tensor_tensor",
                               b.h[:, hc * 512:(hc + 1) * 512], b.gsil[:, :],
                               b.usc[:, :], op=ALU.mult, wait=(ga, ua), inc="DV")
                if ot == 1 and hc % 2 == 0:
                    dq_step()
            HDONE = p.now("DV")
            ytoks = []
            for ec in range(NCH):
                pb = ec % 2
                wdl = p.load(b.wD[pb][:, :], wdn2[ec, :, :], sem=f"WD{pb}",
                             wait=(wfree["D"][pb],) if wfree["D"][pb] else ())
                w2 = bk.w(2)
                for hc in range(HC2):
                    p.do("tensor", "matmul", bk.ap(2)[:, :],
                         b.wD[pb][:, hc * 128:(hc + 1) * 128],
                         b.h[:, hc * 512:(hc + 1) * 512], start=(hc == 0),
                         stop=(hc == HC2 - 1),
                         wait=((wdl, HDONE) + w2) if hc == 0 else (),
                         inc="PE" if hc == HC2 - 1 else None)
                d_pe = p.now("PE")
                ya = p.do("scalar", "copy", b.yb[:, ec * 512:(ec + 1) * 512],
                          bk.ap(2)[:, :], wait=(d_pe,), inc="AC")
                bk.r(2, ya)
                wfree["D"][pb] = ya
                ytoks.append(ya)
                dq_step()
            ln2_residual_store(b.yb, ot, ytoks, x_out, xg="XI")
        for _ in dq:
            pass
        ar.pop()

        ar.push()
        b.xn4 = ar.alloc([128, NCH * 512], BF16)
        for ot in range(2):
            OTB = (p.now("PE"), p.now("DV"), p.now("AC"), p.now("ST"),
                   p.now("SX0"), p.now("SX1"), p.now("PD0"), p.now("PD1"))
            xtoks = ln1_into(b.xn4, ot, OTB, xg="XH", inplace=True)
            for c in range(NCH):
                p.load(xn4_out[c, :, ot * 512:(ot + 1) * 512],
                       b.xn4[:, c * 512:(c + 1) * 512], eng="gpsimd", sem="ST",
                       wait=(xtoks[c],))
        ar.pop()

        p.do("gpsimd", "nop", wait=(p.now("ST"), p.now("SX0"), p.now("SX1"),
                                    p.now("PD0"), p.now("PD1")))
        p.emit()
    return nc


def prep_launch3(I, x3_pc, a2p_pc, qd_pc, sums_pc):
    com = {}
    wo = I["L2_attn_Wo"]
    img = np.empty((NCH, 128, D), np.float32)
    for ec in range(NCH):
        img[ec] = _img_dmaj(wo[:, ec * 128:(ec + 1) * 128])
    com["wo2"] = np.ascontiguousarray(img)
    com["rmsw"] = np.ascontiguousarray(I["L2_attn_rms_w"].reshape(NCH, 128).T)
    guT = I["L2_ffn_w_gate_up"].T
    com["wgu2"] = np.ascontiguousarray(
        guT.reshape(NCH, 128, 2 * HC2, 128).transpose(2, 1, 0, 3)).reshape(2 * HC2, 128, D)
    dnT = I["L2_ffn_w_down"].T
    com["wdn2"] = np.ascontiguousarray(
        dnT.reshape(HC2, 128, NCH, 128).transpose(2, 1, 0, 3)).reshape(NCH, 128, HB2)
    sums_glob = np.empty((16, NHEAD, 128, 128), BF)
    for c in range(NC):
        a, bb = zz_chunks(c)
        sums_glob[a] = sums_pc[c][0]
        sums_glob[bb] = sums_pc[c][1]
    com["sums"] = np.ascontiguousarray(
        sums_glob.transpose(2, 0, 1, 3)).reshape(128, 16 * NHEAD * 128)
    rc = _l2_rc()
    gupT = I["L3_ffn_gate_up_packed"].T
    dnpT = I["L3_ffn_down_packed"].T
    gsc = I["L3_ffn_gate_up_scales"]
    dsc = I["L3_ffn_down_scales"]
    in_maps = []
    for c in range(NC):
        a, bb = zz_chunks(c)
        im = dict(com)
        im["xin"] = x3_pc[c]
        im["a2p"] = a2p_pc[c]
        im["qd"] = qd_pc[c]
        wd = np.zeros((2, 16, NHEAD), np.float32)
        for si, g in enumerate((a, bb)):
            for gp in range(16):
                if gp < g:
                    wd[si, gp] = (rc ** (4.0 * (g - 1 - gp))).astype(np.float32)
        im["wdec"] = np.ascontiguousarray(
            np.broadcast_to(wd.reshape(1, -1), (128, 2 * 16 * NHEAD)).copy())
        gs = gupT[:, c * HSH:(c + 1) * HSH]
        im["gup_sh"] = np.ascontiguousarray(gs.reshape(2, 128, HSH))
        scl = gsc[:, c * HSH:(c + 1) * HSH]
        scl_rows = np.repeat(scl, 16, axis=0)
        im["gup_scl"] = np.ascontiguousarray(scl_rows.reshape(2, 128, HSH))
        ds = dnpT[:, c * ESH:(c + 1) * ESH]
        dsp = np.zeros((768, ESH), np.int32)
        dsp[:704] = ds
        im["dn_sh"] = np.ascontiguousarray(dsp.reshape(6, 128, ESH))
        dscl = dsc[:, c * ESH:(c + 1) * ESH]
        dscl_rows = np.zeros((768, ESH), np.float16)
        dscl_rows[:704] = np.repeat(dscl, 16, axis=0)
        im["dn_scl"] = np.ascontiguousarray(dscl_rows.reshape(6, 128, ESH))
        in_maps.append(im)
    return in_maps

HC3 = 44


def build_launch4():
    nc = bass.Bass("TRN2", target_bir_lowering=False, debug=False)
    stack = ExitStack()
    nc._stk = stack

    xin = nc.dram_tensor("xin", [NCH, 128, TC], F32, kind="ExternalInput").ap()
    xn4 = nc.dram_tensor("xn4", [NCH, 128, TC], BF16, kind="ExternalInput").ap()
    Xtm = nc.dram_tensor("Xtm", [2, 128, D], BF16, kind="ExternalInput").ap()
    Xfm = nc.dram_tensor("Xfm", [128, NCH * 256], BF16, kind="ExternalInput").ap()
    blT = nc.dram_tensor("blT", [2, 128, 256], BF16, kind="ExternalInput").ap()
    clT = nc.dram_tensor("clT", [2, 128, 256], BF16, kind="ExternalInput").ap()
    alT = nc.dram_tensor("alT", [2, 128, 256], BF16, kind="ExternalInput").ap()
    rl = nc.dram_tensor("rl", [128, 1], F32, kind="ExternalInput").ap()
    blendc = nc.dram_tensor("blendc", [128, TC], F32, kind="ExternalInput").ap()
    omblendc = nc.dram_tensor("omblendc", [128, TC], F32, kind="ExternalInput").ap()
    wgu3 = nc.dram_tensor("wgu3", [2 * HC3, 128, D], BF16, kind="ExternalInput").ap()
    wdn3 = nc.dram_tensor("wdn3", [NCH, 128, I3], BF16, kind="ExternalInput").ap()

    x_out = nc.dram_tensor("x_out", [NCH, 128, TC], F32, kind="ExternalOutput").ap()

    with stack:
        p = Prog(nc, stack)
        ar = Arena(nc, stack, "arena", 206 * 1024)
        bk = Banks(nc, stack)
        b = B()

        b.ones_col_bf = ar.alloc([128, 1], BF16)
        b.epscol = ar.alloc([128, 1], F32)
        b.ones_row_f = ar.alloc([1, 128], F32)
        b.ones_row_bf = ar.alloc([1, 128], BF16)
        b.sq = ar.alloc([128, 512], BF16)
        b.t1 = ar.alloc([128, 512], F32)
        b.xf = [ar.alloc([128, 512], F32) for _ in range(2)]
        b.idb = ar.alloc([128, 128], BF16)
        ln = LNH(p, b, ar)
        identb_dram = nc.inline_tensor(np.eye(128, dtype=BF), "idb_c").ap()
        zero_dram = nc.inline_tensor(np.zeros((1, 1), dtype=BF), "z_c").ap()
        p.load(b.idb[:, :], identb_dram[:, :], sem="LC")
        p.do("gpsimd", "memset", b.ones_col_bf[:, :], 1.0, inc="GP")
        p.do("gpsimd", "memset", b.ones_row_f[:, :], 1.0, inc="GP")
        p.do("gpsimd", "memset", b.ones_row_bf[:, :], 1.0, inc="GP")
        p.do("gpsimd", "memset", b.epscol[:, :], 1e-5, inc="GP")
        CGP = p.now("GP")
        CLD = p.now("LC")

        def ln1_into(dstbuf, ot, first_wait, xg="XH", src_dram=None):
            sd = src_dram if src_dram is not None else x_out
            xls = []
            for c in range(NCH):
                xl = p.load(dstbuf[:, c * 512:(c + 1) * 512],
                            sd[c, :, ot * 512:(ot + 1) * 512],
                            sem=f"{xg}{c % 2}",
                            wait=(xls[c - 2],) if c >= 2 else first_wait)
                xls.append(xl)
            srcs = [(dstbuf[:, c * 512:(c + 1) * 512], xls[c])
                    for c in range(NCH)]
            ln.stats(p, bk, srcs, 6, 7)
            a1, a2 = ln.bcast(p, bk, 4, 5)
            xtoks = []
            t1prev = None
            for c in range(NCH):
                xl = p.load(b.xf[c % 2][:, :], sd[c, :, ot * 512:(ot + 1) * 512],
                            sem=f"{xg}{c % 2}",
                            wait=(xtoks[c - 2],) if c >= 2 else (a1, a2))
                d1 = p.do("vector", "tensor_tensor", b.t1[:, :], b.xf[c % 2][:, :],
                          ln.rs_t[:, :], op=ALU.mult,
                          wait=(xl, a1, a2) + ((t1prev,) if t1prev else ()), inc="DV")
                d2 = p.do("vector", "tensor_tensor",
                          dstbuf[:, c * 512:(c + 1) * 512], b.t1[:, :],
                          ln.c_t[:, :], op=ALU.add, wait=(d1,), inc="DV")
                t1prev = d2
                xtoks.append(d2)
            return xtoks

        ar.push()
        b.att = ar.alloc([128, NCH * D], BF16)
        b.xn4s = ar.alloc([128, NCH * TC], BF16)
        b.Xout = ar.alloc([128, NCH * TC], BF16)
        b.Xt = ar.alloc([128, 2 * D], BF16)
        b.Xf = ar.alloc([128, NCH * 256], BF16)
        b.rl = ar.alloc([128, 1], F32)
        b.bl = ar.alloc([128, TC], F32)
        b.omb = ar.alloc([128, TC], F32)
        b.tb = ar.alloc([128, 512], BF16)
        b.tb2 = ar.alloc([128, 512], BF16)
        ar.push()
        b.Ub = ar.alloc([128, 2 * D], BF16)
        b.Wb = ar.alloc([128, 2 * D], BF16)
        b.bcl = [ar.alloc([128, 2 * 256], BF16) for _ in range(3)]

        p.load(b.Xt[:, 0:D], Xtm[0, :, :], sem="LA")
        p.load(b.Xt[:, D:2 * D], Xtm[1, :, :], sem="LA")
        p.load(b.Xf[:, :], Xfm[:, :], sem="LA")
        p.load(b.bcl[0][:, :].rearrange("p (c n) -> p c n", c=2), blT.rearrange("c p n -> p c n"), sem="LA")
        p.load(b.bcl[1][:, :].rearrange("p (c n) -> p c n", c=2), clT.rearrange("c p n -> p c n"), sem="LA")
        p.load(b.bcl[2][:, :].rearrange("p (c n) -> p c n", c=2), alT.rearrange("c p n -> p c n"), sem="LA")
        p.load(b.rl[:, :], rl[:, :], sem="LA")
        p.load(b.bl[:, :], blendc[:, :], sem="LA")
        p.load(b.omb[:, :], omblendc[:, :], sem="LA")
        for ch in range(NCH):
            p.load(b.xn4s[:, ch * TC:(ch + 1) * TC], xn4[ch, :, :], sem="XN")
        XNL = p.now("XN")
        ALD = p.now("LA")
        zpt = p.do("gpsimd", "memset", b.Xout[:, :], 0.0, inc="GP")

        for dst, w_ in ((b.Ub, b.bcl[0]), (b.Wb, b.bcl[1])):
            for ic in range(2):
                for mt in range(4):
                    bi = mt % 2
                    ww = bk.w(bi)
                    for jc in range(2):
                        p.do("tensor", "matmul", bk.ap(bi)[:, :],
                             w_[:, jc * 128 + 0:jc * 128 + 128][:, 0:128] if False else w_.rearrange("p (c n) -> p c n", c=2)[:, jc, ic * 128:(ic + 1) * 128],
                             b.Xt[:, jc * D + mt * 512: jc * D + (mt + 1) * 512],
                             start=(jc == 0), stop=(jc == 1),
                             wait=((ALD, CGP) + ww) if jc == 0 else (),
                             inc="PE" if jc == 1 else None)
                    tpe = p.now("PE")
                    a = p.do("scalar", "copy",
                             dst[:, ic * D + mt * 512: ic * D + (mt + 1) * 512],
                             bk.ap(bi)[:, :], wait=(tpe,), inc="AC")
                    bk.r(bi, a)
        UWT = p.now("AC")
        for ncn in range(NCH):
            for mt in range(4):
                bi = mt % 2
                ww = bk.w(bi)
                for ic in range(2):
                    p.do("tensor", "matmul", bk.ap(bi)[:, :],
                         b.Ub[:, ic * D + ncn * 128: ic * D + (ncn + 1) * 128],
                         b.Wb[:, ic * D + mt * 512: ic * D + (mt + 1) * 512],
                         start=(ic == 0), stop=(ic == 1),
                         wait=((UWT,) + ww) if ic == 0 else (),
                         inc="PE" if ic == 1 else None)
                tpe = p.now("PE")
                a = p.do("scalar", "activation",
                         b.att[:, ncn * D + mt * 512: ncn * D + (mt + 1) * 512],
                         bk.ap(bi)[:, :], AF.Relu, bias=0.0, scale=1.0,
                         wait=(tpe,), inc="AC")
                bk.r(bi, a)
        ATT = p.now("AC")
        zfix = p.do("gpsimd", "dma_start", out=b.att[127:128, 15 * D + 2047: 15 * D + 2048],
                    in_=zero_dram[0:1, 0:1], inc="ZF", inck=16, wait=(ATT,))
        b.Xat = b.Ub
        for dc in range(2):
            for mt in range(4):
                bi = mt % 2
                ww = bk.w(bi)
                for ncn in range(NCH):
                    p.do("tensor", "matmul", bk.ap(bi)[:, :],
                         b.Xf[:, ncn * 256 + dc * 128: ncn * 256 + (dc + 1) * 128],
                         b.att[:, ncn * D + mt * 512: ncn * D + (mt + 1) * 512],
                         start=(ncn == 0), stop=(ncn == NCH - 1),
                         wait=((zfix,) + ww) if ncn == 0 else (),
                         inc="PE" if ncn == NCH - 1 else None)
                tpe = p.now("PE")
                a = p.do("scalar", "copy",
                         b.Xat[:, dc * D + mt * 512: dc * D + (mt + 1) * 512],
                         bk.ap(bi)[:, :], wait=(tpe,), inc="AC")
                bk.r(bi, a)
        XAT = p.now("AC")
        b.Xo_rm = b.Wb
        prev_d = None
        for ic in range(2):
            for mt in range(4):
                bi = mt % 2
                ww = bk.w(bi)
                for jc in range(2):
                    p.do("tensor", "matmul", bk.ap(bi)[:, :],
                         b.bcl[2].rearrange("p (c n) -> p c n", c=2)[:, jc, ic * 128:(ic + 1) * 128],
                         b.Xat[:, jc * D + mt * 512: jc * D + (mt + 1) * 512],
                         start=(jc == 0), stop=(jc == 1),
                         wait=((XAT,) + ww) if jc == 0 else (),
                         inc="PE" if jc == 1 else None)
                tpe = p.now("PE")
                a = p.do("scalar", "copy", b.tb[:, :], bk.ap(bi)[:, :],
                         wait=(tpe,) + ((prev_d,) if prev_d else ()), inc="AC")
                bk.r(bi, a)
                d = p.do("vector", "tensor_tensor",
                         b.Xo_rm[:, ic * D + mt * 512: ic * D + (mt + 1) * 512],
                         b.Xt[:, ic * D + mt * 512: ic * D + (mt + 1) * 512],
                         b.tb[:, :], op=ALU.add, wait=(a,), inc="DV")
                prev_d = d
        XORM = p.now("DV")
        ps_t = bk.ap(7).bitcast(BF16)
        for jc in range(2):
            for fc in range(NCH):
                tt = p.do("tensor", "transpose", ps_t[:, 0:128],
                          b.Xo_rm[:, jc * D + fc * 128: jc * D + (fc + 1) * 128],
                          b.idb[:, :], wait=(XORM, CLD) + bk.w(7), inc="PE")
                a = p.do("scalar", "copy",
                         b.Xout[:, fc * TC + jc * 128: fc * TC + (jc + 1) * 128],
                         ps_t[:, 0:128], wait=(tt, zpt), inc="AC")
                bk.r(7, a)
        XOT = p.now("AC")
        ar.pop()
        b.ybl = ar.alloc([128, NCH * 512], BF16)
        b.tbp = [ar.alloc([128, 512], BF16) for _ in range(2)]
        b.tb2p = [ar.alloc([128, 512], BF16) for _ in range(2)]

        tbfp = [None, None]
        for ot in range(2):
            ytoks = []
            for mc in range(NCH):
                bi = mc % 2
                tbA = b.tbp[mc % 2]
                tbB = b.tb2p[mc % 2]
                ww = bk.w(bi)
                for ncn in range(NCH):
                    p.do("tensor", "matmul", bk.ap(bi)[:, :],
                         b.att[:, ncn * D + mc * 128: ncn * D + (mc + 1) * 128],
                         b.xn4s[:, ncn * TC + ot * 512: ncn * TC + (ot + 1) * 512],
                         start=(ncn == 0), stop=(ncn == NCH - 1),
                         wait=((XOT, XNL) + ww) if ncn == 0 else (),
                         inc="PE" if ncn == NCH - 1 else None)
                tpe = p.now("PE")
                a = p.do("scalar", "activation", tbA[:, :], bk.ap(bi)[:, :],
                         AF.Identity, bias=0.0, scale=b.rl[:, :],
                         wait=(tpe,) + ((tbfp[mc % 2],) if tbfp[mc % 2] else ()),
                         inc="AC")
                bk.r(bi, a)
                d1 = p.do("vector", "tensor_tensor", tbB[:, :],
                          b.xn4s[:, mc * TC + ot * 512: mc * TC + (ot + 1) * 512],
                          tbA[:, :], op=ALU.add, wait=(a,), inc="DV")
                d2 = p.do("vector", "tensor_tensor", tbB[:, :], tbB[:, :],
                          b.omb[:, ot * 512:(ot + 1) * 512], op=ALU.mult,
                          wait=(d1,), inc="DV")
                d3 = p.do("vector", "tensor_tensor", tbA[:, :],
                          b.Xout[:, mc * TC + ot * 512: mc * TC + (ot + 1) * 512],
                          b.bl[:, ot * 512:(ot + 1) * 512], op=ALU.mult,
                          wait=(d2,), inc="DV")
                tbf = p.do("vector", "tensor_tensor",
                           b.ybl[:, mc * 512:(mc + 1) * 512], tbB[:, :],
                           tbA[:, :], op=ALU.add, wait=(d3,), inc="DV")
                tbfp[mc % 2] = tbf
                ytoks.append(tbf)
            SXB = (p.now("SX0"), p.now("SX1"))
            srcs = [(b.ybl[:, c * 512:(c + 1) * 512], ytoks[c]) for c in range(NCH)]
            ln.stats(p, bk, srcs, 6, 7)
            a1, a2 = ln.bcast(p, bk, 4, 5)
            t1prev = None
            sttoks = []
            for c in range(NCH):
                xl = p.load(b.xf[c % 2][:, :], xin[c, :, ot * 512:(ot + 1) * 512],
                            sem=f"XI{c % 2}",
                            wait=(sttoks[c - 2],) if c >= 2 else (a1, a2) + SXB)
                d1 = p.do("vector", "tensor_tensor", b.t1[:, :],
                          b.ybl[:, c * 512:(c + 1) * 512], ln.rs_t[:, :],
                          op=ALU.mult, wait=(a1, a2) + ((t1prev,) if t1prev else ()),
                          inc="DV")
                d2 = p.do("vector", "tensor_tensor", b.t1[:, :], b.t1[:, :],
                          ln.c_t[:, :], op=ALU.add, wait=(d1,), inc="DV")
                xu = p.do("vector", "tensor_tensor", b.xf[c % 2][:, :],
                          b.xf[c % 2][:, :], b.t1[:, :], op=ALU.add,
                          wait=(d2, xl), inc="DV")
                t1prev = xu
                stt = p.load(x_out[c, :, ot * 512:(ot + 1) * 512],
                             b.xf[c % 2][:, :], eng="gpsimd", sem=f"SX{c % 2}",
                             wait=(xu,))
                sttoks.append(stt)
        ar.pop()

        ar.push()
        b.xn = ar.alloc([128, NCH * 512], BF16)
        b.h = ar.alloc([128, HC3 * 512], BF16)
        b.yb = b.xn
        b.gsil = ar.alloc([128, 512], BF16)
        b.usc = ar.alloc([128, 512], BF16)
        b.wG = [ar.alloc([128, D], BF16) for _ in range(2)]
        b.wU = [ar.alloc([128, D], BF16) for _ in range(2)]
        b.wD = [ar.alloc([128, I3], BF16) for _ in range(2)]
        wfree = {"G": [None, None], "U": [None, None], "D": [None, None]}
        for ot in range(2):
            OTB = (p.now("PE"), p.now("DV"), p.now("AC"),
                   p.now("SX0"), p.now("SX1"), p.now("XI0"), p.now("XI1"))
            xtoks = ln1_into(b.xn, ot, OTB, xg="XH")
            WBar = OTB + (p.now("XN"), p.now("LC"))
            prev_hd = None
            for hc in range(HC3):
                pb = hc % 2
                wgl = p.load(b.wG[pb][:, :], wgu3[hc, :, :], eng="sync",
                             sem=f"WG{pb}",
                             wait=((wfree["G"][pb],) if wfree["G"][pb] else ()) + (WBar if hc < 2 else ()))
                wul = p.load(b.wU[pb][:, :], wgu3[HC3 + hc, :, :], eng="sync",
                             sem=f"WU{pb}",
                             wait=((wfree["U"][pb],) if wfree["U"][pb] else ()) + (WBar if hc < 2 else ()))
                w0 = bk.w(0)
                for c in range(NCH):
                    p.do("tensor", "matmul", bk.ap(0)[:, :],
                         b.wG[pb][:, c * 128:(c + 1) * 128],
                         b.xn[:, c * 512:(c + 1) * 512], start=(c == 0),
                         stop=(c == NCH - 1),
                         wait=((wgl, xtoks[-1]) + w0) if c == 0 else (),
                         inc="PE" if c == NCH - 1 else None)
                g_pe = p.now("PE")
                w1 = bk.w(1)
                for c in range(NCH):
                    p.do("tensor", "matmul", bk.ap(1)[:, :],
                         b.wU[pb][:, c * 128:(c + 1) * 128],
                         b.xn[:, c * 512:(c + 1) * 512], start=(c == 0),
                         stop=(c == NCH - 1),
                         wait=((wul,) + w1) if c == 0 else (),
                         inc="PE" if c == NCH - 1 else None)
                u_pe = p.now("PE")
                ga = p.do("scalar", "activation", b.gsil[:, :], bk.ap(0)[:, :],
                          AF.Silu, bias=0.0, scale=1.0,
                          wait=(g_pe,) + ((prev_hd,) if prev_hd else ()), inc="AC")
                bk.r(0, ga)
                wfree["G"][pb] = ga
                ua = p.do("scalar", "copy", b.usc[:, :], bk.ap(1)[:, :],
                          wait=(u_pe,), inc="AC")
                bk.r(1, ua)
                wfree["U"][pb] = ua
                prev_hd = p.do("vector", "tensor_tensor",
                               b.h[:, hc * 512:(hc + 1) * 512], b.gsil[:, :],
                               b.usc[:, :], op=ALU.mult, wait=(ga, ua), inc="DV")
            HDONE = p.now("DV")
            ytoks = []
            for ec in range(NCH):
                pb = ec % 2
                wdl = p.load(b.wD[pb][:, :], wdn3[ec, :, :], eng="sync",
                             sem=f"WD{pb}",
                             wait=((wfree["D"][pb],) if wfree["D"][pb] else ()) + (WBar if ec < 2 else ()))
                w2 = bk.w(2)
                for hc in range(HC3):
                    p.do("tensor", "matmul", bk.ap(2)[:, :],
                         b.wD[pb][:, hc * 128:(hc + 1) * 128],
                         b.h[:, hc * 512:(hc + 1) * 512], start=(hc == 0),
                         stop=(hc == HC3 - 1),
                         wait=((wdl, HDONE) + w2) if hc == 0 else (),
                         inc="PE" if hc == HC3 - 1 else None)
                d_pe = p.now("PE")
                ya = p.do("scalar", "copy", b.yb[:, ec * 512:(ec + 1) * 512],
                          bk.ap(2)[:, :], wait=(d_pe,), inc="AC")
                bk.r(2, ya)
                wfree["D"][pb] = ya
                ytoks.append(ya)
            SXB = (p.now("SX0"), p.now("SX1"))
            srcs = [(b.yb[:, c * 512:(c + 1) * 512], ytoks[c]) for c in range(NCH)]
            bkw6 = bk.w(6); bkw7 = bk.w(7)
            sq_prev = None
            for c, (src, wt) in enumerate(srcs):
                st = dict(start=(c == 0), stop=(c == NCH - 1))
                p.do("tensor", "matmul", bk.ap(6)[0:1, :], b.ones_col_bf[:, :],
                     src, wait=((wt,) if wt else ()) + (bkw6 if c == 0 else ()),
                     **st, inc="PE" if c == NCH - 1 else None)
                at = p.do("scalar", "activation", b.sq[:, :], src, AF.Square,
                          wait=((wt,) if wt else ()) + ((sq_prev,) if sq_prev else ()),
                          inc="AC")
                sq_prev = p.do("tensor", "matmul", bk.ap(7)[0:1, :],
                               b.ones_col_bf[:, :], b.sq[:, :],
                               wait=(at,) + (bkw7 if c == 0 else ()), **st,
                               inc="PE")
            pe = p.now("PE")
            d1 = p.do("vector", "tensor_scalar", ln.vm[:, :], bk.ap(6)[0:1, :],
                      1.0 / D, None, op0=ALU.mult, wait=(pe,), inc="DV")
            d2 = p.do("vector", "tensor_scalar", ln.vrs[:, :], bk.ap(7)[0:1, :],
                      1.0 / D, None, op0=ALU.mult, wait=(pe,), inc="DV")
            bk.r(6, d1); bk.r(7, d2)
            d3 = p.do("vector", "tensor_scalar", ln.vneg[:, :], ln.vrs[:, :],
                      1e-6, None, op0=ALU.add, wait=(d2,), inc="DV")
            d4 = p.do("vector", "tensor_tensor", ln.vsd[:, :], ln.vm[:, :],
                      ln.vm[:, :], op=ALU.mult, wait=(d1,), inc="DV")
            d5 = p.do("vector", "tensor_tensor", ln.vrs[:, :], ln.vrs[:, :],
                      ln.vsd[:, :], op=ALU.subtract, wait=(d2, d4), inc="DV")
            d6 = p.do("vector", "scalar_tensor_tensor", ln.vrs[:, :],
                      ln.vneg[:, :], 1e-5, ln.vrs[:, :], op0=ALU.mult,
                      op1=ALU.add, wait=(d3, d5), inc="DV")
            sa = p.do("scalar", "activation", ln.vsd[:, :], ln.vrs[:, :],
                      AF.Sqrt, bias=0.0, scale=1.0, wait=(d6,), inc="AC")
            d7 = p.do("vector", "reciprocal", ln.vrs[:, :], ln.vsd[:, :],
                      wait=(sa,), inc="DV")
            d8 = p.do("vector", "tensor_scalar", ln.vneg[:, :], ln.vm[:, :],
                      -1.0, None, op0=ALU.mult, wait=(d4,), inc="DV")
            ln.rows = (d7, d8)
            a1, a2 = ln.bcast(p, bk, 4, 5)
            t1prev = None
            sttoks = []
            for c in range(NCH):
                xl = p.load(b.xf[c % 2][:, :], x_out[c, :, ot * 512:(ot + 1) * 512],
                            sem=f"XI{c % 2}",
                            wait=(sttoks[c - 2],) if c >= 2 else (a1, a2) + SXB)
                d1 = p.do("vector", "tensor_tensor", b.t1[:, :],
                          b.yb[:, c * 512:(c + 1) * 512], ln.rs_t[:, :],
                          op=ALU.mult, wait=(a1, a2) + ((t1prev,) if t1prev else ()),
                          inc="DV")
                d2 = p.do("vector", "tensor_tensor", b.t1[:, :], b.t1[:, :],
                          ln.c_t[:, :], op=ALU.add, wait=(d1,), inc="DV")
                xu = p.do("vector", "tensor_tensor", b.xf[c % 2][:, :],
                          b.xf[c % 2][:, :], b.t1[:, :], op=ALU.add,
                          wait=(d2, xl), inc="DV")
                t1prev = xu
                stt = p.load(x_out[c, :, ot * 512:(ot + 1) * 512],
                             b.xf[c % 2][:, :], eng="gpsimd", sem=f"SX{c % 2}",
                             wait=(xu,))
                sttoks.append(stt)
        ar.pop()

        p.do("gpsimd", "nop", wait=(p.now("SX0"), p.now("SX1"), p.now("ZF")))
        p.emit()
    return nc


def prep_launch4(I, x5_pc, xn4_pc, Xn_fm):
    com = {}
    Xf = np.ascontiguousarray(
        Xn_fm.transpose(1, 0, 2)).reshape(128, NCH * 256)
    com["Xfm"] = Xf
    Xtm_full = Xn_fm.reshape(D, 256).T
    com["Xtm"] = np.ascontiguousarray(Xtm_full.reshape(2, 128, D))
    com["blT"] = np.ascontiguousarray(
        I["L3_attn_B_l"].T.astype(BF).reshape(2, 128, 256))
    com["clT"] = np.ascontiguousarray(
        I["L3_attn_C_l"].T.astype(BF).reshape(2, 128, 256))
    com["alT"] = np.ascontiguousarray(
        I["L3_attn_A_l"].T.astype(BF).reshape(2, 128, 256))
    com["rl"] = np.ascontiguousarray(
        np.broadcast_to(I["L3_attn_r_l"].reshape(1, 1), (128, 1)).copy())
    wgu = np.concatenate(prep_launch4.wgu_parts, axis=1)
    com["wgu3"] = np.ascontiguousarray(
        wgu.reshape(NCH, 128, 2 * HC3, 128).transpose(2, 1, 0, 3)).reshape(2 * HC3, 128, D)
    wdn = np.concatenate(prep_launch4.wdn_parts, axis=1)
    com["wdn3"] = np.ascontiguousarray(
        wdn.reshape(HC3, 128, NCH, 128).transpose(2, 1, 0, 3)).reshape(NCH, 128, I3)
    in_maps = []
    for c in range(NC):
        a, bb = zz_chunks(c)
        own = np.r_[a * 512:(a + 1) * 512, bb * 512:(bb + 1) * 512]
        bl = (own < 256).astype(np.float32)
        im = dict(com)
        im["xin"] = x5_pc[c]
        im["xn4"] = xn4_pc[c]
        im["blendc"] = np.ascontiguousarray(
            np.broadcast_to(bl[None, :], (128, TC)).copy())
        im["omblendc"] = np.ascontiguousarray(
            np.broadcast_to((1.0 - bl)[None, :], (128, TC)).copy())
        in_maps.append(im)
    return in_maps

def _probe_devmap():
    if DEVMAP[0] is not None:
        return
    nc = bass.Bass("TRN2", target_bir_lowering=False, debug=False)
    out = nc.dram_tensor("o", [128, 4], F32, kind="ExternalOutput").ap()
    stack = ExitStack()
    with stack:
        ar = stack.enter_context(nc.sbuf_tensor("parena", [128, 1024], U8))
        sem = stack.enter_context(nc.semaphore("S"))
        sem2 = stack.enter_context(nc.semaphore("S2"))
        blk = stack.enter_context(nc.Block())
        fo = ar[:, 0:16].bitcast(F32)

        @blk.gpsimd
        def _(h):
            k = 0
            for arm in nc.gpsimd.switch_core_id(128):
                nc.gpsimd.memset(fo[:, :], float(k)).then_inc(sem, 1)
                k += 1
            nc.gpsimd.wait_ge(sem, 1)
            nc.gpsimd.dma_start(out=out[:, :], in_=fo[:, :]).then_inc(sem2, 16)
            nc.gpsimd.wait_ge(sem2, 16)
    r = run_bass_kernel_spmd(nc, [dict() for _ in range(NC)],
                             core_ids=list(range(NC)))
    DEVMAP[0] = [int(r.results[c]["o"][0, 0]) for c in range(NC)]


def kernel(**inputs):
    I = inputs
    _probe_devmap()
    in1 = prep_launch1(I)
    nc1 = build_launch1()
    r1 = run_bass_kernel_spmd(nc1, in1, core_ids=list(range(NC)))
    x2_pc = [r1.results[c]["x_out"] for c in range(NC)]
    xn1_pc = [r1.results[c]["xn1_out"] for c in range(NC)]
    k1_pc = [r1.results[c]["k1_out"] for c in range(NC)]
    v1_pc = [r1.results[c]["v1_out"] for c in range(NC)]

    in2 = prep_launch2(I, x2_pc, xn1_pc, k1_pc, v1_pc)
    nc2 = build_launch2()
    r2 = run_bass_kernel_spmd(nc2, in2, core_ids=list(range(NC)))
    x3_pc = [r2.results[c]["x_out"] for c in range(NC)]
    a2p_pc = [r2.results[c]["a2p_out"] for c in range(NC)]
    qd_pc = [r2.results[c]["qd_out"] for c in range(NC)]
    sums_pc = [r2.results[c]["sum_out"] for c in range(NC)]

    in3 = prep_launch3(I, x3_pc, a2p_pc, qd_pc, sums_pc)
    nc3 = build_launch3()
    r3 = run_bass_kernel_spmd(nc3, in3, core_ids=list(range(NC)))
    x5_pc = [r3.results[c]["x_out"] for c in range(NC)]
    xn4_pc = [r3.results[c]["xn4_out"] for c in range(NC)]
    prep_launch4.wgu_parts = [r3.results[c]["wdq_gu"] for c in range(NC)]
    prep_launch4.wdn_parts = [r3.results[c]["wdq_dn"] for c in range(NC)]
    Xn_fm = xn4_pc[0][:, :, 0:256].copy()

    in4 = prep_launch4(I, x5_pc, xn4_pc, Xn_fm)
    nc4 = build_launch4()
    r4 = run_bass_kernel_spmd(nc4, in4, core_ids=list(range(NC)))

    out = np.empty((T, D), np.float32)
    for c in range(NC):
        a, bb = zz_chunks(c)
        xfm = r4.results[c]["x_out"].reshape(D, TC)
        out[a * 512:(a + 1) * 512] = xfm[:, 0:512].T
        out[bb * 512:(bb + 1) * 512] = xfm[:, 512:1024].T
    return out.reshape(1, T, D)
